# version 04
# baseline (speedup 1.0000x reference)
import numpy as np
import ml_dtypes
import contextlib
import concourse.bass as bass
import concourse.mybir as mybir

F32 = mybir.dt.float32
F16 = mybir.dt.float16
BF16 = mybir.dt.bfloat16
AF = mybir.ActivationFunctionType
ALU = mybir.AluOpType
AX = mybir.AxisListType.X
BF = ml_dtypes.bfloat16

S = 4096
M = 1024
E = 8
CAP = 1024
H = 5464
NT = S // 128
NHT = (H + 127) // 128
HSPLIT = 20
NHEAD = 16
DH = 64
GELU_SA = float(np.sqrt(np.float64(0.044715)))
GELU_K = float(2.0 * np.sqrt(2.0 / np.pi))

DMA_OF = {"SP": "DSP", "POOL": "DPL", "ACT": "DACT"}
SER_ENGS = ("DVE", "ACT", "POOL")


class Rec:
    __slots__ = ("eng", "fn", "deps", "key", "inc", "sem_kind", "idx")
    def __init__(self, eng, fn, deps, key, inc, sem_kind, idx):
        self.eng = eng
        self.fn = fn
        self.deps = deps
        self.key = key
        self.inc = inc
        self.sem_kind = sem_kind
        self.idx = idx


class Ctx:
    def __init__(self, nc):
        self.nc = nc
        self.es = contextlib.ExitStack()
        self.T = {}
        self.n = {k: 0 for k in ("PE", "DVE", "ACT", "POOL", "SP", "DSP", "DPL", "DACT")}
        self.sem = {}
        for name in list(self.n):
            self.sem[name] = self.es.enter_context(nc.semaphore("s" + name))

    def dstream(self, name, idx=0, pool=1):
        key = f"D_{name}{idx % pool}"
        if key not in self.sem:
            self.sem[key] = self.es.enter_context(self.nc.semaphore(key))
            self.n[key] = 0
        return key

    def sb(self, name, shape, dt):
        return self.es.enter_context(self.nc.sbuf_tensor(name, shape, dt))


class Plan:
    def __init__(self, ctx, name=""):
        self.ctx = ctx
        self.name = name
        self.recs = []
        self._last = {}

    def op(self, eng, key, deps, fn, ser=True):
        deps = list(deps)
        if ser and eng in SER_ENGS and eng in self._last:
            deps.append(("_idx_", self._last[eng]))
        r = Rec(eng, fn, tuple(deps), key, 1, eng, len(self.recs))
        self.recs.append(r)
        self._last[eng] = r.idx
        return r

    def dma(self, eng, key, deps, fn, stream=None, idx=0, pool=4):
        assert stream is not None
        kind = self.ctx.dstream(stream, idx, pool)
        r = Rec(eng, fn, tuple(deps), key, 16, kind, len(self.recs))
        self.recs.append(r)
        return r

    def run(self):
        ctx = self.ctx
        nc = ctx.nc
        cnt = dict(ctx.n)
        toks = {}
        idx_toks = {}
        keyidx = {}
        for r in self.recs:
            cnt[r.sem_kind] += r.inc
            idx_toks[r.idx] = (ctx.sem[r.sem_kind], cnt[r.sem_kind])
            if r.key is not None:
                toks[r.key] = idx_toks[r.idx]
                keyidx[r.key] = r.idx
        for r in self.recs:
            for k in r.deps:
                if isinstance(k, tuple) and len(k) == 2 and k[0] == "_idx_":
                    continue
                if k in toks:
                    if keyidx[k] >= r.idx and self.recs[keyidx[k]].eng == r.eng:
                        raise RuntimeError(
                            f"plan {self.name}: same-engine future dep {k} at op {r.idx}")
                elif k not in ctx.T:
                    raise KeyError(f"plan {self.name}: unresolved dep {k!r}")

        def resolve(k):
            if isinstance(k, tuple) and len(k) == 2 and k[0] == "_idx_":
                return idx_toks[k[1]]
            return toks[k] if k in toks else ctx.T[k]

        per_eng = {}
        for r in self.recs:
            per_eng.setdefault(r.eng, []).append(r)

        def emit(eng_handle, eng_name):
            for r in per_eng.get(eng_name, ()):
                best = {}
                for k in r.deps:
                    sem, val = resolve(k)
                    if sem.num not in best or best[sem.num][1] < val:
                        best[sem.num] = (sem, val)
                for sem, val in best.values():
                    eng_handle.wait_ge(sem, val)
                inst = r.fn(eng_handle)
                inst.then_inc(ctx.sem[r.sem_kind], r.inc)

        with nc.Block() as blk:
            if "SP" in per_eng:
                @blk.sync
                def _(e):
                    emit(e, "SP")
            if "POOL" in per_eng:
                @blk.gpsimd
                def _(e):
                    emit(e, "POOL")
            if "PE" in per_eng:
                @blk.tensor
                def _(e):
                    emit(e, "PE")
            if "DVE" in per_eng:
                @blk.vector
                def _(e):
                    emit(e, "DVE")
            if "ACT" in per_eng:
                @blk.scalar
                def _(e):
                    emit(e, "ACT")
        ctx.n.update(cnt)
        ctx.T.update(toks)


def make_consts():
    iota8f = np.broadcast_to(np.arange(8, dtype=np.float32), (128, 8)).copy()
    return {
        "iota8f": iota8f,
        "iota8h": iota8f.astype(np.float16),
        "iotam999": iota8f - 999.0,
        "iota1024h": np.broadcast_to(np.arange(1024, dtype=np.float16), (128, 1024)).copy(),
        "triu": (np.arange(128)[:, None] < np.arange(128)[None, :]).astype(BF),
        "onescol": np.ones((128, 1), BF),
        "onescolf": np.ones((128, 1), np.float32),
        "ones1f": np.ones((1, 128), np.float32),
        "ones128f": np.ones((128, 128), np.float32),
        "ident8": np.eye(8, dtype=np.float32),
        "ident128f": np.eye(128, dtype=np.float32),
        "iotapc": (np.arange(128, dtype=np.float32)[:, None]
                   + 128.0 * np.arange(64, dtype=np.float32)[None, :]),
        "eps5": np.full((128, 1), 1e-5, np.float32),
    }


def make_rope_consts():
    inv = (1.0 / (np.float32(10000.0) **
                  (np.arange(0, DH, 2, dtype=np.float32) / np.float32(DH)))).astype(np.float32)
    ang = np.outer(np.arange(S, dtype=np.float32), inv).astype(np.float32)
    cos = np.cos(ang).astype(np.float32).T
    sin = np.sin(ang).astype(np.float32).T
    return {"cos2": np.concatenate([cos, cos], axis=0),
            "sin2": np.concatenate([sin, sin], axis=0)}

CONST_DTYPES = {
    "iota8f": F32, "iota8h": F16, "iotam999": F32, "iota1024h": F16,
    "triu": BF16, "onescol": BF16, "onescolf": F32, "ones1f": F32, "ones128f": F32,
    "ident8": F32, "ident128f": F32, "iotapc": F32, "eps5": F32,
}
ROPE_DTYPES = {"cos2": F32, "sin2": F32}


def feat_perm():
    perm = np.zeros(M, dtype=np.int64)
    for ft in range(8):
        hA, hB = 2 * ft, 2 * ft + 1
        base = 128 * ft
        for j in range(32):
            perm[base + j] = hA * 64 + 2 * j
            perm[base + 32 + j] = hB * 64 + 2 * j
            perm[base + 64 + j] = hA * 64 + 2 * j + 1
            perm[base + 96 + j] = hB * 64 + 2 * j + 1
    return perm


def declare_consts(ctx):
    nc = ctx.nc
    ctx.cd = {}
    ctx.cs = {}
    consts = make_consts()
    for k, v in consts.items():
        ctx.cd[k] = nc.dram_tensor("c_" + k, list(v.shape), CONST_DTYPES[k], kind="ExternalInput")
        ctx.cs[k] = ctx.sb("s_" + k, list(v.shape), CONST_DTYPES[k])
    return consts


def emit_fence(ctx, key="fence"):
    nc = ctx.nc
    with nc.Block() as blk:
        @blk.sync
        def _(e):
            for k, sem in ctx.sem.items():
                if k.startswith("D") and ctx.n.get(k, 0) > 0:
                    e.wait_ge(sem, ctx.n[k])
            e.nop().then_inc(ctx.sem["SP"], 1)
    ctx.n["SP"] += 1
    ctx.T[key] = (ctx.sem["SP"], ctx.n["SP"])


def emit_finish(ctx):
    nc = ctx.nc
    with nc.Block() as blk:
        @blk.sync
        def _(e):
            for k, sem in ctx.sem.items():
                if k.startswith("D") and ctx.n.get(k, 0) > 0:
                    e.wait_ge(sem, ctx.n[k])


def emit_const_loads(ctx):
    P = Plan(ctx, "consts")
    for k in ctx.cd:
        dst, src = ctx.cs[k], ctx.cd[k]
        P.dma("SP", "consts", (), lambda e, d=dst, s=src: e.dma_start(d[:], s[:]), stream="misc", pool=1)
    P.run()
    ctx.T["fence"] = ctx.T["consts"]


def emit_ffn(ctx, tag, x_src, x_dst, W):
    nc = ctx.nc
    xhi_d = nc.dram_tensor(f"{tag}_xhi", [S, M], BF16, kind="Internal")
    xlo_d = nc.dram_tensor(f"{tag}_xlo", [S, M], BF16, kind="Internal")
    eiT_d = nc.dram_tensor(f"{tag}_eiT", [16, 8, 128, 512], BF16, kind="Internal")
    eo_d = nc.dram_tensor(f"{tag}_eo", [8, 8, 128, 1024], BF16, kind="Internal")

    with contextlib.ExitStack() as ph:
        sbp = lambda n, s, d: ph.enter_context(nc.sbuf_tensor(f"{tag}_{n}", s, d))
        colbuf = sbp("colbuf", [128, 32 * NT], F32)
        m1kb = sbp("m1kb", [128, 8 * NT], F32)
        m2kb = sbp("m2kb", [128, 8 * NT], F32)
        gsb = sbp("gsb", [128, 64], F32)

        with contextlib.ExitStack() as p1:
            sb1 = lambda n, s, d: p1.enter_context(nc.sbuf_tensor(f"{tag}1_{n}", s, d))
            ps1 = lambda n, s: p1.enter_context(nc.psum_tensor(f"{tag}1_{n}", s, F32))
            oc1 = sb1("oc1", [128, 1024 * NT], BF16)
            oc2 = sb1("oc2", [128, 1024 * NT], BF16)
            B = dict(
                m1b=sb1("m1b", [128, 8 * NT], BF16),
                m2b=sb1("m2b", [128, 8 * NT], BF16),
                g1k=sb1("g1k", [128, 8 * NT], BF16),
                g2k=sb1("g2k", [128, 8 * NT], BF16),
                lg=sb1("lg", [128, 8 * NT], F32),
                rawb=sb1("rawb", [128, 8 * NT], F32),
                p1m=sb1("p1m", [128, 8 * NT], F32),
                p2m=sb1("p2m", [128, 8 * NT], F32),
                run1=sb1("run1", [1, 8], F32),
                run2=sb1("run2", [1, 8], F32),
                scr=sb1("scr", [128, 8], F32),
                scr2=sb1("scr2", [128, 8], F32),
                gsT=sb1("gsT", [8, 1024], F32),
            )
            with contextlib.ExitStack() as p1g:
                ps1g = lambda n, s: p1g.enter_context(nc.psum_tensor(f"{tag}1_{n}", s, F32))
                B["pp"] = ps1g("pp", [128, 8])
                B["pcs"] = ps1g("pcs", [1, 8])
                B["gps"] = ps1g("gps", [8, 1024])
                B["ptp"] = ps1g("ptp", [128, 8])
                _emit_f1(ctx, tag, x_src, xhi_d, xlo_d, W, colbuf, B, sb1, ps1g)
                _emit_gating(ctx, tag, colbuf, m1kb, m2kb, gsb, oc1, oc2, B)
            _emit_dispatch(ctx, tag, xhi_d, eiT_d, oc1, oc2, m1kb, m2kb, sb1, ps1)

        _emit_experts(ctx, tag, W, eiT_d, eo_d, gsb)
        _emit_combine(ctx, tag, eo_d, colbuf, x_src, x_dst)
    emit_fence(ctx)


def _emit_f1(ctx, tag, x_src, xhi_d, xlo_d, W, colbuf, B, sb1, ps1):
    cs = ctx.cs
    xt = sb1("xt", [128, 2 * M], F32)
    st6 = sb1("st6", [128, 2 * 16], F32)
    mv = sb1("mv", [128, 4 * NT], F32)
    xln = sb1("xln", [128, 2 * M], F32)
    xhi = sb1("xhi", [128, 2 * M], BF16)
    xlo = sb1("xlo", [128, 2 * M], BF16)
    xhiT = sb1("xhiT", [128, 2 * 512], BF16)
    xloT = sb1("xloT", [128, 2 * 512], BF16)
    lgT = sb1("lgT", [8, 512], F32)
    rwh = sb1("rwh", [128, 64], BF16)
    rwl = sb1("rwl", [128, 64], BF16)
    rwf = sb1("rwf", [128, 64], F32)
    pl8 = ps1("pl8", [8, 512])
    lg = B["lg"]
    ptp = B["ptp"]

    P = Plan(ctx, tag + "F1a")
    P.dma("POOL", (tag, "rwh"), ["consts", "fence"],
          lambda e: e.dma_start(rwh[:].rearrange("p (c e) -> p c e", c=8),
                                W["router_w"].rearrange("(c p) e -> p c e", p=128)),
          stream="miscP", pool=1)
    P.dma("SP", (tag, "rwf"), ["consts", "fence"],
          lambda e: e.dma_start(rwf[:].rearrange("p (c e) -> p c e", c=8),
                                W["router_w"].rearrange("(c p) e -> p c e", p=128)), stream="misc", pool=1)
    P.op("DVE", (tag, "cbz"), ["fence"], lambda e: e.memset(colbuf[:], 0.0))
    P.op("DVE", (tag, "rwl"), [(tag, "rwh"), (tag, "rwf")],
         lambda e: e.tensor_tensor(rwl[:], rwf[:], rwh[:], ALU.subtract))
    for t in range(NT):
        b = t % 2
        X = xt[:, M * b:M * b + M]
        S6 = st6[:, 16 * b:16 * b + 12]
        MV2 = st6[:, 16 * b + 12:16 * b + 14]
        VV = st6[:, 16 * b + 13:16 * b + 14]
        XL = xln[:, M * b:M * b + M]
        XH = xhi[:, M * b:M * b + M]
        XO = xlo[:, M * b:M * b + M]
        deps = ["fence"] + ([(tag, "xlnf", t - 2)] if t >= 2 else [])
        P.dma("SP", (tag, "xt", t), deps,
              lambda e, X=X, t=t: e.dma_start(X, x_src[128 * t:128 * t + 128, :]), stream="xt", idx=t, pool=3)
        deps = [(tag, "xt", t)] + ([(tag, "rstd", t - 2)] if t >= 2 else [])
        P.op("DVE", (tag, "bns", t), deps,
             lambda e, S6=S6, X=X: e.bn_stats(S6[:, 0:6], X[:, 0:512]))
        P.op("DVE", None, (),
             lambda e, S6=S6, X=X: e.bn_stats(S6[:, 6:12], X[:, 512:1024]))
        P.op("DVE", (tag, "bn", t), (), lambda e, MV2=MV2, S6=S6: e.bn_aggr(MV2, S6))
        P.op("DVE", (tag, "mn", t), (),
             lambda e, t=t, MV2=MV2: e.tensor_copy(mv[:, 4 * t:4 * t + 1], MV2[:, 0:1]))
        P.op("ACT", None, [(tag, "bn", t)],
             lambda e, VV=VV: e.activation(VV, VV, AF.Ln, bias=cs['eps5'][:, 0:1], scale=1.0))
        P.op("ACT", (tag, "rstd", t), (),
             lambda e, t=t, VV=VV: e.activation(mv[:, 4 * t + 1:4 * t + 2], VV, AF.Exp,
                                                bias=0.0, scale=-0.5))
        deps = [(tag, "rstd", t)]
        if t >= 2:
            deps += [(tag, "sthi", t - 2), (tag, "stlo", t - 2)]
        P.op("DVE", (tag, "xlnf", t), deps,
             lambda e, XL=XL, X=X, t=t: e.tensor_scalar(
                 XL, X, mv[:, 4 * t:4 * t + 1], mv[:, 4 * t + 1:4 * t + 2],
                 ALU.subtract, ALU.mult))
        P.op("DVE", (tag, "xhi", t), (), lambda e, XH=XH, XL=XL: e.tensor_copy(XH, XL))
        P.op("DVE", (tag, "xlo", t), (),
             lambda e, XO=XO, XL=XL, XH=XH: e.tensor_tensor(XO, XL, XH, ALU.subtract))
        P.dma("SP", (tag, "sthi", t), [(tag, "xhi", t)],
              lambda e, XH=XH, t=t: e.dma_start(xhi_d[128 * t:128 * t + 128, :], XH), stream="sthi", idx=t, pool=3)
        P.dma("SP", (tag, "stlo", t), [(tag, "xlo", t)],
              lambda e, XO=XO, t=t: e.dma_start(xlo_d[128 * t:128 * t + 128, :], XO), stream="stlo", idx=t, pool=3)
    P.run()

    P = Plan(ctx, tag + "F1b")
    for w in range(8):
        for c in range(8):
            i = w * 8 + c
            b = i % 2
            hiT = xhiT[:, 512 * b:512 * b + 512]
            loT = xloT[:, 512 * b:512 * b + 512]
            deps = [(tag, "sthi", NT - k) for k in (1, 2, 3, 4)] + \
                   [(tag, "stlo", NT - k) for k in (1, 2, 3, 4)]
            if i >= 2:
                deps.append((tag, "lgmm", i - 2))
            P.dma("SP", (tag, "ldTh", i), deps,
                  lambda e, hiT=hiT, w=w, c=c: e.dma_start(
                      hiT, xhi_d[512 * w:512 * w + 512, 128 * c:128 * c + 128], transpose=True), stream="ldT", idx=i, pool=3)
            P.dma("SP", (tag, "ldT", i), (),
                  lambda e, loT=loT, w=w, c=c: e.dma_start(
                      loT, xlo_d[512 * w:512 * w + 512, 128 * c:128 * c + 128], transpose=True), stream="ldT", idx=i, pool=3)
            deps = [(tag, "ldT", i), (tag, "ldTh", i), (tag, "rwl")]
            if c == 0 and w > 0:
                deps.append((tag, "lgTc", w - 1))
            P.op("PE", None, deps,
                 lambda e, c=c, hiT=hiT: e.matmul(pl8[:], rwh[:, 8 * c:8 * c + 8], hiT,
                                                  start=(c == 0), stop=False))
            P.op("PE", None, (),
                 lambda e, c=c, loT=loT: e.matmul(pl8[:], rwh[:, 8 * c:8 * c + 8], loT,
                                                  start=False, stop=False))
            P.op("PE", (tag, "lgmm", i), (),
                 lambda e, c=c, hiT=hiT: e.matmul(pl8[:], rwl[:, 8 * c:8 * c + 8], hiT,
                                                  start=False, stop=(c == 7)))
        deps = [(tag, "lgmm", w * 8 + 7)]
        if w > 0:
            deps.append((tag, "lgtr", 4 * w - 1))
        P.op("ACT", (tag, "lgTc", w), deps, lambda e: e.copy(lgT[:], pl8[:]))
        for j in range(4):
            t_idx = 4 * w + j
            deps = [(tag, "lgTc", w)]
            if t_idx > 0:
                deps.append((tag, "lgn", t_idx - 1))
            P.op("PE", (tag, "lgtr", t_idx), deps,
                 lambda e, j=j: e.transpose(ptp[:], lgT[:, 128 * j:128 * j + 128], cs["ident8"][:]))
            P.op("ACT", (tag, "lgn", t_idx), [(tag, "lgtr", t_idx)],
                 lambda e, t_idx=t_idx: e.copy(lg[:, 8 * t_idx:8 * t_idx + 8], ptp[:]))
    P.run()


def _emit_gating(ctx, tag, colbuf, m1kb, m2kb, gsb, oc1, oc2, B):
    cs = ctx.cs
    C = float(CAP)
    lg, rawb = B["lg"], B["rawb"]
    m1b, m2b = B["m1b"], B["m2b"]
    g1k, g2k = B["g1k"], B["g2k"]
    p1m, p2m = B["p1m"], B["p2m"]
    run1, run2 = B["run1"], B["run2"]
    scr, scr2 = B["scr"], B["scr2"]
    pp, pcs, gps, ptp, gsT = B["pp"], B["pcs"], B["gps"], B["ptp"], B["gsT"]

    P = Plan(ctx, tag + "G1")
    for t in range(NT):
        L = lg[:, 8 * t:8 * t + 8]
        R = rawb[:, 8 * t:8 * t + 8]
        cb = colbuf[:, 32 * t:32 * t + 32]
        P.op("DVE", (tag, "negmax", t), [(tag, "cbz"), (tag, "lgn", t)],
             lambda e, cb=cb, L=L: e.reduce_max(cb[:, 0:1], L, axis=AX, negate=True))
        P.op("ACT", (tag, "exp", t), [(tag, "negmax", t)],
             lambda e, R=R, L=L, cb=cb: e.activation(R, L, AF.Exp, bias=cb[:, 0:1], scale=1.0,
                                                     accum_out=cb[:, 1:2]))
        P.op("DVE", None, [(tag, "exp", t)],
             lambda e, cb=cb: e.reciprocal(cb[:, 2:3], cb[:, 1:2]))
        P.op("DVE", None, (), lambda e, R=R, cb=cb: e.tensor_scalar(R, R, cb[:, 2:3], None, ALU.mult))
        P.op("DVE", (tag, "g1", t), (), lambda e, cb=cb, R=R: e.reduce_max(cb[:, 3:4], R, axis=AX))
        P.op("DVE", None, (), lambda e, R=R, cb=cb: e.tensor_scalar(scr[:], R, cb[:, 3:4], None, ALU.is_equal))
        P.op("DVE", None, (), lambda e: e.tensor_tensor(scr[:], scr[:], cs["iotam999"][:], ALU.mult))
        P.op("DVE", None, (), lambda e: e.tensor_scalar(scr[:], scr[:], 999.0, None, ALU.add))
        P.op("DVE", None, (), lambda e, cb=cb: e.tensor_reduce(cb[:, 4:5], scr[:], axis=AX, op=ALU.min))
        P.op("DVE", (tag, "m1", t), (),
             lambda e, t=t, cb=cb: e.tensor_scalar(m1b[:, 8 * t:8 * t + 8], cs["iota8h"][:],
                                                   cb[:, 4:5], None, ALU.is_equal))
        P.op("DVE", None, (), lambda e, t=t, R=R: e.tensor_tensor(scr2[:], m1b[:, 8 * t:8 * t + 8], R, ALU.mult))
        P.op("DVE", None, (), lambda e, R=R: e.tensor_tensor(scr2[:], R, scr2[:], ALU.subtract))
        P.op("DVE", (tag, "g2", t), (), lambda e, cb=cb: e.reduce_max(cb[:, 5:6], scr2[:], axis=AX))
        P.op("DVE", None, (), lambda e, cb=cb: e.tensor_scalar(scr[:], scr2[:], cb[:, 5:6], None, ALU.is_equal))
        P.op("DVE", None, (), lambda e: e.tensor_tensor(scr[:], scr[:], cs["iotam999"][:], ALU.mult))
        P.op("DVE", None, (), lambda e: e.tensor_scalar(scr[:], scr[:], 999.0, None, ALU.add))
        P.op("DVE", None, (), lambda e, cb=cb: e.tensor_reduce(cb[:, 6:7], scr[:], axis=AX, op=ALU.min))
        P.op("DVE", (tag, "m2", t), (),
             lambda e, t=t, cb=cb: e.tensor_scalar(m2b[:, 8 * t:8 * t + 8], cs["iota8h"][:],
                                                   cb[:, 6:7], None, ALU.is_equal))
    P.run()

    for mx in (1, 2):
        mk = m1kb if mx == 1 else m2kb
        pm = p1m if mx == 1 else p2m
        runx = run1 if mx == 1 else run2
        mb_ = m1b if mx == 1 else m2b
        P = Plan(ctx, tag + f"G{mx + 1}")
        if mx == 1:
            P.op("DVE", (tag, "run1", 0), (), lambda e: e.memset(run1[:], 0.0))
        else:
            for t in range(NT):
                P.op("PE", (tag, "m1cnt", t), [(tag, "p1ev", NT - 1), (tag, "run1", NT)],
                     lambda e, t=t: e.matmul(pcs[:], cs["onescolf"][:], m1kb[:, 8 * t:8 * t + 8],
                                             start=(t == 0), stop=(t == NT - 1)))
            P.op("DVE", (tag, "run2", 0), [(tag, "m1cnt", NT - 1)],
                 lambda e: e.tensor_copy(run2[:], pcs[0:1, :]))
        for t in range(NT):
            cb = colbuf[:, 32 * t:32 * t + 32]
            deps = [(tag, f"run{mx}", t), (tag, "m2", NT - 1)]
            if t > 0:
                deps.append((tag, f"p{mx}ev", t - 1))
            P.op("PE", None, deps,
                 lambda e, t=t, mb_=mb_: e.matmul(pp[:], cs["triu"][:], mb_[:, 8 * t:8 * t + 8],
                                                  start=True, stop=False))
            P.op("PE", (tag, f"ppmm{mx}", t), (),
                 lambda e, runx=runx: e.matmul(pp[:], cs["ones1f"][:], runx[:], start=False, stop=True))
            P.op("PE", (tag, f"csmm{mx}", t), [(tag, f"run{mx}", t)],
                 lambda e, t=t, mb_=mb_: e.matmul(pcs[:], cs["onescol"][:], mb_[:, 8 * t:8 * t + 8],
                                                  start=True, stop=True))
            P.op("DVE", None, [(tag, f"ppmm{mx}", t)],
                 lambda e, t=t, pm=pm: e.tensor_copy(pm[:, 8 * t:8 * t + 8], pp[:]))
            P.op("DVE", (tag, f"p{mx}ev", t), (),
                 lambda e, t=t, mk=mk, mb_=mb_, cb=cb: e.scalar_tensor_tensor(
                     mk[:, 8 * t:8 * t + 8], pp[:], C, mb_[:, 8 * t:8 * t + 8],
                     ALU.is_lt, ALU.mult, accum_out=cb[:, 6 + mx:7 + mx]))
            P.op("DVE", (tag, f"run{mx}", t + 1), [(tag, f"csmm{mx}", t)],
                 lambda e, runx=runx: e.tensor_tensor(runx[:], runx[:], pcs[0:1, :], ALU.add))
        P.run()

    P = Plan(ctx, tag + "G5")
    for t in range(NT):
        cb = colbuf[:, 32 * t:32 * t + 32]
        P.op("DVE", None, [(tag, "p2ev", NT - 1), (tag, "run2", NT)],
             lambda e, t=t, cb=cb: e.scalar_tensor_tensor(
                 scr[:], p1m[:, 8 * t:8 * t + 8], 1.0, m1kb[:, 8 * t:8 * t + 8],
                 ALU.mult, ALU.mult, accum_out=cb[:, 9:10]))
        P.op("DVE", None, (),
             lambda e, t=t, cb=cb: e.scalar_tensor_tensor(
                 scr[:], p2m[:, 8 * t:8 * t + 8], 1.0, m2kb[:, 8 * t:8 * t + 8],
                 ALU.mult, ALU.mult, accum_out=cb[:, 10:11]))
        P.op("DVE", None, (), lambda e, cb=cb: e.tensor_scalar(cb[:, 11:12], cb[:, 4:5], 1024.0, None, ALU.mult))
        P.op("DVE", None, (), lambda e, cb=cb: e.tensor_scalar(cb[:, 12:13], cb[:, 7:8], 1.0, 99999.0,
                                                               ALU.subtract, ALU.mult))
        P.op("DVE", None, (), lambda e, cb=cb: e.tensor_tensor(cb[:, 11:12], cb[:, 11:12], cb[:, 9:10], ALU.add))
        P.op("DVE", None, (), lambda e, cb=cb: e.tensor_tensor(cb[:, 11:12], cb[:, 11:12], cb[:, 12:13], ALU.add))
        P.op("DVE", None, (), lambda e, cb=cb: e.tensor_tensor(cb[:, 15:16], cb[:, 9:10], cb[:, 12:13], ALU.add))
        P.op("DVE", None, (), lambda e, cb=cb: e.tensor_scalar(cb[:, 13:14], cb[:, 6:7], 1024.0, None, ALU.mult))
        P.op("DVE", None, (), lambda e, cb=cb: e.tensor_scalar(cb[:, 14:15], cb[:, 8:9], 1.0, 99999.0,
                                                               ALU.subtract, ALU.mult))
        P.op("DVE", None, (), lambda e, cb=cb: e.tensor_tensor(cb[:, 13:14], cb[:, 13:14], cb[:, 10:11], ALU.add))
        P.op("DVE", None, (), lambda e, cb=cb: e.tensor_tensor(cb[:, 13:14], cb[:, 13:14], cb[:, 14:15], ALU.add))
        P.op("DVE", None, (), lambda e, cb=cb: e.tensor_tensor(cb[:, 16:17], cb[:, 10:11], cb[:, 14:15], ALU.add))
        P.op("DVE", None, [(tag, "g1", t)],
             lambda e, t=t, cb=cb: e.tensor_scalar(g1k[:, 8 * t:8 * t + 8], m1kb[:, 8 * t:8 * t + 8],
                                                   cb[:, 3:4], None, ALU.mult))
        P.op("DVE", None, [(tag, "g2", t)],
             lambda e, t=t, cb=cb: e.tensor_scalar(g2k[:, 8 * t:8 * t + 8], m2kb[:, 8 * t:8 * t + 8],
                                                   cb[:, 5:6], None, ALU.mult))
        P.op("DVE", None, (),
             lambda e, t=t, cb=cb: e.tensor_scalar(oc1[:, 1024 * t:1024 * t + 1024],
                                                   cs["iota1024h"][:], cb[:, 15:16], None, ALU.is_equal))
        P.op("DVE", (tag, "oc", t), (),
             lambda e, t=t, cb=cb: e.tensor_scalar(oc2[:, 1024 * t:1024 * t + 1024],
                                                   cs["iota1024h"][:], cb[:, 16:17], None, ALU.is_equal))
        for h in range(2):
            P.op("PE", None, [(tag, "oc", t)],
                 lambda e, t=t, h=h: e.matmul(gps[:, 512 * h:512 * h + 512], g1k[:, 8 * t:8 * t + 8],
                                              oc1[:, 1024 * t + 512 * h:1024 * t + 512 * h + 512],
                                              start=(t == 0), stop=False))
            P.op("PE", (tag, "gpsmm", t, h), (),
                 lambda e, t=t, h=h: e.matmul(gps[:, 512 * h:512 * h + 512], g2k[:, 8 * t:8 * t + 8],
                                              oc2[:, 1024 * t + 512 * h:1024 * t + 512 * h + 512],
                                              start=False, stop=(t == NT - 1)))
    P.op("ACT", (tag, "gsTc"), [(tag, "gpsmm", NT - 1, 1), (tag, "lgn", NT - 1)],
         lambda e: e.copy(gsT[:], gps[:]))
    for w in range(8):
        deps = [(tag, "gsTc")]
        if w > 0:
            deps.append((tag, "gsb", w - 1))
        P.op("PE", (tag, "gstr", w), deps,
             lambda e, w=w: e.transpose(ptp[:], gsT[:, 128 * w:128 * w + 128], cs["ident8"][:]))
        P.op("ACT", (tag, "gsb", w), [(tag, "gstr", w)],
             lambda e, w=w: e.activation(gsb[:, 8 * w:8 * w + 8], ptp[:], AF.Copy,
                                         bias=0.0, scale=0.5))
    P.run()


def _emit_dispatch(ctx, tag, xhi_d, eiT_d, oc1, oc2, m1kb, m2kb, sb1, ps1):
    xw = sb1("xw", [128, 4 * M], BF16)
    dsb = sb1("dsb", [128, 4 * 512], BF16)
    eisb = sb1("eisb", [128, 2 * 8 * 512], BF16)
    pei = [ps1(f"pei{i}", [128, 512]) for i in range(8)]

    P = Plan(ctx, tag + "F2")
    for w in range(16):
        ex = w // 2
        lo = 512 * (w % 2)
        for t in range(NT):
            i = w * NT + t
            b = i % 4
            D = dsb[:, 512 * b:512 * b + 512]
            X = xw[:, M * b:M * b + M]
            deps = [(tag, "sthi", t)]
            if i >= 4:
                deps.append((tag, "dmm", i - 4))
            P.dma("SP", (tag, "xw", i), deps,
                  lambda e, X=X, t=t: e.dma_start(X, xhi_d[128 * t:128 * t + 128, :]), stream="xw", idx=i, pool=4)
            deps = [(tag, "oc", NT - 1)]
            if i >= 4:
                deps.append((tag, "dmm", i - 4))
            P.op("DVE", None, deps,
                 lambda e, D=D, t=t, lo=lo, ex=ex: e.tensor_scalar(
                     D, oc1[:, 1024 * t + lo:1024 * t + lo + 512],
                     m1kb[:, 8 * t + ex:8 * t + ex + 1], None, ALU.mult))
            P.op("DVE", (tag, "dsum", i), (),
                 lambda e, D=D, t=t, lo=lo, ex=ex: e.scalar_tensor_tensor(
                     D, oc2[:, 1024 * t + lo:1024 * t + lo + 512],
                     m2kb[:, 8 * t + ex:8 * t + ex + 1], D, ALU.mult, ALU.add))
            for mt in range(8):
                deps = []
                if mt == 0:
                    deps = [(tag, "dsum", i), (tag, "xw", i)]
                    if t == 0 and w >= 1:
                        deps += [(tag, "eiev", w - 1, mt2) for mt2 in range(8)]
                P.op("PE", (tag, "dmm", i) if mt == 7 else None, deps,
                     lambda e, mt=mt, X=X, D=D, t=t: e.matmul(
                         pei[mt][:], X[:, 128 * mt:128 * mt + 128], D,
                         start=(t == 0), stop=(t == NT - 1)))
        for mt in range(8):
            deps = [(tag, "dmm", w * NT + NT - 1)]
            if w >= 2:
                deps.append((tag, "eist", w - 2, 7))
            P.op("ACT", (tag, "eiev", w, mt), deps,
                 lambda e, w=w, mt=mt: e.copy(
                     eisb[:, (w % 2) * 4096 + 512 * mt:(w % 2) * 4096 + 512 * mt + 512],
                     pei[mt][:]))
            P.dma("SP", (tag, "eist", w, mt), [(tag, "eiev", w, mt)],
                  lambda e, w=w, mt=mt: e.dma_start(
                      eiT_d[w, mt],
                      eisb[:, (w % 2) * 4096 + 512 * mt:(w % 2) * 4096 + 512 * mt + 512]), stream="eiw", idx=w, pool=4)
    P.run()


def _emit_experts(ctx, tag, W, eiT_d, eo_d, gsb):
    nc = ctx.nc
    HC = 512
    halves = [(0, HSPLIT), (HSPLIT, NHT)]
    with contextlib.ExitStack() as p3:
        sb3 = lambda n, s, d: p3.enter_context(nc.sbuf_tensor(f"{tag}3_{n}", s, d))
        ps3 = lambda n, s: p3.enter_context(nc.psum_tensor(f"{tag}3_{n}", s, F32))
        hidT = sb3("hidT", [128, 23 * 1024], BF16)
        wores = sb3("wores", [128, 23 * 1024], BF16)
        eoacc = sb3("eoacc", [128, 8 * 1024], F32)
        eib = sb3("eib", [128, 16 * 512], BF16)
        w0b = sb3("w0b", [128, 2 * 8 * HC], BF16)
        w1b = sb3("w1b", [128, 2 * 8 * HC], BF16)
        sqb = sb3("sqb", [128, 1024], F32)
        u2b = sb3("u2b", [128, 1024], F32)
        gb = sb3("gb", [128, 1024], F32)
        eosb = sb3("eosb", [128, 2 * 1024], BF16)
        ph0 = [ps3(f"ph0{i}", [128, 1024]) for i in range(2)]
        ph1 = [ps3(f"ph1{i}", [128, 1024]) for i in range(2)]

        wmmC_key = {}
        g = 0
        for ex_ in range(8):
            for hh_, (ht0_, ht1_) in enumerate(halves):
                hbase_ = 128 * ht0_
                nchunk_ = (128 * ht1_ - hbase_ + HC - 1) // HC
                for lc_ in range(nchunk_):
                    last_ht = min(ht1_ - 1, (hbase_ + (lc_ + 1) * HC - 1) // 128)
                    wmmC_key[g] = (tag, "wmmht", ex_, last_ht)
                    g += 1

        P = Plan(ctx, tag + "F3")
        gw = [0]
        prev_tg = [None]
        for ex in range(8):
            for wi in range(2):
                w = 2 * ex + wi
                for mt in range(8):
                    deps = [(tag, "eist", w, 7)]
                    if ex > 0:
                        deps.append((tag, "wmmht", ex - 1, NHT - 1))
                    if ex == 0:
                        deps += [(tag, "eist", 15, 7), (tag, "eist", 14, 7), (tag, "xw", 16 * NT - 1)]
                    P.dma("SP", (tag, "eild", ex, wi, mt), deps,
                          lambda e, wi=wi, mt=mt, w=w: e.dma_start(
                              eib[:, wi * 4096 + 512 * mt:wi * 4096 + 512 * mt + 512],
                              eiT_d[w, mt]), stream="eild", idx=ex, pool=3)
            for hh, (ht0, ht1) in enumerate(halves):
                hbase = 128 * ht0
                nchunk = (128 * ht1 - hbase + HC - 1) // HC
                for lc in range(nchunk):
                    cw = min(HC, H - hbase - HC * lc)
                    b = gw[0] % 2
                    deps = []
                    if gw[0] >= 2:
                        deps.append(wmmC_key[gw[0] - 2])
                    if gw[0] < 2:
                        deps += [(tag, "eist", 15, 7), (tag, "eist", 14, 7), (tag, "xw", 16 * NT - 1)]
                    P.dma("POOL", (tag, "w0", ex, hh, lc), deps,
                          lambda e, b=b, cw=cw, ex=ex, h0=hbase + HC * lc: e.dma_start(
                              w0b[:, b * 8 * HC:b * 8 * HC + 8 * cw].rearrange("p (c h) -> p c h", c=8),
                              W["w0"][ex].rearrange("(c p) h -> p c h", p=128)[:, :, h0:h0 + cw]),
                          stream="w01", idx=gw[0], pool=3)
                    P.dma("POOL", (tag, "w1", ex, hh, lc), (),
                          lambda e, b=b, cw=cw, ex=ex, h0=hbase + HC * lc: e.dma_start(
                              w1b[:, b * 8 * HC:b * 8 * HC + 8 * cw].rearrange("p (c h) -> p c h", c=8),
                              W["w1"][ex].rearrange("(c p) h -> p c h", p=128)[:, :, h0:h0 + cw]),
                          stream="w01", idx=gw[0], pool=3)
                    gw[0] += 1
                for ht in range(ht0, ht1):
                    hrows = min(128, H - 128 * ht)
                    deps = []
                    if (ex, hh) > (0, 0):
                        pex, phh = (ex, hh - 1) if hh == 1 else (ex - 1, 1)
                        deps.append((tag, "odone", pex, phh))
                    else:
                        deps += [(tag, "eist", 15, 7), (tag, "eist", 14, 7), (tag, "xw", 16 * NT - 1)]
                    P.dma("POOL", (tag, "wo", ex, ht), deps,
                          lambda e, hrows=hrows, ex=ex, ht=ht, ht0=ht0: e.dma_start(
                              wores[0:hrows, 1024 * (ht - ht0):1024 * (ht - ht0) + 1024],
                              W["wo"][ex][128 * ht:128 * ht + hrows, :]),
                          stream="woh", idx=2 * ex + hh, pool=2)
                for ht in range(ht0, ht1):
                    lc = (128 * ht - hbase) // HC
                    cw = min(HC, H - hbase - HC * lc)
                    off = 128 * ht - hbase - HC * lc
                    hrows = min(128, H - 128 * ht)
                    b = (gw[0] - nchunk + lc) % 2
                    pb = ht % 2
                    for half in range(2):
                        for mt in range(8):
                            deps = []
                            if mt == 0:
                                deps = [(tag, "w0", ex, hh, lc), (tag, "w1", ex, hh, lc),
                                        (tag, "eild", ex, 1, 7)]
                                gi = ex * NHT + ht
                                if gi >= 2 and half == 0:
                                    pht = ht - 2
                                    pe_ = ex
                                    if pht < 0:
                                        pe_ -= 1
                                        pht += NHT
                                    deps.append((tag, "hid", pe_, pht))
                                if ht - ht0 < 2 and half == 0 and (ex, hh) > (0, 0):
                                    pex, phh = (ex, 0) if hh == 1 else (ex - 1, 1)
                                    for ctp in (6, 7):
                                        for mwp in (0, 1):
                                            deps.append((tag, "eoh", pex, phh, ctp, mwp))
                            P.op("PE", None, deps,
                                 lambda e, pb=pb, hrows=hrows, half=half, b=b, mt=mt, cw=cw,
                                        off=off: e.matmul(
                                     ph0[pb][0:hrows, 512 * half:512 * half + 512],
                                     w0b[:, b * 8 * HC + mt * cw + off:
                                            b * 8 * HC + mt * cw + off + hrows],
                                     eib[:, half * 4096 + 512 * mt:half * 4096 + 512 * mt + 512],
                                     start=(mt == 0), stop=(mt == 7)))
                        for mt in range(8):
                            key = (tag, "wmmht", ex, ht) if (half == 1 and mt == 7) else None
                            P.op("PE", key, (),
                                 lambda e, pb=pb, hrows=hrows, half=half, b=b, mt=mt, cw=cw,
                                        off=off: e.matmul(
                                     ph1[pb][0:hrows, 512 * half:512 * half + 512],
                                     w1b[:, b * 8 * HC + mt * cw + off:
                                            b * 8 * HC + mt * cw + off + hrows],
                                     eib[:, half * 4096 + 512 * mt:half * 4096 + 512 * mt + 512],
                                     start=(mt == 0), stop=(mt == 7)))
                    if ht == ht1 - 1:
                        ctxkey = (tag, "wmmC", gw[0] - nchunk + lc)
                    SQ = sqb[0:hrows, :]
                    U2 = u2b[0:hrows, :]
                    G = gb[0:hrows, :]
                    HD = hidT[0:hrows, 1024 * (ht - ht0):1024 * (ht - ht0) + 1024]
                    deps_sq = [(tag, "wmmht", ex, ht)]
                    if prev_tg[0] is not None:
                        deps_sq.append(prev_tg[0])
                    P.op("ACT", (tag, "sq", ex, ht), deps_sq,
                         lambda e, SQ=SQ, pb=pb, hrows=hrows: e.activation(
                             SQ, ph0[pb][0:hrows, :], AF.Square, bias=0.0, scale=GELU_SA))
                    P.op("DVE", (tag, "u2", ex, ht), [(tag, "sq", ex, ht)],
                         lambda e, U2=U2, SQ=SQ, pb=pb, hrows=hrows: e.scalar_tensor_tensor(
                             U2, SQ, 1.0, ph0[pb][0:hrows, :], ALU.add, ALU.mult))
                    P.op("ACT", (tag, "sg", ex, ht), [(tag, "u2", ex, ht)],
                         lambda e, SQ=SQ, U2=U2: e.activation(SQ, U2, AF.Sigmoid,
                                                              bias=0.0, scale=GELU_K))
                    P.op("DVE", (tag, "tg", ex, ht), [(tag, "sg", ex, ht)],
                         lambda e, G=G, SQ=SQ, pb=pb, hrows=hrows: e.tensor_tensor(
                             G, SQ, ph0[pb][0:hrows, :], ALU.mult))
                    prev_tg[0] = (tag, "tg", ex, ht)
                    deps = []
                    if (ex, hh) > (0, 0):
                        pex, phh = (ex, 0) if hh == 1 else (ex - 1, 1)
                        deps.append((tag, "odone", pex, phh))
                    P.op("DVE", (tag, "hid", ex, ht), deps,
                         lambda e, G=G, pb=pb, hrows=hrows, HD=HD: e.tensor_tensor(
                             HD, G, ph1[pb][0:hrows, :], ALU.mult))
                for ct in range(8):
                    pb = ct % 2
                    for mw in range(2):
                        for ht in range(ht0, ht1):
                            hrows = min(128, H - 128 * ht)
                            deps = []
                            if ht == ht0:
                                deps = [(tag, "wo", ex, ht1 - 1),
                                        (tag, "hid", ex, ht1 - 1), (tag, "hid", ex, ht1 - 2)]
                                if ct >= 2:
                                    deps += [(tag, "eoh", ex, hh, ct - 2, 0),
                                             (tag, "eoh", ex, hh, ct - 2, 1)]
                            key = (tag, "omm", ex, hh, ct, mw) if ht == ht1 - 1 else None
                            P.op("PE", key, deps,
                                 lambda e, pb=pb, mw=mw, hrows=hrows, ht=ht, ct=ct,
                                        ht0=ht0: e.matmul(
                                     ph0[pb][0:128, 512 * mw:512 * mw + 512],
                                     hidT[0:hrows, 1024 * (ht - ht0) + 128 * ct:
                                              1024 * (ht - ht0) + 128 * ct + 128],
                                     wores[0:hrows, 1024 * (ht - ht0) + 512 * mw:
                                               1024 * (ht - ht0) + 512 * mw + 512],
                                     start=(ht == ht0), stop=(ht == ht1 - 1),
                                     skip_group_check=True))
                        EA = eoacc[:, 1024 * ct + 512 * mw:1024 * ct + 512 * mw + 512]
                        if hh == 0:
                            P.op("ACT", (tag, "eoh", ex, 0, ct, mw),
                                 [(tag, "omm", ex, 0, ct, 1)],
                                 lambda e, EA=EA, pb=pb, mw=mw: e.copy(
                                     EA, ph0[pb][0:128, 512 * mw:512 * mw + 512]))
                        else:
                            P.op("DVE", (tag, "eoh", ex, 1, ct, mw),
                                 [(tag, "omm", ex, 1, ct, 1), (tag, "eoh", ex, 0, ct, mw)],
                                 lambda e, EA=EA, pb=pb, mw=mw: e.tensor_tensor(
                                     EA, EA, ph0[pb][0:128, 512 * mw:512 * mw + 512], ALU.add))
                            deps = [(tag, "eoh", ex, 1, ct, mw), (tag, "gsb", ct)]
                            if ct >= 2:
                                deps.append((tag, "eost", ex, ct - 2))
                            elif ex > 0:
                                deps.append((tag, "eost", ex - 1, ct + 6))
                            P.op("ACT", (tag, "eoev", ex, ct, mw), deps,
                                 lambda e, ct=ct, mw=mw, ex=ex, EA=EA: e.activation(
                                     eosb[:, (ct % 2) * 1024 + 512 * mw:
                                              (ct % 2) * 1024 + 512 * mw + 512],
                                     EA, AF.Copy, bias=0.0,
                                     scale=gsb[:, 8 * ct + ex:8 * ct + ex + 1]))
                    if hh == 1:
                        P.dma("SP", (tag, "eost", ex, ct),
                              [(tag, "eoev", ex, ct, 0), (tag, "eoev", ex, ct, 1)],
                              lambda e, ex=ex, ct=ct: e.dma_start(
                                  eo_d[ex, ct], eosb[:, (ct % 2) * 1024:(ct % 2) * 1024 + 1024]), stream="eost", idx=ex * 8 + ct, pool=3)
                P.op("PE", (tag, "odone", ex, hh), [(tag, "omm", ex, hh, 7, 1)],
                     lambda e: e.nop(), ser=False)
        P.run()


def _emit_combine(ctx, tag, eo_d, colbuf, x_src, x_dst):
    nc = ctx.nc
    cs = ctx.cs
    with contextlib.ExitStack() as p4:
        sb4 = lambda n, s, d: p4.enter_context(nc.sbuf_tensor(f"{tag}4_{n}", s, d))
        bc1 = sb4("bc1", [128, 512 * 8], F32)
        bc2 = sb4("bc2", [128, 512 * 8], F32)
        diag = sb4("diag", [128, 4 * 128], F32)
        dsT = sb4("dsT", [128, 8 * 512], BF16)
        eob = sb4("eob", [128, 8 * 1024], BF16)
        resb = sb4("resb", [128, 4 * M], F32)
        outb = sb4("outb", [128, 4 * M], F32)

        with contextlib.ExitStack() as p4a:
            pbc1 = p4a.enter_context(nc.psum_tensor(f"{tag}4_pbc1", [128, 512], F32))
            pbc2 = p4a.enter_context(nc.psum_tensor(f"{tag}4_pbc2", [128, 512], F32))
            P = Plan(ctx, tag + "F4a")
            for t in range(NT):
                b = t % 4
                w, j = divmod(t, 4)
                DG = diag[:, 128 * b:128 * b + 128]
                deps = [(tag, "eost", 7, 4), (tag, "eost", 7, 5), (tag, "eost", 7, 6), (tag, "eost", 7, 7)]
                if t >= 4:
                    deps.append((tag, "bmm2", t - 4))
                P.op("DVE", (tag, "dg1", t), deps,
                     lambda e, DG=DG, t=t: e.tensor_scalar(
                         DG, cs["ident128f"][:], colbuf[:, 32 * t + 11:32 * t + 12], None, ALU.mult))
                deps = [(tag, "dg1", t)]
                if w > 0 and j == 0:
                    deps.append((tag, "bcev1", w - 1))
                P.op("PE", (tag, "bmm1", t), deps,
                     lambda e, DG=DG, j=j: e.matmul(pbc1[:, 128 * j:128 * j + 128],
                                                    cs["ones128f"][:], DG, start=True, stop=True))
                P.op("DVE", (tag, "dg2", t), [(tag, "bmm1", t)],
                     lambda e, DG=DG, t=t: e.tensor_scalar(
                         DG, cs["ident128f"][:], colbuf[:, 32 * t + 13:32 * t + 14], None, ALU.mult))
                deps = [(tag, "dg2", t)]
                if w > 0 and j == 0:
                    deps.append((tag, "bcev2", w - 1))
                P.op("PE", (tag, "bmm2", t), deps,
                     lambda e, DG=DG, j=j: e.matmul(pbc2[:, 128 * j:128 * j + 128],
                                                    cs["ones128f"][:], DG, start=True, stop=True))
                if j == 3:
                    P.op("ACT", (tag, "bcev1", w), [(tag, "bmm1", t)],
                         lambda e, w=w: e.copy(bc1[:, 512 * w:512 * w + 512], pbc1[:]))
                    P.op("ACT", (tag, "bcev2", w), [(tag, "bmm2", t)],
                         lambda e, w=w: e.copy(bc2[:, 512 * w:512 * w + 512], pbc2[:]))
            P.run()

        with contextlib.ExitStack() as p4b:
            pcmb = [p4b.enter_context(nc.psum_tensor(f"{tag}4_pcmb{i}", [128, 512], F32))
                    for i in range(8)]
            P = Plan(ctx, tag + "F4b")
            for w in range(8):
                for ch in range(64):
                    i = w * 64 + ch
                    b = i % 8
                    DT = dsT[:, 512 * b:512 * b + 512]
                    EB = eob[:, 1024 * b:1024 * b + 1024]
                    deps = list(((tag, "eost", 7, 4), (tag, "eost", 7, 5), (tag, "eost", 7, 6), (tag, "eost", 7, 7)))
                    if i >= 8:
                        deps.append((tag, "cmm", i - 8))
                    P.dma("SP", (tag, "eol", i), deps,
                          lambda e, EB=EB, ch=ch: e.dma_start(EB, eo_d[ch // 8, ch % 8]), stream="eol", idx=i, pool=8)
                    deps = [(tag, "bcev1", w), (tag, "bcev2", w)]
                    if i >= 8:
                        deps.append((tag, "cmm", i - 8))
                    P.op("DVE", None, deps,
                         lambda e, DT=DT, w=w, ch=ch: e.tensor_scalar(
                             DT, bc1[:, 512 * w:512 * w + 512], cs["iotapc"][:, ch:ch + 1],
                             None, ALU.is_equal))
                    P.op("DVE", (tag, "dsT", i), (),
                         lambda e, DT=DT, w=w, ch=ch: e.scalar_tensor_tensor(
                             DT, bc2[:, 512 * w:512 * w + 512], cs["iotapc"][:, ch:ch + 1],
                             DT, ALU.is_equal, ALU.add))
                    for idx in range(8):
                        sj, mw = divmod(idx, 2)
                        deps = []
                        if idx == 0:
                            deps = [(tag, "dsT", i), (tag, "eol", i)]
                            if ch == 0 and w > 0:
                                deps += [(tag, "outev", w - 1, k) for k in range(8)]
                        key = (tag, "cmm", i) if idx == 7 else None
                        P.op("PE", key, deps,
                             lambda e, idx=idx, sj=sj, mw=mw, DT=DT, EB=EB, ch=ch: e.matmul(
                                 pcmb[idx][:], DT[:, 128 * sj:128 * sj + 128],
                                 EB[:, 512 * mw:512 * mw + 512],
                                 start=(ch == 0), stop=(ch == 63)))
                for sj in range(4):
                    t = 4 * w + sj
                    b = t % 4
                    RS = resb[:, M * b:M * b + M]
                    OB = outb[:, M * b:M * b + M]
                    deps = [(tag, "outst", t - 4)] if t >= 4 else [(tag, "eost", 7, 4), (tag, "eost", 7, 5), (tag, "eost", 7, 6), (tag, "eost", 7, 7)]
                    P.dma("SP", (tag, "res", t), deps,
                          lambda e, RS=RS, t=t: e.dma_start(RS, x_src[128 * t:128 * t + 128, :]), stream="res", idx=t, pool=3)
                    for mw in range(2):
                        idx = sj * 2 + mw
                        P.op("DVE", (tag, "outev", w, idx),
                             [(tag, "cmm", w * 64 + 63), (tag, "res", t)],
                             lambda e, OB=OB, RS=RS, idx=idx, mw=mw: e.tensor_tensor(
                                 OB[:, 512 * mw:512 * mw + 512], pcmb[idx][:],
                                 RS[:, 512 * mw:512 * mw + 512], ALU.add))
                    P.dma("SP", (tag, "outst", t),
                          [(tag, "outev", w, sj * 2), (tag, "outev", w, sj * 2 + 1)],
                          lambda e, OB=OB, t=t: e.dma_start(x_dst[128 * t:128 * t + 128, :], OB), stream="outst", idx=t, pool=4)
            P.run()


def attn_perm():
    perm = np.zeros(M, dtype=np.int64)
    for h in range(NHEAD):
        for j in range(32):
            perm[64 * h + j] = 64 * h + 2 * j
            perm[64 * h + 32 + j] = 64 * h + 2 * j + 1
    return perm




def _emit_rope2(P, tag, key, i, prev, psrc, dst, cosF, sinF, rtmp, sw0, extra0=()):
    for bi, B in enumerate((0, 64)):
        E_ = psrc[B:B + 32, :]
        O_ = psrc[B + 32:B + 64, :]
        CO = cosF[B:B + 32, 512 * sw0:512 * sw0 + 1024]
        SI = sinF[B:B + 32, 512 * sw0:512 * sw0 + 1024]
        pre = P._last.get("DVE")
        base = [prev] + (list(extra0) if bi == 0 else [])
        if pre is not None:
            base.append(("_idx_", pre))
        P.op("DVE", None, base,
             lambda e, E_=E_, CO=CO: e.tensor_tensor(rtmp[:, 0:1024], E_, CO, ALU.mult), ser=False)
        P.op("DVE", None, list(base),
             lambda e, O_=O_, SI=SI: e.tensor_tensor(rtmp[:, 1024:2048], O_, SI, ALU.mult), ser=False)
        P.op("DVE", None, list(base),
             lambda e, E_=E_, SI=SI: e.tensor_tensor(rtmp[:, 2048:3072], E_, SI, ALU.mult), ser=False)
        P.op("DVE", None, list(base),
             lambda e, O_=O_, CO=CO: e.tensor_tensor(rtmp[:, 3072:4096], O_, CO, ALU.mult), ser=False)
        P.op("DVE", None, (), lambda e, dst=dst, B=B: e.tensor_tensor(
            dst[B:B + 32, :], rtmp[:, 0:1024], rtmp[:, 1024:2048], ALU.subtract))
        P.op("DVE", key if B == 64 else None, (), lambda e, dst=dst, B=B: e.tensor_tensor(
            dst[B + 32:B + 64, :], rtmp[:, 2048:3072], rtmp[:, 3072:4096], ALU.add))


def emit_attention(ctx, tag, x_src, x_dst, W):
    nc = ctx.nc
    cs = ctx.cs
    xln_d = nc.dram_tensor(f"{tag}_xln", [S, M], BF16, kind="Internal")
    qT_d = nc.dram_tensor(f"{tag}_qT", [8, 128, S], BF16, kind="Internal")
    latT_d = nc.dram_tensor(f"{tag}_latT", [4, 128, S], BF16, kind="Internal")
    ropes = make_rope_consts()
    cosF_d = nc.dram_tensor(f"{tag}_cosF", [128, S], BF16, kind="ExternalInput")
    sinF_d = nc.dram_tensor(f"{tag}_sinF", [128, S], BF16, kind="ExternalInput")
    ctx.rope_inputs = {
        f"{tag}_cosF": np.concatenate([ropes["cos2"]] * 2, axis=0).astype(ml_dtypes.bfloat16),
        f"{tag}_sinF": np.concatenate([ropes["sin2"]] * 2, axis=0).astype(ml_dtypes.bfloat16)}

    rope_scope = contextlib.ExitStack()
    sbr = lambda n, s, d: rope_scope.enter_context(
        nc.sbuf_tensor(f"{tag}_{n}", s, d, side="right"))
    cosF = sbr("cosFs", [128, S], BF16)
    sinF = sbr("sinFs", [128, S], BF16)
    rtmp = sbr("rtmp", [32, 4 * 1024], F32)
    rtmp2 = rtmp

    with contextlib.ExitStack() as p1:
        sb1 = lambda n, s, d: p1.enter_context(nc.sbuf_tensor(f"{tag}1_{n}", s, d))
        xt = sb1("xt", [128, 2 * M], F32)
        st6 = sb1("st6", [128, 2 * 16], F32)
        mvb = sb1("mvb", [128, 4 * NT], F32)
        xlnb = sb1("xlnb", [128, 2 * M], BF16)
        P = Plan(ctx, tag + "A1")
        P.dma("SP", (tag, "cosF"), ["fence"], lambda e: e.dma_start(cosF[:], cosF_d[:]),
              stream="rope", pool=1)
        P.dma("SP", (tag, "sinF"), ["fence"], lambda e: e.dma_start(sinF[:], sinF_d[:]),
              stream="rope", pool=1)
        for t in range(NT):
            b = t % 2
            X = xt[:, M * b:M * b + M]
            S6 = st6[:, 16 * b:16 * b + 12]
            MV2 = st6[:, 16 * b + 12:16 * b + 14]
            VV = st6[:, 16 * b + 13:16 * b + 14]
            XL = xlnb[:, M * b:M * b + M]
            deps = ["fence"] + ([(tag, "xlnf", t - 2)] if t >= 2 else [])
            P.dma("SP", (tag, "xt", t), deps,
                  lambda e, X=X, t=t: e.dma_start(X, x_src[128 * t:128 * t + 128, :]),
                  stream="xt", idx=t, pool=3)
            deps = [(tag, "xt", t)] + ([(tag, "rstd", t - 2)] if t >= 2 else [])
            P.op("DVE", (tag, "bns", t), deps,
                 lambda e, S6=S6, X=X: e.bn_stats(S6[:, 0:6], X[:, 0:512]))
            P.op("DVE", None, (), lambda e, S6=S6, X=X: e.bn_stats(S6[:, 6:12], X[:, 512:1024]))
            P.op("DVE", (tag, "bn", t), (), lambda e, MV2=MV2, S6=S6: e.bn_aggr(MV2, S6))
            P.op("DVE", (tag, "mn", t), (),
                 lambda e, t=t, MV2=MV2: e.tensor_copy(mvb[:, 4 * t:4 * t + 1], MV2[:, 0:1]))
            P.op("ACT", None, [(tag, "bn", t)],
                 lambda e, VV=VV: e.activation(VV, VV, AF.Ln, bias=cs['eps5'][:, 0:1], scale=1.0))
            P.op("ACT", (tag, "rstd", t), (),
                 lambda e, t=t, VV=VV: e.activation(mvb[:, 4 * t + 1:4 * t + 2], VV, AF.Exp,
                                                    bias=0.0, scale=-0.5))
            deps = [(tag, "rstd", t)]
            if t >= 2:
                deps.append((tag, "stln", t - 2))
            P.op("DVE", (tag, "xlnf", t), deps,
                 lambda e, XL=XL, X=X, t=t: e.tensor_scalar(
                     XL, X, mvb[:, 4 * t:4 * t + 1], mvb[:, 4 * t + 1:4 * t + 2],
                     ALU.subtract, ALU.mult))
            P.dma("SP", (tag, "stln", t), [(tag, "xlnf", t)],
                  lambda e, XL=XL, t=t: e.dma_start(xln_d[128 * t:128 * t + 128, :], XL),
                  stream="sthi", idx=t, pool=3)
        P.run()

    with contextlib.ExitStack() as p2:
        sb2 = lambda n, s, d: p2.enter_context(nc.sbuf_tensor(f"{tag}2_{n}", s, d))
        ps2 = lambda n, s: p2.enter_context(nc.psum_tensor(f"{tag}2_{n}", s, F32))
        x2T = sb2("x2T", [128, 8 * S], BF16)
        wqb = sb2("wqb", [128, 8 * 1024], BF16)
        wdb = sb2("wdb", [128, 8 * 512], BF16)
        qsb = sb2("qsb", [128, 2 * 1024], BF16)
        latsb = sb2("latsb", [128, 2 * 1024], BF16)
        pq = [ps2(f"pq{i}", [128, 1024]) for i in range(2)]
        pl = ps2("pl", [128, 1024])
        P = Plan(ctx, tag + "A2a")
        P.dma("POOL", (tag, "wqb"), [(tag, "stln", NT - 1)],
              lambda e: e.dma_start(wqb[:].rearrange("p (c f) -> p c f", c=8),
                                    W["wq"].rearrange("(c p) f -> p c f", p=128)),
              stream="watt", pool=1)
        P.dma("POOL", (tag, "wdb"), (),
              lambda e: e.dma_start(wdb[:].rearrange("p (c l) -> p c l", c=8),
                                    W["w_down"].rearrange("(c p) l -> p c l", p=128)),
              stream="watt", pool=1)
        for c in range(8):
            deps = [(tag, "stln", NT - k) for k in (1, 2, 3)]
            P.dma("SP", (tag, "x2T", c), deps,
                  lambda e, c=c: e.dma_start(x2T[:, S * c:S * c + S],
                                             xln_d[:, 128 * c:128 * c + 128], transpose=True),
                  stream="ldT", idx=c, pool=3)
        WQDEP = [(tag, "wqb"), (tag, "wdb")]
        for ft in range(8):
            for swp in range(4):
                i = ft * 4 + swp
                pb = i % 2
                for wi in range(2):
                    sw = 2 * swp + wi
                    for c in range(8):
                        deps = []
                        if c == 0 and wi == 0:
                            deps = WQDEP + [(tag, "x2T", cc) for cc in range(8)]
                            if i >= 2:
                                deps.append((tag, "rope", i - 2))
                        P.op("PE", (tag, "qmm", i) if (c == 7 and wi == 1) else None, deps,
                             lambda e, pb=pb, c=c, ft=ft, sw=sw, wi=wi: e.matmul(
                                 pq[pb][:, 512 * wi:512 * wi + 512],
                                 wqb[:, 1024 * c + 128 * ft:1024 * c + 128 * ft + 128],
                                 x2T[:, S * c + 512 * sw:S * c + 512 * sw + 512],
                                 start=(c == 0), stop=(c == 7)))
                QO = qsb[:, 1024 * pb:1024 * pb + 1024]
                extra0 = [(tag, "cosF"), (tag, "sinF")]
                if i >= 2:
                    extra0.append((tag, "qst", i - 2))
                _emit_rope2(P, tag, (tag, "rope", i), i, (tag, "qmm", i),
                            pq[pb], QO, cosF, sinF, rtmp, 2 * swp, extra0=extra0)
                P.dma("SP", (tag, "qst", i), [(tag, "rope", i)],
                      lambda e, QO=QO, ft=ft, swp=swp: e.dma_start(
                          qT_d[ft][:, 1024 * swp:1024 * swp + 1024], QO),
                      stream="qst", idx=i, pool=3)
        for lt in range(4):
            for swp in range(4):
                i = lt * 4 + swp
                for wi in range(2):
                    sw = 2 * swp + wi
                    for c in range(8):
                        deps = []
                        if c == 0 and wi == 0:
                            deps = WQDEP + [(tag, "x2T", cc) for cc in range(8)]
                            if i >= 1:
                                deps.append((tag, "latev", i - 1))
                        P.op("PE", (tag, "lmm", i) if (c == 7 and wi == 1) else None, deps,
                             lambda e, c=c, lt=lt, sw=sw, wi=wi: e.matmul(
                                 pl[:, 512 * wi:512 * wi + 512],
                                 wdb[:, 512 * c + 128 * lt:512 * c + 128 * lt + 128],
                                 x2T[:, S * c + 512 * sw:S * c + 512 * sw + 512],
                                 start=(c == 0), stop=(c == 7)))
                LO = latsb[:, 1024 * (i % 2):1024 * (i % 2) + 1024]
                deps = [(tag, "lmm", i)]
                if i >= 2:
                    deps.append((tag, "latst", i - 2))
                P.op("ACT", (tag, "latev", i), deps,
                     lambda e, LO=LO: e.copy(LO, pl[:]))
                P.dma("SP", (tag, "latst", i), [(tag, "latev", i)],
                      lambda e, LO=LO, lt=lt, swp=swp: e.dma_start(
                          latT_d[lt][:, 1024 * swp:1024 * swp + 1024], LO),
                      stream="latst", idx=i, pool=3)
        P.run()

    with contextlib.ExitStack() as p34:
        sb34 = lambda n, s, d: p34.enter_context(nc.sbuf_tensor(f"{tag}34_{n}", s, d))
        kTs = sb34("kTs", [128, 8 * S], BF16)
        vab = sb34("vab", [128, NT * 1040], BF16)
        with contextlib.ExitStack() as p3:
            sb3 = lambda n, s, d: p3.enter_context(nc.sbuf_tensor(f"{tag}3_{n}", s, d))
            ps3 = lambda n, s: p3.enter_context(nc.psum_tensor(f"{tag}3_{n}", s, F32))
            latw = sb3("latw", [128, 2 * 4 * 1024], BF16)
            wkb = sb3("wkb", [128, 4 * 1024], BF16)
            wvb = sb3("wvb", [128, 4 * 1024], BF16)
            psk = [ps3(f"psk{i}", [128, 1024]) for i in range(2)]
            psv = [ps3(f"psv{i}", [128, 512]) for i in range(2)]
            qstF = [(tag, "qst", 32 - k) for k in (1, 2, 3)]
            latF = [(tag, "latst", 16 - k) for k in (1, 2, 3)]
            P = Plan(ctx, tag + "A2b")
            P.dma("POOL", (tag, "wkb"), qstF + latF,
                  lambda e: e.dma_start(wkb[:].rearrange("p (c f) -> p c f", c=4),
                                        W["wk_up"].rearrange("(c p) f -> p c f", p=128)),
                  stream="watt", pool=1)
            P.dma("POOL", (tag, "wvb"), (),
                  lambda e: e.dma_start(wvb[:].rearrange("p (c f) -> p c f", c=4),
                                        W["wv_up"].rearrange("(c p) f -> p c f", p=128)),
                  stream="watt", pool=1)
            P.op("DVE", (tag, "vone"), qstF + latF, lambda e: e.memset(vab[:], 1.0))
            WKDEP = [(tag, "wkb"), (tag, "wvb")]
            for swp in range(4):
                lb = swp % 2
                for c in range(4):
                    deps = [(tag, "latst", 4 * c + swp)]
                    if swp >= 2:
                        deps.append((tag, "vmm", (8 * (swp - 2) + 7) * 2 + 1))
                    P.dma("SP", (tag, "latld", swp, c), deps,
                          lambda e, lb=lb, c=c, swp=swp: e.dma_start(
                              latw[:, 4096 * lb + 1024 * c:4096 * lb + 1024 * c + 1024],
                              latT_d[c][:, 1024 * swp:1024 * swp + 1024]),
                          stream="latld", idx=4 * swp + c, pool=8)
                for ft in range(8):
                    k_i = swp * 8 + ft
                    pb = k_i % 2
                    for wi in range(2):
                        for c in range(4):
                            deps = []
                            if c == 0 and wi == 0:
                                deps = WKDEP + [(tag, "latld", swp, cc) for cc in range(4)]
                                if k_i >= 2:
                                    deps.append((tag, "krope", k_i - 2))
                            P.op("PE", (tag, "kmm", k_i) if (c == 3 and wi == 1) else None, deps,
                                 lambda e, pb=pb, c=c, ft=ft, lb=lb, wi=wi: e.matmul(
                                     psk[pb][:, 512 * wi:512 * wi + 512],
                                     wkb[:, 1024 * c + 128 * ft:1024 * c + 128 * ft + 128],
                                     latw[:, 4096 * lb + 1024 * c + 512 * wi:
                                             4096 * lb + 1024 * c + 512 * wi + 512],
                                     start=(c == 0), stop=(c == 3)))
                    KO = kTs[:, S * ft + 1024 * swp:S * ft + 1024 * swp + 1024]
                    _emit_rope2(P, tag, (tag, "krope", k_i), k_i, (tag, "kmm", k_i),
                                psk[pb], KO, cosF, sinF, rtmp2, 2 * swp)
                for stl in range(8):
                    st = 8 * swp + stl
                    for mwin in range(2):
                        i = st * 2 + mwin
                        for c in range(4):
                            deps = []
                            if c == 0:
                                deps = WKDEP + [(tag, "vone")] +                                        [(tag, "latld", swp, cc) for cc in range(4)]
                                if i >= 2:
                                    deps.append((tag, "vev", i - 2))
                            P.op("PE", (tag, "vmm", i) if c == 3 else None, deps,
                                 lambda e, c=c, stl=stl, mwin=mwin, i=i, lb=lb: e.matmul(
                                     psv[i % 2][:],
                                     latw[:, 4096 * lb + 1024 * c + 128 * stl:
                                             4096 * lb + 1024 * c + 128 * stl + 128],
                                     wvb[:, 1024 * c + 512 * mwin:1024 * c + 512 * mwin + 512],
                                     start=(c == 0), stop=(c == 3)))
                        P.op("ACT", (tag, "vev", i), [(tag, "vmm", i)],
                             lambda e, st=st, mwin=mwin, i=i: e.copy(
                                 vab[:, 1040 * st:1040 * st + 1040].rearrange(
                                     "p (h d) -> p h d", d=65)[:, 8 * mwin:8 * mwin + 8, 0:64],
                                 psv[i % 2][:].rearrange("p (h d) -> p h d", d=64)))
            P.run()
        rope_scope.close()

        with contextlib.ExitStack() as p4:
            sb4 = lambda n, s, d: p4.enter_context(nc.sbuf_tensor(f"{tag}4_{n}", s, d))
            ps4 = lambda n, s: p4.enter_context(nc.psum_tensor(f"{tag}4_{n}", s, F32))
            wob = sb4("wob", [128, 8 * 1024], BF16)
            qtb = sb4("qtb", [128, 2 * 512], BF16)
            expb = sb4("expb", [128, 4 * 1024], BF16)
            outT = sb4("outT", [128, 8 * 512], BF16)
            rcb = sb4("rcb", [1, 1024], F32)
            bcsb = sb4("bcsb", [128, 1024], F32)
            resb = sb4("resb", [128, 2 * M], F32)
            outb = sb4("outb", [128, 2 * M], F32)
            scAB = [ps4(f"sc{i}", [128, 1024]) for i in range(2)]
            avAB = ps4("avAB", [128, 1024])
            bcps = ps4("bcps", [128, 1024])
            P = Plan(ctx, tag + "A3")
            vevF = [(tag, "vev", 2 * NT - 1)]
            P.dma("POOL", (tag, "wob"), vevF,
                  lambda e: e.dma_start(wob[:].rearrange("p (c f) -> p c f", c=8),
                                        W["wo"].rearrange("(c p) f -> p c f", p=128)),
                  stream="watt", pool=1)

            for qw in range(8):
                nkt = 4 * (qw + 1)
                for hp in range(8):
                    j = qw * 8 + hp
                    qb = j % 2
                    hA, hB = 2 * hp, 2 * hp + 1
                    QT = qtb[:, 512 * qb:512 * qb + 512]
                    deps = list(vevF) + [(tag, "qst", 4 * hp + k) for k in range(4)]
                    if j >= 2:
                        deps.append((tag, "avg", j - 2, 4 * ((j - 2) // 8 + 1) - 1))
                    P.dma("SP", (tag, "qld", j), deps,
                          lambda e, QT=QT, hp=hp, qw=qw: e.dma_start(
                              QT, qT_d[hp][:, 512 * qw:512 * qw + 512]),
                          stream="qld", idx=j, pool=3)

                    def emit_sc(kt, j=j, hp=hp, qw=qw, nkt=nkt, QT=QT):
                        SC = scAB[kt % 2]
                        deps = []
                        if kt <= 1:
                            deps = [(tag, "qld", j), (tag, "wob")]
                            if j >= 1:
                                pnkt = 4 * ((j - 1) // 8 + 1)
                                deps.append((tag, "exp", j - 1, pnkt - 2 + kt))
                            else:
                                deps.append((tag, "krope", 31))
                            if kt == 0 and hp == 0 and qw > 0:
                                deps += [(tag, "outev", 4 * qw - 1, 0),
                                         (tag, "outev", 4 * qw - 1, 1),
                                         (tag, "outev", 4 * qw - 2, 0),
                                         (tag, "outev", 4 * qw - 2, 1)]
                        else:
                            deps = [(tag, "exp", j, kt - 2)]
                        P.op("PE", None, deps,
                             lambda e, SC=SC, kt=kt: e.matmul(
                                 SC[:, 0:512],
                                 kTs[0:64, S * hp + 128 * kt:S * hp + 128 * kt + 128],
                                 QT[0:64, :], start=True, stop=True, tile_position=(0, 0)))
                        P.op("PE", (tag, "scmm", j, kt), (),
                             lambda e, SC=SC, kt=kt: e.matmul(
                                 SC[:, 512:1024],
                                 kTs[64:128, S * hp + 128 * kt:S * hp + 128 * kt + 128],
                                 QT[64:128, :], start=True, stop=True, tile_position=(64, 0)))
                        EX = expb[:, 1024 * (kt % 4):1024 * (kt % 4) + 1024]
                        deps = [(tag, "scmm", j, kt)]
                        if kt >= 4:
                            deps.append((tag, "avg", j, kt - 4))
                        elif j >= 1:
                            pnkt = 4 * ((j - 1) // 8 + 1)
                            deps.append((tag, "avg", j - 1, pnkt - 4 + kt))
                        P.op("ACT", (tag, "exp", j, kt), deps,
                             lambda e, EX=EX, SC=SC: e.activation(EX, SC[:], AF.Exp,
                                                                  bias=0.0, scale=0.125))
                        if kt >= nkt - 2:
                            for off in (0, 512):
                                key = (tag, "msk", j, kt) if off == 512 else None
                                P.op("DVE", key, [(tag, "exp", j, kt)] if off == 0 else (),
                                     lambda e, EX=EX, off=off: e.memset(EX[:, off:off + 256], 0.0))

                    def emit_av(kt, j=j, hp=hp, nkt=nkt, hA=hA, hB=hB):
                        EX = expb[:, 1024 * (kt % 4):1024 * (kt % 4) + 1024]
                        deps = [(tag, "msk", j, kt) if kt >= nkt - 2 else (tag, "exp", j, kt)]
                        if kt == 0 and j >= 1:
                            deps.append((tag, "outTev", j - 1, 1))
                        P.op("PE", None, deps,
                             lambda e, EX=EX, kt=kt: e.matmul(
                                 avAB[0:65, 0:512],
                                 vab[:, 1040 * kt + 65 * hA:1040 * kt + 65 * hA + 65],
                                 EX[:, 0:512], start=(kt == 0), stop=(kt == nkt - 1)))
                        P.op("PE", (tag, "avg", j, kt), (),
                             lambda e, EX=EX, kt=kt: e.matmul(
                                 avAB[0:65, 512:1024],
                                 vab[:, 1040 * kt + 65 * hB:1040 * kt + 65 * hB + 65],
                                 EX[:, 512:1024], start=(kt == 0), stop=(kt == nkt - 1)))

                    emit_sc(0)
                    for kt in range(1, nkt):
                        emit_sc(kt)
                        emit_av(kt - 1)
                    emit_av(nkt - 1)
                    deps = [(tag, "avg", j, nkt - 1)]
                    if j >= 1:
                        deps.append((tag, "bcmm", j - 1, 1))
                    P.op("DVE", (tag, "rc", j), deps,
                         lambda e: e.reciprocal(rcb[0:1, :], avAB[64:65, :]))
                    for half in range(2):
                        deps = [(tag, "rc", j)]
                        if j >= 1 and half == 0:
                            deps.append((tag, "bcev", j - 1))
                        P.op("PE", (tag, "bcmm", j, half), deps,
                             lambda e, half=half: e.matmul(
                                 bcps[:, 512 * half:512 * half + 512], cs["ones1f"][:],
                                 rcb[0:1, 512 * half:512 * half + 512], start=True, stop=True))
                    P.op("ACT", (tag, "bcev", j), [(tag, "bcmm", j, 1)],
                         lambda e: e.copy(bcsb[:], bcps[:]))
                    OT = outT[:, 512 * hp:512 * hp + 512]
                    deps = [(tag, "bcev", j)]
                    if j >= 8:
                        deps.append((tag, "fmm", 4 * (qw - 1) + 3, 1))
                    P.op("DVE", (tag, "outTev", j, 0), deps,
                         lambda e, OT=OT: e.tensor_tensor(OT[0:64, :], avAB[0:64, 0:512],
                                                          bcsb[0:64, 0:512], ALU.mult))
                    P.op("DVE", (tag, "outTev", j, 1), (),
                         lambda e, OT=OT: e.tensor_tensor(OT[64:128, :], avAB[0:64, 512:1024],
                                                          bcsb[64:128, 512:1024], ALU.mult))
                for ssub in range(4):
                    t = 4 * qw + ssub
                    b = t % 2
                    RS = resb[:, M * b:M * b + M]
                    OB = outb[:, M * b:M * b + M]
                    deps = [(tag, "outTev", qw * 8 + 7, 1)] if ssub == 0 else []
                    if t >= 2:
                        deps.append((tag, "outst", t - 2))
                    P.dma("SP", (tag, "res", t), deps + list(vevF),
                          lambda e, RS=RS, t=t: e.dma_start(RS, x_src[128 * t:128 * t + 128, :]),
                          stream="res", idx=t, pool=3)
                    for mw in range(2):
                        fps = scAB[ssub % 2][:, 512 * mw:512 * mw + 512]
                        for hp in range(8):
                            deps = []
                            if hp == 0:
                                deps = [(tag, "outTev", qw * 8 + 7, 1),
                                        (tag, "exp", qw * 8 + 7, nkt - 1)]
                                if t >= 2:
                                    deps.append((tag, "outev", t - 2, mw))
                            P.op("PE", (tag, "fmm", t, mw) if hp == 7 else None, deps,
                                 lambda e, fps=fps, hp=hp, ssub=ssub, mw=mw: e.matmul(
                                     fps, outT[:, 512 * hp + 128 * ssub:512 * hp + 128 * ssub + 128],
                                     wob[:, 1024 * hp + 512 * mw:1024 * hp + 512 * mw + 512],
                                     start=(hp == 0), stop=(hp == 7)))
                        P.op("DVE", (tag, "outev", t, mw), [(tag, "fmm", t, mw), (tag, "res", t)],
                             lambda e, OB=OB, RS=RS, mw=mw, fps=fps: e.tensor_tensor(
                                 OB[:, 512 * mw:512 * mw + 512], fps,
                                 RS[:, 512 * mw:512 * mw + 512], ALU.add))
                    P.dma("SP", (tag, "outst", t), [(tag, "outev", t, 0), (tag, "outev", t, 1)],
                          lambda e, OB=OB, t=t: e.dma_start(x_dst[128 * t:128 * t + 128, :], OB),
                          stream="outst", idx=t, pool=4)
            P.run()
    emit_fence(ctx)


from concourse.bass_utils import run_bass_kernel_spmd

_BUILD_CACHE = {}


def _build_program():
    if "nc" in _BUILD_CACHE:
        return _BUILD_CACHE["nc"], _BUILD_CACHE["consts"], _BUILD_CACHE["rope"]
    nc = bass.Bass("TRN2", target_bir_lowering=False)
    ctx = Ctx(nc)
    x_d = nc.dram_tensor("x", [S, M], F32, kind="ExternalInput")
    out_d = nc.dram_tensor("out", [S, M], F32, kind="ExternalOutput")
    x1_d = nc.dram_tensor("x1s", [S, M], F32, kind="Internal")
    x2_d = nc.dram_tensor("x2s", [S, M], F32, kind="Internal")
    rw_d = nc.dram_tensor("rw", [M, E], F32, kind="ExternalInput")
    w0_d = nc.dram_tensor("w0", [E, M, H], F32, kind="ExternalInput")
    w1_d = nc.dram_tensor("w1", [E, M, H], F32, kind="ExternalInput")
    wof_d = nc.dram_tensor("wof", [E, H, M], F32, kind="ExternalInput")
    wq_d = nc.dram_tensor("wqp", [M, M], F32, kind="ExternalInput")
    wd_d = nc.dram_tensor("wd", [M, 512], F32, kind="ExternalInput")
    wk_d = nc.dram_tensor("wkp", [512, M], F32, kind="ExternalInput")
    wv_d = nc.dram_tensor("wv", [512, M], F32, kind="ExternalInput")
    woa_d = nc.dram_tensor("woa", [M, M], F32, kind="ExternalInput")

    consts = declare_consts(ctx)
    emit_const_loads(ctx)
    WF = dict(router_w=rw_d.ap(), w0=w0_d.ap(), w1=w1_d.ap(), wo=wof_d.ap())
    WA = dict(wq=wq_d.ap(), w_down=wd_d.ap(), wk_up=wk_d.ap(), wv_up=wv_d.ap(),
              wo=woa_d.ap())
    emit_ffn(ctx, "A", x_d.ap(), x1_d.ap(), WF)
    emit_attention(ctx, "T", x1_d.ap(), x2_d.ap(), WA)
    rope = dict(ctx.rope_inputs)
    emit_ffn(ctx, "B", x2_d.ap(), out_d.ap(), WF)
    emit_finish(ctx)
    _BUILD_CACHE.update(nc=nc, consts=consts, rope=rope)
    return nc, consts, rope


def kernel(L0_attn_x, L0_attn_wq, L0_attn_w_down, L0_attn_wk_up, L0_attn_wv_up,
           L0_attn_wo, L0_ffn_router_w, L0_ffn_w0_kernel, L0_ffn_w1_kernel,
           L0_ffn_wo_kernel):
    x = np.ascontiguousarray(np.asarray(L0_attn_x, dtype=np.float32))
    nc, consts, rope = _build_program()
    perm = attn_perm()
    wqp = np.ascontiguousarray(np.asarray(L0_attn_wq, np.float32)[:, perm])
    wkp = np.ascontiguousarray(np.asarray(L0_attn_wk_up, np.float32)[:, perm])
    base = dict(
        rw=np.ascontiguousarray(np.asarray(L0_ffn_router_w, np.float32)),
        w0=np.ascontiguousarray(np.asarray(L0_ffn_w0_kernel, np.float32)),
        w1=np.ascontiguousarray(np.asarray(L0_ffn_w1_kernel, np.float32)),
        wof=np.ascontiguousarray(np.asarray(L0_ffn_wo_kernel, np.float32)),
        wqp=wqp,
        wd=np.ascontiguousarray(np.asarray(L0_attn_w_down, np.float32)),
        wkp=wkp,
        wv=np.ascontiguousarray(np.asarray(L0_attn_wv_up, np.float32)),
        woa=np.ascontiguousarray(np.asarray(L0_attn_wo, np.float32)),
    )
    for k, v in consts.items():
        base["c_" + k] = v
    base.update(rope)
    in_maps = []
    for b in range(8):
        m = dict(base)
        m["x"] = np.ascontiguousarray(x[b])
        in_maps.append(m)
    res = run_bass_kernel_spmd(nc, in_maps, core_ids=list(range(8)))
    return np.stack([res.results[b]["out"] for b in range(8)], axis=0)


# version 06
# speedup vs baseline: 1.1319x; 1.1319x over previous
import numpy as np
import ml_dtypes
import contextlib
import concourse.bass as bass
import concourse.mybir as mybir

F32 = mybir.dt.float32
F16 = mybir.dt.float16
BF16 = mybir.dt.bfloat16
AF = mybir.ActivationFunctionType
ALU = mybir.AluOpType
AX = mybir.AxisListType.X
BF = ml_dtypes.bfloat16

S = 4096
M = 1024
E = 8
CAP = 1024
H = 5464
NT = S // 128
NHT = (H + 127) // 128
HSPLIT = 20
NHEAD = 16
DH = 64
GELU_SA = float(np.sqrt(np.float64(0.044715)))
GELU_K = float(2.0 * np.sqrt(2.0 / np.pi))

DMA_OF = {"SP": "DSP", "POOL": "DPL", "ACT": "DACT"}
SER_ENGS = ("DVE", "ACT", "POOL")


class Rec:
    __slots__ = ("eng", "fn", "deps", "key", "inc", "sem_kind", "idx")
    def __init__(self, eng, fn, deps, key, inc, sem_kind, idx):
        self.eng = eng
        self.fn = fn
        self.deps = deps
        self.key = key
        self.inc = inc
        self.sem_kind = sem_kind
        self.idx = idx


class Ctx:
    def __init__(self, nc):
        self.nc = nc
        self.es = contextlib.ExitStack()
        self.T = {}
        self.n = {k: 0 for k in ("PE", "DVE", "ACT", "POOL", "SP", "DSP", "DPL", "DACT")}
        self.sem = {}
        for name in list(self.n):
            self.sem[name] = self.es.enter_context(nc.semaphore("s" + name))

    def dstream(self, name, idx=0, pool=1):
        key = f"D_{name}{idx % pool}"
        if key not in self.sem:
            self.sem[key] = self.es.enter_context(self.nc.semaphore(key))
            self.n[key] = 0
        return key

    def sb(self, name, shape, dt):
        return self.es.enter_context(self.nc.sbuf_tensor(name, shape, dt))


class Plan:
    def __init__(self, ctx, name=""):
        self.ctx = ctx
        self.name = name
        self.recs = []
        self._last = {}

    def op(self, eng, key, deps, fn, ser=True):
        deps = list(deps)
        if ser and eng in SER_ENGS and eng in self._last:
            deps.append(("_idx_", self._last[eng]))
        r = Rec(eng, fn, tuple(deps), key, 1, eng, len(self.recs))
        self.recs.append(r)
        self._last[eng] = r.idx
        return r

    def dma(self, eng, key, deps, fn, stream=None, idx=0, pool=4):
        assert stream is not None
        kind = self.ctx.dstream(stream, idx, pool)
        r = Rec(eng, fn, tuple(deps), key, 16, kind, len(self.recs))
        self.recs.append(r)
        return r

    def run(self):
        ctx = self.ctx
        nc = ctx.nc
        cnt = dict(ctx.n)
        toks = {}
        idx_toks = {}
        keyidx = {}
        for r in self.recs:
            cnt[r.sem_kind] += r.inc
            idx_toks[r.idx] = (ctx.sem[r.sem_kind], cnt[r.sem_kind])
            if r.key is not None:
                toks[r.key] = idx_toks[r.idx]
                keyidx[r.key] = r.idx
        for r in self.recs:
            for k in r.deps:
                if isinstance(k, tuple) and len(k) == 2 and k[0] == "_idx_":
                    continue
                if k in toks:
                    if keyidx[k] >= r.idx and self.recs[keyidx[k]].eng == r.eng:
                        raise RuntimeError(
                            f"plan {self.name}: same-engine future dep {k} at op {r.idx}")
                elif k not in ctx.T:
                    raise KeyError(f"plan {self.name}: unresolved dep {k!r}")

        def resolve(k):
            if isinstance(k, tuple) and len(k) == 2 and k[0] == "_idx_":
                return idx_toks[k[1]]
            return toks[k] if k in toks else ctx.T[k]

        per_eng = {}
        for r in self.recs:
            per_eng.setdefault(r.eng, []).append(r)

        def emit(eng_handle, eng_name):
            for r in per_eng.get(eng_name, ()):
                best = {}
                for k in r.deps:
                    sem, val = resolve(k)
                    if sem.num not in best or best[sem.num][1] < val:
                        best[sem.num] = (sem, val)
                for sem, val in best.values():
                    eng_handle.wait_ge(sem, val)
                inst = r.fn(eng_handle)
                inst.then_inc(ctx.sem[r.sem_kind], r.inc)

        with nc.Block() as blk:
            if "SP" in per_eng:
                @blk.sync
                def _(e):
                    emit(e, "SP")
            if "POOL" in per_eng:
                @blk.gpsimd
                def _(e):
                    emit(e, "POOL")
            if "PE" in per_eng:
                @blk.tensor
                def _(e):
                    emit(e, "PE")
            if "DVE" in per_eng:
                @blk.vector
                def _(e):
                    emit(e, "DVE")
            if "ACT" in per_eng:
                @blk.scalar
                def _(e):
                    emit(e, "ACT")
        ctx.n.update(cnt)
        ctx.T.update(toks)


def make_consts():
    iota8f = np.broadcast_to(np.arange(8, dtype=np.float32), (128, 8)).copy()
    return {
        "iota8f": iota8f,
        "iota8h": iota8f.astype(np.float16),
        "iotam999": iota8f - 999.0,
        "iota1024h": np.broadcast_to(np.arange(1024, dtype=np.float16), (128, 1024)).copy(),
        "triu": (np.arange(128)[:, None] < np.arange(128)[None, :]).astype(BF),
        "onescol": np.ones((128, 1), BF),
        "onescolf": np.ones((128, 1), np.float32),
        "ones1f": np.ones((1, 128), np.float32),
        "ones128f": np.ones((128, 128), np.float32),
        "ident8": np.eye(8, dtype=np.float32),
        "ident128f": np.eye(128, dtype=np.float32),
        "iotapc": (np.arange(128, dtype=np.float32)[:, None]
                   + 128.0 * np.arange(64, dtype=np.float32)[None, :]),
        "eps5": np.full((128, 1), 1e-5, np.float32),
    }


def make_rope_consts():
    inv = (1.0 / (np.float32(10000.0) **
                  (np.arange(0, DH, 2, dtype=np.float32) / np.float32(DH)))).astype(np.float32)
    ang = np.outer(np.arange(S, dtype=np.float32), inv).astype(np.float32)
    cos = np.cos(ang).astype(np.float32).T
    sin = np.sin(ang).astype(np.float32).T
    return {"ropeA": np.concatenate([cos, cos, cos, cos], axis=0),
            "ropeB": np.concatenate([sin, sin, sin, sin], axis=0)}

CONST_DTYPES = {
    "iota8f": F32, "iota8h": F16, "iotam999": F32, "iota1024h": F16,
    "triu": BF16, "onescol": BF16, "onescolf": F32, "ones1f": F32, "ones128f": F32,
    "ident8": F32, "ident128f": F32, "iotapc": F32, "eps5": F32,
}
ROPE_DTYPES = {"ropeA": F32, "ropeB": F32}


def feat_perm():
    perm = np.zeros(M, dtype=np.int64)
    for ft in range(8):
        hA, hB = 2 * ft, 2 * ft + 1
        base = 128 * ft
        for j in range(32):
            perm[base + j] = hA * 64 + 2 * j
            perm[base + 32 + j] = hB * 64 + 2 * j
            perm[base + 64 + j] = hA * 64 + 2 * j + 1
            perm[base + 96 + j] = hB * 64 + 2 * j + 1
    return perm


def declare_consts(ctx):
    nc = ctx.nc
    ctx.cd = {}
    ctx.cs = {}
    consts = make_consts()
    for k, v in consts.items():
        ctx.cd[k] = nc.dram_tensor("c_" + k, list(v.shape), CONST_DTYPES[k], kind="ExternalInput")
        ctx.cs[k] = ctx.sb("s_" + k, list(v.shape), CONST_DTYPES[k])
    return consts


def emit_fence(ctx, key="fence"):
    nc = ctx.nc
    with nc.Block() as blk:
        @blk.sync
        def _(e):
            for k, sem in ctx.sem.items():
                if k.startswith("D") and ctx.n.get(k, 0) > 0:
                    e.wait_ge(sem, ctx.n[k])
            e.nop().then_inc(ctx.sem["SP"], 1)
    ctx.n["SP"] += 1
    ctx.T[key] = (ctx.sem["SP"], ctx.n["SP"])


def emit_finish(ctx):
    nc = ctx.nc
    with nc.Block() as blk:
        @blk.sync
        def _(e):
            for k, sem in ctx.sem.items():
                if k.startswith("D") and ctx.n.get(k, 0) > 0:
                    e.wait_ge(sem, ctx.n[k])


def emit_const_loads(ctx):
    P = Plan(ctx, "consts")
    for k in ctx.cd:
        dst, src = ctx.cs[k], ctx.cd[k]
        P.dma("SP", "consts", (), lambda e, d=dst, s=src: e.dma_start(d[:], s[:]), stream="misc", pool=1)
    P.run()
    ctx.T["fence"] = ctx.T["consts"]


def emit_ffn(ctx, tag, x_src, x_dst, W):
    nc = ctx.nc
    xhi_d = nc.dram_tensor(f"{tag}_xhi", [S, M], BF16, kind="Internal")
    xlo_d = nc.dram_tensor(f"{tag}_xlo", [S, M], BF16, kind="Internal")
    eiT_d = nc.dram_tensor(f"{tag}_eiT", [16, 8, 128, 512], BF16, kind="Internal")
    eo_d = nc.dram_tensor(f"{tag}_eo", [8, 8, 128, 1024], BF16, kind="Internal")

    with contextlib.ExitStack() as ph:
        sbp = lambda n, s, d: ph.enter_context(nc.sbuf_tensor(f"{tag}_{n}", s, d))
        colbuf = sbp("colbuf", [128, 32 * NT], F32)
        m1kb = sbp("m1kb", [128, 8 * NT], F32)
        m2kb = sbp("m2kb", [128, 8 * NT], F32)
        gsb = sbp("gsb", [128, 64], F32)

        with contextlib.ExitStack() as p1:
            sb1 = lambda n, s, d: p1.enter_context(nc.sbuf_tensor(f"{tag}1_{n}", s, d))
            ps1 = lambda n, s: p1.enter_context(nc.psum_tensor(f"{tag}1_{n}", s, F32))
            oc1 = sb1("oc1", [128, 1024 * NT], BF16)
            oc2 = sb1("oc2", [128, 1024 * NT], BF16)
            B = dict(
                m1b=sb1("m1b", [128, 8 * NT], BF16),
                m2b=sb1("m2b", [128, 8 * NT], BF16),
                g1k=sb1("g1k", [128, 8 * NT], BF16),
                g2k=sb1("g2k", [128, 8 * NT], BF16),
                lg=sb1("lg", [128, 8 * NT], F32),
                rawb=sb1("rawb", [128, 8 * NT], F32),
                p1m=sb1("p1m", [128, 8 * NT], F32),
                p2m=sb1("p2m", [128, 8 * NT], F32),
                run1=sb1("run1", [1, 8], F32),
                run2=sb1("run2", [1, 8], F32),
                scr=sb1("scr", [128, 8], F32),
                scr2=sb1("scr2", [128, 8], F32),
                gsT=sb1("gsT", [8, 1024], F32),
            )
            with contextlib.ExitStack() as p1g:
                ps1g = lambda n, s: p1g.enter_context(nc.psum_tensor(f"{tag}1_{n}", s, F32))
                B["pp"] = ps1g("pp", [128, 8])
                B["pcs"] = ps1g("pcs", [1, 8])
                B["gps"] = ps1g("gps", [8, 1024])
                B["ptp"] = ps1g("ptp", [128, 8])
                _emit_f1(ctx, tag, x_src, xhi_d, xlo_d, W, colbuf, B, sb1, ps1g)
                _emit_gating(ctx, tag, colbuf, m1kb, m2kb, gsb, oc1, oc2, B)
            _emit_dispatch(ctx, tag, xhi_d, eiT_d, oc1, oc2, m1kb, m2kb, sb1, ps1)

        _emit_experts(ctx, tag, W, eiT_d, eo_d, gsb)
        _emit_combine(ctx, tag, eo_d, colbuf, x_src, x_dst)
    emit_fence(ctx)


def _emit_f1(ctx, tag, x_src, xhi_d, xlo_d, W, colbuf, B, sb1, ps1):
    cs = ctx.cs
    xt = sb1("xt", [128, 2 * M], F32)
    st6 = sb1("st6", [128, 2 * 16], F32)
    mv = sb1("mv", [128, 4 * NT], F32)
    xln = sb1("xln", [128, 2 * M], F32)
    xhi = sb1("xhi", [128, 2 * M], BF16)
    xlo = sb1("xlo", [128, 2 * M], BF16)
    xhiT = sb1("xhiT", [128, 2 * 512], BF16)
    xloT = sb1("xloT", [128, 2 * 512], BF16)
    lgT = sb1("lgT", [8, 512], F32)
    rwh = sb1("rwh", [128, 64], BF16)
    rwl = sb1("rwl", [128, 64], BF16)
    rwf = sb1("rwf", [128, 64], F32)
    pl8 = ps1("pl8", [8, 512])
    lg = B["lg"]
    ptp = B["ptp"]

    P = Plan(ctx, tag + "F1a")
    P.dma("POOL", (tag, "rwh"), ["consts", "fence"],
          lambda e: e.dma_start(rwh[:].rearrange("p (c e) -> p c e", c=8),
                                W["router_w"].rearrange("(c p) e -> p c e", p=128)),
          stream="miscP", pool=1)
    P.dma("SP", (tag, "rwf"), ["consts", "fence"],
          lambda e: e.dma_start(rwf[:].rearrange("p (c e) -> p c e", c=8),
                                W["router_w"].rearrange("(c p) e -> p c e", p=128)), stream="misc", pool=1)
    P.op("DVE", (tag, "cbz"), ["fence"], lambda e: e.memset(colbuf[:], 0.0))
    P.op("DVE", (tag, "rwl"), [(tag, "rwh"), (tag, "rwf")],
         lambda e: e.tensor_tensor(rwl[:], rwf[:], rwh[:], ALU.subtract))
    for t in range(NT):
        b = t % 2
        X = xt[:, M * b:M * b + M]
        S6 = st6[:, 16 * b:16 * b + 12]
        MV2 = st6[:, 16 * b + 12:16 * b + 14]
        VV = st6[:, 16 * b + 13:16 * b + 14]
        XL = xln[:, M * b:M * b + M]
        XH = xhi[:, M * b:M * b + M]
        XO = xlo[:, M * b:M * b + M]
        deps = ["fence"] + ([(tag, "xlnf", t - 2)] if t >= 2 else [])
        P.dma("SP", (tag, "xt", t), deps,
              lambda e, X=X, t=t: e.dma_start(X, x_src[128 * t:128 * t + 128, :]), stream="xt", idx=t, pool=3)
        deps = [(tag, "xt", t)] + ([(tag, "rstd", t - 2)] if t >= 2 else [])
        P.op("DVE", (tag, "bns", t), deps,
             lambda e, S6=S6, X=X: e.bn_stats(S6[:, 0:6], X[:, 0:512]))
        P.op("DVE", None, (),
             lambda e, S6=S6, X=X: e.bn_stats(S6[:, 6:12], X[:, 512:1024]))
        P.op("DVE", (tag, "bn", t), (), lambda e, MV2=MV2, S6=S6: e.bn_aggr(MV2, S6))
        P.op("DVE", (tag, "mn", t), (),
             lambda e, t=t, MV2=MV2: e.tensor_copy(mv[:, 4 * t:4 * t + 1], MV2[:, 0:1]))
        P.op("ACT", None, [(tag, "bn", t)],
             lambda e, VV=VV: e.activation(VV, VV, AF.Ln, bias=cs['eps5'][:, 0:1], scale=1.0))
        P.op("ACT", (tag, "rstd", t), (),
             lambda e, t=t, VV=VV: e.activation(mv[:, 4 * t + 1:4 * t + 2], VV, AF.Exp,
                                                bias=0.0, scale=-0.5))
        deps = [(tag, "rstd", t)]
        if t >= 2:
            deps += [(tag, "sthi", t - 2), (tag, "stlo", t - 2)]
        P.op("DVE", (tag, "xlnf", t), deps,
             lambda e, XL=XL, X=X, t=t: e.tensor_scalar(
                 XL, X, mv[:, 4 * t:4 * t + 1], mv[:, 4 * t + 1:4 * t + 2],
                 ALU.subtract, ALU.mult))
        P.op("DVE", (tag, "xhi", t), (), lambda e, XH=XH, XL=XL: e.tensor_copy(XH, XL))
        P.op("DVE", (tag, "xlo", t), (),
             lambda e, XO=XO, XL=XL, XH=XH: e.tensor_tensor(XO, XL, XH, ALU.subtract))
        P.dma("SP", (tag, "sthi", t), [(tag, "xhi", t)],
              lambda e, XH=XH, t=t: e.dma_start(xhi_d[128 * t:128 * t + 128, :], XH), stream="sthi", idx=t, pool=3)
        P.dma("SP", (tag, "stlo", t), [(tag, "xlo", t)],
              lambda e, XO=XO, t=t: e.dma_start(xlo_d[128 * t:128 * t + 128, :], XO), stream="stlo", idx=t, pool=3)
    P.run()

    P = Plan(ctx, tag + "F1b")
    for w in range(8):
        for c in range(8):
            i = w * 8 + c
            b = i % 2
            hiT = xhiT[:, 512 * b:512 * b + 512]
            loT = xloT[:, 512 * b:512 * b + 512]
            deps = [(tag, "sthi", NT - k) for k in (1, 2, 3, 4)] + \
                   [(tag, "stlo", NT - k) for k in (1, 2, 3, 4)]
            if i >= 2:
                deps.append((tag, "lgmm", i - 2))
            P.dma("SP", (tag, "ldTh", i), deps,
                  lambda e, hiT=hiT, w=w, c=c: e.dma_start(
                      hiT, xhi_d[512 * w:512 * w + 512, 128 * c:128 * c + 128], transpose=True), stream="ldT", idx=i, pool=3)
            P.dma("SP", (tag, "ldT", i), (),
                  lambda e, loT=loT, w=w, c=c: e.dma_start(
                      loT, xlo_d[512 * w:512 * w + 512, 128 * c:128 * c + 128], transpose=True), stream="ldT", idx=i, pool=3)
            deps = [(tag, "ldT", i), (tag, "ldTh", i), (tag, "rwl")]
            if c == 0 and w > 0:
                deps.append((tag, "lgTc", w - 1))
            P.op("PE", None, deps,
                 lambda e, c=c, hiT=hiT: e.matmul(pl8[:], rwh[:, 8 * c:8 * c + 8], hiT,
                                                  start=(c == 0), stop=False))
            P.op("PE", None, (),
                 lambda e, c=c, loT=loT: e.matmul(pl8[:], rwh[:, 8 * c:8 * c + 8], loT,
                                                  start=False, stop=False))
            P.op("PE", (tag, "lgmm", i), (),
                 lambda e, c=c, hiT=hiT: e.matmul(pl8[:], rwl[:, 8 * c:8 * c + 8], hiT,
                                                  start=False, stop=(c == 7)))
        deps = [(tag, "lgmm", w * 8 + 7)]
        if w > 0:
            deps.append((tag, "lgtr", 4 * w - 1))
        P.op("ACT", (tag, "lgTc", w), deps, lambda e: e.copy(lgT[:], pl8[:]))
        for j in range(4):
            t_idx = 4 * w + j
            deps = [(tag, "lgTc", w)]
            if t_idx > 0:
                deps.append((tag, "lgn", t_idx - 1))
            P.op("PE", (tag, "lgtr", t_idx), deps,
                 lambda e, j=j: e.transpose(ptp[:], lgT[:, 128 * j:128 * j + 128], cs["ident8"][:]))
            P.op("ACT", (tag, "lgn", t_idx), [(tag, "lgtr", t_idx)],
                 lambda e, t_idx=t_idx: e.copy(lg[:, 8 * t_idx:8 * t_idx + 8], ptp[:]))
    P.run()


def _emit_gating(ctx, tag, colbuf, m1kb, m2kb, gsb, oc1, oc2, B):
    cs = ctx.cs
    C = float(CAP)
    lg, rawb = B["lg"], B["rawb"]
    m1b, m2b = B["m1b"], B["m2b"]
    g1k, g2k = B["g1k"], B["g2k"]
    p1m, p2m = B["p1m"], B["p2m"]
    run1, run2 = B["run1"], B["run2"]
    scr, scr2 = B["scr"], B["scr2"]
    pp, pcs, gps, ptp, gsT = B["pp"], B["pcs"], B["gps"], B["ptp"], B["gsT"]

    P = Plan(ctx, tag + "G1")
    for t in range(NT):
        L = lg[:, 8 * t:8 * t + 8]
        R = rawb[:, 8 * t:8 * t + 8]
        cb = colbuf[:, 32 * t:32 * t + 32]
        P.op("DVE", (tag, "negmax", t), [(tag, "cbz"), (tag, "lgn", t)],
             lambda e, cb=cb, L=L: e.reduce_max(cb[:, 0:1], L, axis=AX, negate=True))
        P.op("ACT", (tag, "exp", t), [(tag, "negmax", t)],
             lambda e, R=R, L=L, cb=cb: e.activation(R, L, AF.Exp, bias=cb[:, 0:1], scale=1.0,
                                                     accum_out=cb[:, 1:2]))
        P.op("DVE", None, [(tag, "exp", t)],
             lambda e, cb=cb: e.reciprocal(cb[:, 2:3], cb[:, 1:2]))
        P.op("DVE", None, (), lambda e, R=R, cb=cb: e.tensor_scalar(R, R, cb[:, 2:3], None, ALU.mult))
        P.op("DVE", (tag, "g1", t), (), lambda e, cb=cb, R=R: e.reduce_max(cb[:, 3:4], R, axis=AX))
        P.op("DVE", None, (), lambda e, R=R, cb=cb: e.tensor_scalar(scr[:], R, cb[:, 3:4], None, ALU.is_equal))
        P.op("DVE", None, (), lambda e: e.tensor_tensor(scr[:], scr[:], cs["iotam999"][:], ALU.mult))
        P.op("DVE", None, (), lambda e: e.tensor_scalar(scr[:], scr[:], 999.0, None, ALU.add))
        P.op("DVE", None, (), lambda e, cb=cb: e.tensor_reduce(cb[:, 4:5], scr[:], axis=AX, op=ALU.min))
        P.op("DVE", (tag, "m1", t), (),
             lambda e, t=t, cb=cb: e.tensor_scalar(m1b[:, 8 * t:8 * t + 8], cs["iota8h"][:],
                                                   cb[:, 4:5], None, ALU.is_equal))
        P.op("DVE", None, (), lambda e, t=t, R=R: e.tensor_tensor(scr2[:], m1b[:, 8 * t:8 * t + 8], R, ALU.mult))
        P.op("DVE", None, (), lambda e, R=R: e.tensor_tensor(scr2[:], R, scr2[:], ALU.subtract))
        P.op("DVE", (tag, "g2", t), (), lambda e, cb=cb: e.reduce_max(cb[:, 5:6], scr2[:], axis=AX))
        P.op("DVE", None, (), lambda e, cb=cb: e.tensor_scalar(scr[:], scr2[:], cb[:, 5:6], None, ALU.is_equal))
        P.op("DVE", None, (), lambda e: e.tensor_tensor(scr[:], scr[:], cs["iotam999"][:], ALU.mult))
        P.op("DVE", None, (), lambda e: e.tensor_scalar(scr[:], scr[:], 999.0, None, ALU.add))
        P.op("DVE", None, (), lambda e, cb=cb: e.tensor_reduce(cb[:, 6:7], scr[:], axis=AX, op=ALU.min))
        P.op("DVE", (tag, "m2", t), (),
             lambda e, t=t, cb=cb: e.tensor_scalar(m2b[:, 8 * t:8 * t + 8], cs["iota8h"][:],
                                                   cb[:, 6:7], None, ALU.is_equal))
    P.run()

    for mx in (1, 2):
        mk = m1kb if mx == 1 else m2kb
        pm = p1m if mx == 1 else p2m
        runx = run1 if mx == 1 else run2
        mb_ = m1b if mx == 1 else m2b
        P = Plan(ctx, tag + f"G{mx + 1}")
        if mx == 1:
            P.op("DVE", (tag, "run1", 0), (), lambda e: e.memset(run1[:], 0.0))
        else:
            for t in range(NT):
                P.op("PE", (tag, "m1cnt", t), [(tag, "p1ev", NT - 1), (tag, "run1", NT)],
                     lambda e, t=t: e.matmul(pcs[:], cs["onescolf"][:], m1kb[:, 8 * t:8 * t + 8],
                                             start=(t == 0), stop=(t == NT - 1)))
            P.op("DVE", (tag, "run2", 0), [(tag, "m1cnt", NT - 1)],
                 lambda e: e.tensor_copy(run2[:], pcs[0:1, :]))
        for t in range(NT):
            cb = colbuf[:, 32 * t:32 * t + 32]
            deps = [(tag, f"run{mx}", t), (tag, "m2", NT - 1)]
            if t > 0:
                deps.append((tag, f"p{mx}ev", t - 1))
            P.op("PE", None, deps,
                 lambda e, t=t, mb_=mb_: e.matmul(pp[:], cs["triu"][:], mb_[:, 8 * t:8 * t + 8],
                                                  start=True, stop=False))
            P.op("PE", (tag, f"ppmm{mx}", t), (),
                 lambda e, runx=runx: e.matmul(pp[:], cs["ones1f"][:], runx[:], start=False, stop=True))
            P.op("PE", (tag, f"csmm{mx}", t), [(tag, f"run{mx}", t)],
                 lambda e, t=t, mb_=mb_: e.matmul(pcs[:], cs["onescol"][:], mb_[:, 8 * t:8 * t + 8],
                                                  start=True, stop=True))
            P.op("DVE", None, [(tag, f"ppmm{mx}", t)],
                 lambda e, t=t, pm=pm: e.tensor_copy(pm[:, 8 * t:8 * t + 8], pp[:]))
            P.op("DVE", (tag, f"p{mx}ev", t), (),
                 lambda e, t=t, mk=mk, mb_=mb_, cb=cb: e.scalar_tensor_tensor(
                     mk[:, 8 * t:8 * t + 8], pp[:], C, mb_[:, 8 * t:8 * t + 8],
                     ALU.is_lt, ALU.mult, accum_out=cb[:, 6 + mx:7 + mx]))
            P.op("DVE", (tag, f"run{mx}", t + 1), [(tag, f"csmm{mx}", t)],
                 lambda e, runx=runx: e.tensor_tensor(runx[:], runx[:], pcs[0:1, :], ALU.add))
        P.run()

    P = Plan(ctx, tag + "G5")
    for t in range(NT):
        cb = colbuf[:, 32 * t:32 * t + 32]
        P.op("DVE", None, [(tag, "p2ev", NT - 1), (tag, "run2", NT)],
             lambda e, t=t, cb=cb: e.scalar_tensor_tensor(
                 scr[:], p1m[:, 8 * t:8 * t + 8], 1.0, m1kb[:, 8 * t:8 * t + 8],
                 ALU.mult, ALU.mult, accum_out=cb[:, 9:10]))
        P.op("DVE", None, (),
             lambda e, t=t, cb=cb: e.scalar_tensor_tensor(
                 scr[:], p2m[:, 8 * t:8 * t + 8], 1.0, m2kb[:, 8 * t:8 * t + 8],
                 ALU.mult, ALU.mult, accum_out=cb[:, 10:11]))
        P.op("DVE", None, (), lambda e, cb=cb: e.tensor_scalar(cb[:, 11:12], cb[:, 4:5], 1024.0, None, ALU.mult))
        P.op("DVE", None, (), lambda e, cb=cb: e.tensor_scalar(cb[:, 12:13], cb[:, 7:8], 1.0, 99999.0,
                                                               ALU.subtract, ALU.mult))
        P.op("DVE", None, (), lambda e, cb=cb: e.tensor_tensor(cb[:, 11:12], cb[:, 11:12], cb[:, 9:10], ALU.add))
        P.op("DVE", None, (), lambda e, cb=cb: e.tensor_tensor(cb[:, 11:12], cb[:, 11:12], cb[:, 12:13], ALU.add))
        P.op("DVE", None, (), lambda e, cb=cb: e.tensor_tensor(cb[:, 15:16], cb[:, 9:10], cb[:, 12:13], ALU.add))
        P.op("DVE", None, (), lambda e, cb=cb: e.tensor_scalar(cb[:, 13:14], cb[:, 6:7], 1024.0, None, ALU.mult))
        P.op("DVE", None, (), lambda e, cb=cb: e.tensor_scalar(cb[:, 14:15], cb[:, 8:9], 1.0, 99999.0,
                                                               ALU.subtract, ALU.mult))
        P.op("DVE", None, (), lambda e, cb=cb: e.tensor_tensor(cb[:, 13:14], cb[:, 13:14], cb[:, 10:11], ALU.add))
        P.op("DVE", None, (), lambda e, cb=cb: e.tensor_tensor(cb[:, 13:14], cb[:, 13:14], cb[:, 14:15], ALU.add))
        P.op("DVE", None, (), lambda e, cb=cb: e.tensor_tensor(cb[:, 16:17], cb[:, 10:11], cb[:, 14:15], ALU.add))
        P.op("DVE", None, [(tag, "g1", t)],
             lambda e, t=t, cb=cb: e.tensor_scalar(g1k[:, 8 * t:8 * t + 8], m1kb[:, 8 * t:8 * t + 8],
                                                   cb[:, 3:4], None, ALU.mult))
        P.op("DVE", None, [(tag, "g2", t)],
             lambda e, t=t, cb=cb: e.tensor_scalar(g2k[:, 8 * t:8 * t + 8], m2kb[:, 8 * t:8 * t + 8],
                                                   cb[:, 5:6], None, ALU.mult))
        P.op("DVE", None, (),
             lambda e, t=t, cb=cb: e.tensor_scalar(oc1[:, 1024 * t:1024 * t + 1024],
                                                   cs["iota1024h"][:], cb[:, 15:16], None, ALU.is_equal))
        P.op("DVE", (tag, "oc", t), (),
             lambda e, t=t, cb=cb: e.tensor_scalar(oc2[:, 1024 * t:1024 * t + 1024],
                                                   cs["iota1024h"][:], cb[:, 16:17], None, ALU.is_equal))
        for h in range(2):
            P.op("PE", None, [(tag, "oc", t)],
                 lambda e, t=t, h=h: e.matmul(gps[:, 512 * h:512 * h + 512], g1k[:, 8 * t:8 * t + 8],
                                              oc1[:, 1024 * t + 512 * h:1024 * t + 512 * h + 512],
                                              start=(t == 0), stop=False))
            P.op("PE", (tag, "gpsmm", t, h), (),
                 lambda e, t=t, h=h: e.matmul(gps[:, 512 * h:512 * h + 512], g2k[:, 8 * t:8 * t + 8],
                                              oc2[:, 1024 * t + 512 * h:1024 * t + 512 * h + 512],
                                              start=False, stop=(t == NT - 1)))
    P.op("ACT", (tag, "gsTc"), [(tag, "gpsmm", NT - 1, 1), (tag, "lgn", NT - 1)],
         lambda e: e.copy(gsT[:], gps[:]))
    for w in range(8):
        deps = [(tag, "gsTc")]
        if w > 0:
            deps.append((tag, "gsb", w - 1))
        P.op("PE", (tag, "gstr", w), deps,
             lambda e, w=w: e.transpose(ptp[:], gsT[:, 128 * w:128 * w + 128], cs["ident8"][:]))
        P.op("ACT", (tag, "gsb", w), [(tag, "gstr", w)],
             lambda e, w=w: e.activation(gsb[:, 8 * w:8 * w + 8], ptp[:], AF.Copy,
                                         bias=0.0, scale=0.5))
    P.run()


def _emit_dispatch(ctx, tag, xhi_d, eiT_d, oc1, oc2, m1kb, m2kb, sb1, ps1):
    xw = sb1("xw", [128, 4 * M], BF16)
    dsb = sb1("dsb", [128, 4 * 512], BF16)
    eisb = sb1("eisb", [128, 2 * 8 * 512], BF16)
    pei = [ps1(f"pei{i}", [128, 512]) for i in range(8)]

    P = Plan(ctx, tag + "F2")
    for w in range(16):
        ex = w // 2
        lo = 512 * (w % 2)
        for t in range(NT):
            i = w * NT + t
            b = i % 4
            D = dsb[:, 512 * b:512 * b + 512]
            X = xw[:, M * b:M * b + M]
            deps = [(tag, "sthi", t)]
            if i >= 4:
                deps.append((tag, "dmm", i - 4))
            P.dma("SP", (tag, "xw", i), deps,
                  lambda e, X=X, t=t: e.dma_start(X, xhi_d[128 * t:128 * t + 128, :]), stream="xw", idx=i, pool=4)
            deps = [(tag, "oc", NT - 1)]
            if i >= 4:
                deps.append((tag, "dmm", i - 4))
            P.op("DVE", None, deps,
                 lambda e, D=D, t=t, lo=lo, ex=ex: e.tensor_scalar(
                     D, oc1[:, 1024 * t + lo:1024 * t + lo + 512],
                     m1kb[:, 8 * t + ex:8 * t + ex + 1], None, ALU.mult))
            P.op("DVE", (tag, "dsum", i), (),
                 lambda e, D=D, t=t, lo=lo, ex=ex: e.scalar_tensor_tensor(
                     D, oc2[:, 1024 * t + lo:1024 * t + lo + 512],
                     m2kb[:, 8 * t + ex:8 * t + ex + 1], D, ALU.mult, ALU.add))
            for mt in range(8):
                deps = []
                if mt == 0:
                    deps = [(tag, "dsum", i), (tag, "xw", i)]
                    if t == 0 and w >= 1:
                        deps += [(tag, "eiev", w - 1, mt2) for mt2 in range(8)]
                P.op("PE", (tag, "dmm", i) if mt == 7 else None, deps,
                     lambda e, mt=mt, X=X, D=D, t=t: e.matmul(
                         pei[mt][:], X[:, 128 * mt:128 * mt + 128], D,
                         start=(t == 0), stop=(t == NT - 1)))
        for mt in range(8):
            deps = [(tag, "dmm", w * NT + NT - 1)]
            if w >= 2:
                deps.append((tag, "eist", w - 2, 7))
            P.op("ACT", (tag, "eiev", w, mt), deps,
                 lambda e, w=w, mt=mt: e.copy(
                     eisb[:, (w % 2) * 4096 + 512 * mt:(w % 2) * 4096 + 512 * mt + 512],
                     pei[mt][:]))
            P.dma("SP", (tag, "eist", w, mt), [(tag, "eiev", w, mt)],
                  lambda e, w=w, mt=mt: e.dma_start(
                      eiT_d[w, mt],
                      eisb[:, (w % 2) * 4096 + 512 * mt:(w % 2) * 4096 + 512 * mt + 512]), stream="eiw", idx=w, pool=4)
    P.run()


def _emit_experts(ctx, tag, W, eiT_d, eo_d, gsb):
    nc = ctx.nc
    HC = 512
    halves = [(0, HSPLIT), (HSPLIT, NHT)]
    with contextlib.ExitStack() as p3:
        sb3 = lambda n, s, d: p3.enter_context(nc.sbuf_tensor(f"{tag}3_{n}", s, d))
        ps3 = lambda n, s: p3.enter_context(nc.psum_tensor(f"{tag}3_{n}", s, F32))
        hidT = sb3("hidT", [128, 23 * 1024], BF16)
        wores = sb3("wores", [128, 23 * 1024], BF16)
        eoacc = sb3("eoacc", [128, 8 * 1024], F32)
        eib = sb3("eib", [128, 16 * 512], BF16)
        w0b = sb3("w0b", [128, 2 * 8 * HC], BF16)
        w1b = sb3("w1b", [128, 2 * 8 * HC], BF16)
        sqb = sb3("sqb", [128, 1024], F32)
        u2b = sb3("u2b", [128, 1024], F32)
        gb = sb3("gb", [128, 1024], F32)
        eosb = sb3("eosb", [128, 2 * 1024], BF16)
        ph0 = [ps3(f"ph0{i}", [128, 1024]) for i in range(2)]
        ph1 = [ps3(f"ph1{i}", [128, 1024]) for i in range(2)]

        wmmC_key = {}
        g = 0
        for ex_ in range(8):
            for hh_, (ht0_, ht1_) in enumerate(halves):
                hbase_ = 128 * ht0_
                nchunk_ = (128 * ht1_ - hbase_ + HC - 1) // HC
                for lc_ in range(nchunk_):
                    last_ht = min(ht1_ - 1, (hbase_ + (lc_ + 1) * HC - 1) // 128)
                    wmmC_key[g] = (tag, "wmmht", ex_, last_ht)
                    g += 1

        P = Plan(ctx, tag + "F3")
        gw = [0]
        prev_tg = [None]
        for ex in range(8):
            for wi in range(2):
                w = 2 * ex + wi
                for mt in range(8):
                    deps = [(tag, "eist", w, 7)]
                    if ex > 0:
                        deps.append((tag, "wmmht", ex - 1, NHT - 1))
                    if ex == 0:
                        deps += [(tag, "eist", 15, 7), (tag, "eist", 14, 7), (tag, "xw", 16 * NT - 1)]
                    P.dma("SP", (tag, "eild", ex, wi, mt), deps,
                          lambda e, wi=wi, mt=mt, w=w: e.dma_start(
                              eib[:, wi * 4096 + 512 * mt:wi * 4096 + 512 * mt + 512],
                              eiT_d[w, mt]), stream="eild", idx=ex, pool=3)
            for hh, (ht0, ht1) in enumerate(halves):
                hbase = 128 * ht0
                nchunk = (128 * ht1 - hbase + HC - 1) // HC
                for lc in range(nchunk):
                    cw = min(HC, H - hbase - HC * lc)
                    b = gw[0] % 2
                    deps = []
                    if gw[0] >= 2:
                        deps.append(wmmC_key[gw[0] - 2])
                    if gw[0] < 2:
                        deps += [(tag, "eist", 15, 7), (tag, "eist", 14, 7), (tag, "xw", 16 * NT - 1)]
                    P.dma("POOL", (tag, "w0", ex, hh, lc), deps,
                          lambda e, b=b, cw=cw, ex=ex, h0=hbase + HC * lc: e.dma_start(
                              w0b[:, b * 8 * HC:b * 8 * HC + 8 * cw].rearrange("p (c h) -> p c h", c=8),
                              W["w0"][ex].rearrange("(c p) h -> p c h", p=128)[:, :, h0:h0 + cw]),
                          stream="w01", idx=gw[0], pool=3)
                    P.dma("POOL", (tag, "w1", ex, hh, lc), (),
                          lambda e, b=b, cw=cw, ex=ex, h0=hbase + HC * lc: e.dma_start(
                              w1b[:, b * 8 * HC:b * 8 * HC + 8 * cw].rearrange("p (c h) -> p c h", c=8),
                              W["w1"][ex].rearrange("(c p) h -> p c h", p=128)[:, :, h0:h0 + cw]),
                          stream="w01", idx=gw[0], pool=3)
                    gw[0] += 1
                for ht in range(ht0, ht1):
                    hrows = min(128, H - 128 * ht)
                    deps = []
                    if (ex, hh) > (0, 0):
                        pex, phh = (ex, hh - 1) if hh == 1 else (ex - 1, 1)
                        deps.append((tag, "odone", pex, phh))
                    else:
                        deps += [(tag, "eist", 15, 7), (tag, "eist", 14, 7), (tag, "xw", 16 * NT - 1)]
                    P.dma("POOL", (tag, "wo", ex, ht), deps,
                          lambda e, hrows=hrows, ex=ex, ht=ht, ht0=ht0: e.dma_start(
                              wores[0:hrows, 1024 * (ht - ht0):1024 * (ht - ht0) + 1024],
                              W["wo"][ex][128 * ht:128 * ht + hrows, :]),
                          stream="woh", idx=2 * ex + hh, pool=2)
                for ht in range(ht0, ht1):
                    lc = (128 * ht - hbase) // HC
                    cw = min(HC, H - hbase - HC * lc)
                    off = 128 * ht - hbase - HC * lc
                    hrows = min(128, H - 128 * ht)
                    b = (gw[0] - nchunk + lc) % 2
                    pb = ht % 2
                    for half in range(2):
                        for mt in range(8):
                            deps = []
                            if mt == 0:
                                deps = [(tag, "w0", ex, hh, lc), (tag, "w1", ex, hh, lc),
                                        (tag, "eild", ex, 1, 7)]
                                gi = ex * NHT + ht
                                if gi >= 2 and half == 0:
                                    pht = ht - 2
                                    pe_ = ex
                                    if pht < 0:
                                        pe_ -= 1
                                        pht += NHT
                                    deps.append((tag, "hid", pe_, pht))
                                if ht - ht0 < 2 and half == 0 and (ex, hh) > (0, 0):
                                    pex, phh = (ex, 0) if hh == 1 else (ex - 1, 1)
                                    for ctp in (6, 7):
                                        for mwp in (0, 1):
                                            deps.append((tag, "eoh", pex, phh, ctp, mwp))
                            P.op("PE", None, deps,
                                 lambda e, pb=pb, hrows=hrows, half=half, b=b, mt=mt, cw=cw,
                                        off=off: e.matmul(
                                     ph0[pb][0:hrows, 512 * half:512 * half + 512],
                                     w0b[:, b * 8 * HC + mt * cw + off:
                                            b * 8 * HC + mt * cw + off + hrows],
                                     eib[:, half * 4096 + 512 * mt:half * 4096 + 512 * mt + 512],
                                     start=(mt == 0), stop=(mt == 7)))
                        for mt in range(8):
                            key = (tag, "wmmht", ex, ht) if (half == 1 and mt == 7) else None
                            P.op("PE", key, (),
                                 lambda e, pb=pb, hrows=hrows, half=half, b=b, mt=mt, cw=cw,
                                        off=off: e.matmul(
                                     ph1[pb][0:hrows, 512 * half:512 * half + 512],
                                     w1b[:, b * 8 * HC + mt * cw + off:
                                            b * 8 * HC + mt * cw + off + hrows],
                                     eib[:, half * 4096 + 512 * mt:half * 4096 + 512 * mt + 512],
                                     start=(mt == 0), stop=(mt == 7)))
                    if ht == ht1 - 1:
                        ctxkey = (tag, "wmmC", gw[0] - nchunk + lc)
                    SQ = sqb[0:hrows, :]
                    U2 = u2b[0:hrows, :]
                    G = gb[0:hrows, :]
                    HD = hidT[0:hrows, 1024 * (ht - ht0):1024 * (ht - ht0) + 1024]
                    deps_sq = [(tag, "wmmht", ex, ht)]
                    if prev_tg[0] is not None:
                        deps_sq.append(prev_tg[0])
                    P.op("ACT", (tag, "sq", ex, ht), deps_sq,
                         lambda e, SQ=SQ, pb=pb, hrows=hrows: e.activation(
                             SQ, ph0[pb][0:hrows, :], AF.Square, bias=0.0, scale=GELU_SA))
                    P.op("DVE", (tag, "u2", ex, ht), [(tag, "sq", ex, ht)],
                         lambda e, U2=U2, SQ=SQ, pb=pb, hrows=hrows: e.scalar_tensor_tensor(
                             U2, SQ, 1.0, ph0[pb][0:hrows, :], ALU.add, ALU.mult))
                    P.op("ACT", (tag, "sg", ex, ht), [(tag, "u2", ex, ht)],
                         lambda e, SQ=SQ, U2=U2: e.activation(SQ, U2, AF.Sigmoid,
                                                              bias=0.0, scale=GELU_K))
                    P.op("DVE", (tag, "tg", ex, ht), [(tag, "sg", ex, ht)],
                         lambda e, G=G, SQ=SQ, pb=pb, hrows=hrows: e.tensor_tensor(
                             G, SQ, ph0[pb][0:hrows, :], ALU.mult))
                    prev_tg[0] = (tag, "tg", ex, ht)
                    deps = []
                    if (ex, hh) > (0, 0):
                        pex, phh = (ex, 0) if hh == 1 else (ex - 1, 1)
                        deps.append((tag, "odone", pex, phh))
                    P.op("DVE", (tag, "hid", ex, ht), deps,
                         lambda e, G=G, pb=pb, hrows=hrows, HD=HD: e.tensor_tensor(
                             HD, G, ph1[pb][0:hrows, :], ALU.mult))
                for ct in range(8):
                    pb = ct % 2
                    for mw in range(2):
                        for ht in range(ht0, ht1):
                            hrows = min(128, H - 128 * ht)
                            deps = []
                            if ht == ht0:
                                deps = [(tag, "wo", ex, ht1 - 1),
                                        (tag, "hid", ex, ht1 - 1), (tag, "hid", ex, ht1 - 2)]
                                if ct >= 2:
                                    deps += [(tag, "eoh", ex, hh, ct - 2, 0),
                                             (tag, "eoh", ex, hh, ct - 2, 1)]
                            key = (tag, "omm", ex, hh, ct, mw) if ht == ht1 - 1 else None
                            P.op("PE", key, deps,
                                 lambda e, pb=pb, mw=mw, hrows=hrows, ht=ht, ct=ct,
                                        ht0=ht0: e.matmul(
                                     ph0[pb][0:128, 512 * mw:512 * mw + 512],
                                     hidT[0:hrows, 1024 * (ht - ht0) + 128 * ct:
                                              1024 * (ht - ht0) + 128 * ct + 128],
                                     wores[0:hrows, 1024 * (ht - ht0) + 512 * mw:
                                               1024 * (ht - ht0) + 512 * mw + 512],
                                     start=(ht == ht0), stop=(ht == ht1 - 1),
                                     skip_group_check=True))
                        EA = eoacc[:, 1024 * ct + 512 * mw:1024 * ct + 512 * mw + 512]
                        if hh == 0:
                            P.op("ACT", (tag, "eoh", ex, 0, ct, mw),
                                 [(tag, "omm", ex, 0, ct, 1)],
                                 lambda e, EA=EA, pb=pb, mw=mw: e.copy(
                                     EA, ph0[pb][0:128, 512 * mw:512 * mw + 512]))
                        else:
                            P.op("DVE", (tag, "eoh", ex, 1, ct, mw),
                                 [(tag, "omm", ex, 1, ct, 1), (tag, "eoh", ex, 0, ct, mw)],
                                 lambda e, EA=EA, pb=pb, mw=mw: e.tensor_tensor(
                                     EA, EA, ph0[pb][0:128, 512 * mw:512 * mw + 512], ALU.add))
                            deps = [(tag, "eoh", ex, 1, ct, mw), (tag, "gsb", ct)]
                            if ct >= 2:
                                deps.append((tag, "eost", ex, ct - 2))
                            elif ex > 0:
                                deps.append((tag, "eost", ex - 1, ct + 6))
                            P.op("ACT", (tag, "eoev", ex, ct, mw), deps,
                                 lambda e, ct=ct, mw=mw, ex=ex, EA=EA: e.activation(
                                     eosb[:, (ct % 2) * 1024 + 512 * mw:
                                              (ct % 2) * 1024 + 512 * mw + 512],
                                     EA, AF.Copy, bias=0.0,
                                     scale=gsb[:, 8 * ct + ex:8 * ct + ex + 1]))
                    if hh == 1:
                        P.dma("SP", (tag, "eost", ex, ct),
                              [(tag, "eoev", ex, ct, 0), (tag, "eoev", ex, ct, 1)],
                              lambda e, ex=ex, ct=ct: e.dma_start(
                                  eo_d[ex, ct], eosb[:, (ct % 2) * 1024:(ct % 2) * 1024 + 1024]), stream="eost", idx=ex * 8 + ct, pool=3)
                P.op("PE", (tag, "odone", ex, hh), [(tag, "omm", ex, hh, 7, 1)],
                     lambda e: e.nop(), ser=False)
        P.run()


def _emit_combine(ctx, tag, eo_d, colbuf, x_src, x_dst):
    nc = ctx.nc
    cs = ctx.cs
    with contextlib.ExitStack() as p4:
        sb4 = lambda n, s, d: p4.enter_context(nc.sbuf_tensor(f"{tag}4_{n}", s, d))
        bc1 = sb4("bc1", [128, 512 * 8], F32)
        bc2 = sb4("bc2", [128, 512 * 8], F32)
        diag = sb4("diag", [128, 4 * 128], F32)
        dsT = sb4("dsT", [128, 8 * 512], BF16)
        eob = sb4("eob", [128, 8 * 1024], BF16)
        resb = sb4("resb", [128, 4 * M], F32)
        outb = sb4("outb", [128, 4 * M], F32)

        with contextlib.ExitStack() as p4a:
            pbc1 = p4a.enter_context(nc.psum_tensor(f"{tag}4_pbc1", [128, 512], F32))
            pbc2 = p4a.enter_context(nc.psum_tensor(f"{tag}4_pbc2", [128, 512], F32))
            P = Plan(ctx, tag + "F4a")
            for t in range(NT):
                b = t % 4
                w, j = divmod(t, 4)
                DG = diag[:, 128 * b:128 * b + 128]
                deps = [(tag, "eost", 7, 4), (tag, "eost", 7, 5), (tag, "eost", 7, 6), (tag, "eost", 7, 7)]
                if t >= 4:
                    deps.append((tag, "bmm2", t - 4))
                P.op("DVE", (tag, "dg1", t), deps,
                     lambda e, DG=DG, t=t: e.tensor_scalar(
                         DG, cs["ident128f"][:], colbuf[:, 32 * t + 11:32 * t + 12], None, ALU.mult))
                deps = [(tag, "dg1", t)]
                if w > 0 and j == 0:
                    deps.append((tag, "bcev1", w - 1))
                P.op("PE", (tag, "bmm1", t), deps,
                     lambda e, DG=DG, j=j: e.matmul(pbc1[:, 128 * j:128 * j + 128],
                                                    cs["ones128f"][:], DG, start=True, stop=True))
                P.op("DVE", (tag, "dg2", t), [(tag, "bmm1", t)],
                     lambda e, DG=DG, t=t: e.tensor_scalar(
                         DG, cs["ident128f"][:], colbuf[:, 32 * t + 13:32 * t + 14], None, ALU.mult))
                deps = [(tag, "dg2", t)]
                if w > 0 and j == 0:
                    deps.append((tag, "bcev2", w - 1))
                P.op("PE", (tag, "bmm2", t), deps,
                     lambda e, DG=DG, j=j: e.matmul(pbc2[:, 128 * j:128 * j + 128],
                                                    cs["ones128f"][:], DG, start=True, stop=True))
                if j == 3:
                    P.op("ACT", (tag, "bcev1", w), [(tag, "bmm1", t)],
                         lambda e, w=w: e.copy(bc1[:, 512 * w:512 * w + 512], pbc1[:]))
                    P.op("ACT", (tag, "bcev2", w), [(tag, "bmm2", t)],
                         lambda e, w=w: e.copy(bc2[:, 512 * w:512 * w + 512], pbc2[:]))
            P.run()

        with contextlib.ExitStack() as p4b:
            pcmb = [p4b.enter_context(nc.psum_tensor(f"{tag}4_pcmb{i}", [128, 512], F32))
                    for i in range(8)]
            P = Plan(ctx, tag + "F4b")
            for w in range(8):
                for ch in range(64):
                    i = w * 64 + ch
                    b = i % 8
                    DT = dsT[:, 512 * b:512 * b + 512]
                    EB = eob[:, 1024 * b:1024 * b + 1024]
                    deps = list(((tag, "eost", 7, 4), (tag, "eost", 7, 5), (tag, "eost", 7, 6), (tag, "eost", 7, 7)))
                    if i >= 8:
                        deps.append((tag, "cmm", i - 8))
                    P.dma("SP", (tag, "eol", i), deps,
                          lambda e, EB=EB, ch=ch: e.dma_start(EB, eo_d[ch // 8, ch % 8]), stream="eol", idx=i, pool=8)
                    deps = [(tag, "bcev1", w), (tag, "bcev2", w)]
                    if i >= 8:
                        deps.append((tag, "cmm", i - 8))
                    P.op("DVE", None, deps,
                         lambda e, DT=DT, w=w, ch=ch: e.tensor_scalar(
                             DT, bc1[:, 512 * w:512 * w + 512], cs["iotapc"][:, ch:ch + 1],
                             None, ALU.is_equal))
                    P.op("DVE", (tag, "dsT", i), (),
                         lambda e, DT=DT, w=w, ch=ch: e.scalar_tensor_tensor(
                             DT, bc2[:, 512 * w:512 * w + 512], cs["iotapc"][:, ch:ch + 1],
                             DT, ALU.is_equal, ALU.add))
                    for idx in range(8):
                        sj, mw = divmod(idx, 2)
                        deps = []
                        if idx == 0:
                            deps = [(tag, "dsT", i), (tag, "eol", i)]
                            if ch == 0 and w > 0:
                                deps += [(tag, "outev", w - 1, k) for k in range(8)]
                        key = (tag, "cmm", i) if idx == 7 else None
                        P.op("PE", key, deps,
                             lambda e, idx=idx, sj=sj, mw=mw, DT=DT, EB=EB, ch=ch: e.matmul(
                                 pcmb[idx][:], DT[:, 128 * sj:128 * sj + 128],
                                 EB[:, 512 * mw:512 * mw + 512],
                                 start=(ch == 0), stop=(ch == 63)))
                for sj in range(4):
                    t = 4 * w + sj
                    b = t % 4
                    RS = resb[:, M * b:M * b + M]
                    OB = outb[:, M * b:M * b + M]
                    deps = [(tag, "outst", t - 4)] if t >= 4 else [(tag, "eost", 7, 4), (tag, "eost", 7, 5), (tag, "eost", 7, 6), (tag, "eost", 7, 7)]
                    P.dma("SP", (tag, "res", t), deps,
                          lambda e, RS=RS, t=t: e.dma_start(RS, x_src[128 * t:128 * t + 128, :]), stream="res", idx=t, pool=3)
                    for mw in range(2):
                        idx = sj * 2 + mw
                        P.op("DVE", (tag, "outev", w, idx),
                             [(tag, "cmm", w * 64 + 63), (tag, "res", t)],
                             lambda e, OB=OB, RS=RS, idx=idx, mw=mw: e.tensor_tensor(
                                 OB[:, 512 * mw:512 * mw + 512], pcmb[idx][:],
                                 RS[:, 512 * mw:512 * mw + 512], ALU.add))
                    P.dma("SP", (tag, "outst", t),
                          [(tag, "outev", w, sj * 2), (tag, "outev", w, sj * 2 + 1)],
                          lambda e, OB=OB, t=t: e.dma_start(x_dst[128 * t:128 * t + 128, :], OB), stream="outst", idx=t, pool=4)
            P.run()


def attn_perm():
    perm = np.zeros(M, dtype=np.int64)
    for h in range(NHEAD):
        for j in range(32):
            perm[64 * h + j] = 64 * h + 2 * j
            perm[64 * h + 32 + j] = 64 * h + 2 * j + 1
    return perm




def _emit_rope2(P, tag, key, i, prev, psrc, dst, tabC, tabS, rtmp, rps, sw0, extra0=()):
    pre = P._last.get("DVE")
    base = [prev] + list(extra0)
    if pre is not None:
        base.append(("_idx_", pre))
    CO = tabC[:, 512 * sw0:512 * sw0 + 1024]
    SI = tabS[:, 512 * sw0:512 * sw0 + 1024]
    P.op("DVE", None, base,
         lambda e: e.tensor_tensor(rps[:], psrc[:], CO, ALU.mult))
    P.op("DVE", None, (),
         lambda e: e.tensor_tensor(rtmp[:], psrc[:], SI, ALU.mult))
    for B in (0, 64):
        P.op("DVE", None, (), lambda e, B=B: e.tensor_tensor(
            dst[B:B + 32, :], rps[B:B + 32, :], rtmp[B + 32:B + 64, :], ALU.subtract))
        P.op("DVE", key if B == 64 else None, (), lambda e, B=B: e.tensor_tensor(
            dst[B + 32:B + 64, :], rtmp[B:B + 32, :], rps[B + 32:B + 64, :], ALU.add))


def emit_attention(ctx, tag, x_src, x_dst, W):
    nc = ctx.nc
    cs = ctx.cs
    xln_d = nc.dram_tensor(f"{tag}_xln", [S, M], BF16, kind="Internal")
    qT_d = nc.dram_tensor(f"{tag}_qT", [8, 128, S], BF16, kind="Internal")
    latT_d = nc.dram_tensor(f"{tag}_latT", [4, 128, S], BF16, kind="Internal")
    ropes = make_rope_consts()
    cosF_d = nc.dram_tensor(f"{tag}_cosF", [128, S], BF16, kind="ExternalInput")
    sinF_d = nc.dram_tensor(f"{tag}_sinF", [128, S], BF16, kind="ExternalInput")
    ctx.rope_inputs = {
        f"{tag}_cosF": ropes["ropeA"].astype(ml_dtypes.bfloat16),
        f"{tag}_sinF": ropes["ropeB"].astype(ml_dtypes.bfloat16)}

    rope_scope = contextlib.ExitStack()
    sbr = lambda n, s, d: rope_scope.enter_context(
        nc.sbuf_tensor(f"{tag}_{n}", s, d, side="right"))
    cosF = sbr("cosFs", [128, S], BF16)
    sinF = sbr("sinFs", [128, S], BF16)
    rtmp = sbr("rtmp", [128, 1024], F32)

    with contextlib.ExitStack() as p1:
        sb1 = lambda n, s, d: p1.enter_context(nc.sbuf_tensor(f"{tag}1_{n}", s, d))
        xt = sb1("xt", [128, 2 * M], F32)
        st6 = sb1("st6", [128, 2 * 16], F32)
        mvb = sb1("mvb", [128, 4 * NT], F32)
        xlnb = sb1("xlnb", [128, 2 * M], BF16)
        P = Plan(ctx, tag + "A1")
        P.dma("SP", (tag, "cosF"), ["fence"], lambda e: e.dma_start(cosF[:], cosF_d[:]),
              stream="rope", pool=1)
        P.dma("SP", (tag, "sinF"), ["fence"], lambda e: e.dma_start(sinF[:], sinF_d[:]),
              stream="rope", pool=1)
        for t in range(NT):
            b = t % 2
            X = xt[:, M * b:M * b + M]
            S6 = st6[:, 16 * b:16 * b + 12]
            MV2 = st6[:, 16 * b + 12:16 * b + 14]
            VV = st6[:, 16 * b + 13:16 * b + 14]
            XL = xlnb[:, M * b:M * b + M]
            deps = ["fence"] + ([(tag, "xlnf", t - 2)] if t >= 2 else [])
            P.dma("SP", (tag, "xt", t), deps,
                  lambda e, X=X, t=t: e.dma_start(X, x_src[128 * t:128 * t + 128, :]),
                  stream="xt", idx=t, pool=3)
            deps = [(tag, "xt", t)] + ([(tag, "rstd", t - 2)] if t >= 2 else [])
            P.op("DVE", (tag, "bns", t), deps,
                 lambda e, S6=S6, X=X: e.bn_stats(S6[:, 0:6], X[:, 0:512]))
            P.op("DVE", None, (), lambda e, S6=S6, X=X: e.bn_stats(S6[:, 6:12], X[:, 512:1024]))
            P.op("DVE", (tag, "bn", t), (), lambda e, MV2=MV2, S6=S6: e.bn_aggr(MV2, S6))
            P.op("DVE", (tag, "mn", t), (),
                 lambda e, t=t, MV2=MV2: e.tensor_copy(mvb[:, 4 * t:4 * t + 1], MV2[:, 0:1]))
            P.op("ACT", None, [(tag, "bn", t)],
                 lambda e, VV=VV: e.activation(VV, VV, AF.Ln, bias=cs['eps5'][:, 0:1], scale=1.0))
            P.op("ACT", (tag, "rstd", t), (),
                 lambda e, t=t, VV=VV: e.activation(mvb[:, 4 * t + 1:4 * t + 2], VV, AF.Exp,
                                                    bias=0.0, scale=-0.5))
            deps = [(tag, "rstd", t)]
            if t >= 2:
                deps.append((tag, "stln", t - 2))
            P.op("DVE", (tag, "xlnf", t), deps,
                 lambda e, XL=XL, X=X, t=t: e.tensor_scalar(
                     XL, X, mvb[:, 4 * t:4 * t + 1], mvb[:, 4 * t + 1:4 * t + 2],
                     ALU.subtract, ALU.mult))
            P.dma("SP", (tag, "stln", t), [(tag, "xlnf", t)],
                  lambda e, XL=XL, t=t: e.dma_start(xln_d[128 * t:128 * t + 128, :], XL),
                  stream="sthi", idx=t, pool=3)
        P.run()

    with contextlib.ExitStack() as p2:
        sb2 = lambda n, s, d: p2.enter_context(nc.sbuf_tensor(f"{tag}2_{n}", s, d))
        ps2 = lambda n, s: p2.enter_context(nc.psum_tensor(f"{tag}2_{n}", s, F32))
        x2T = sb2("x2T", [128, 8 * S], BF16)
        wqb = sb2("wqb", [128, 8 * 1024], BF16)
        wdb = sb2("wdb", [128, 8 * 512], BF16)
        qsb = sb2("qsb", [128, 2 * 1024], BF16)
        latsb = sb2("latsb", [128, 2 * 1024], BF16)
        pq = [ps2(f"pq{i}", [128, 1024]) for i in range(2)]
        pl = ps2("pl", [128, 1024])
        rps2a = ps2("rps", [128, 1024])
        P = Plan(ctx, tag + "A2a")
        P.dma("POOL", (tag, "wqb"), [(tag, "stln", NT - 1)],
              lambda e: e.dma_start(wqb[:].rearrange("p (c f) -> p c f", c=8),
                                    W["wq"].rearrange("(c p) f -> p c f", p=128)),
              stream="watt", pool=1)
        P.dma("POOL", (tag, "wdb"), (),
              lambda e: e.dma_start(wdb[:].rearrange("p (c l) -> p c l", c=8),
                                    W["w_down"].rearrange("(c p) l -> p c l", p=128)),
              stream="watt", pool=1)
        for c in range(8):
            deps = [(tag, "stln", NT - k) for k in (1, 2, 3)]
            P.dma("SP", (tag, "x2T", c), deps,
                  lambda e, c=c: e.dma_start(x2T[:, S * c:S * c + S],
                                             xln_d[:, 128 * c:128 * c + 128], transpose=True),
                  stream="ldT", idx=c, pool=3)
        WQDEP = [(tag, "wqb"), (tag, "wdb")]
        for ft in range(8):
            for swp in range(4):
                i = ft * 4 + swp
                pb = i % 2
                for wi in range(2):
                    sw = 2 * swp + wi
                    for c in range(8):
                        deps = []
                        if c == 0 and wi == 0:
                            deps = WQDEP + [(tag, "x2T", cc) for cc in range(8)]
                            if i >= 2:
                                deps.append((tag, "rope", i - 2))
                        P.op("PE", (tag, "qmm", i) if (c == 7 and wi == 1) else None, deps,
                             lambda e, pb=pb, c=c, ft=ft, sw=sw, wi=wi: e.matmul(
                                 pq[pb][:, 512 * wi:512 * wi + 512],
                                 wqb[:, 1024 * c + 128 * ft:1024 * c + 128 * ft + 128],
                                 x2T[:, S * c + 512 * sw:S * c + 512 * sw + 512],
                                 start=(c == 0), stop=(c == 7)))
                QO = qsb[:, 1024 * pb:1024 * pb + 1024]
                extra0 = [(tag, "cosF"), (tag, "sinF")]
                if i >= 2:
                    extra0.append((tag, "qst", i - 2))
                _emit_rope2(P, tag, (tag, "rope", i), i, (tag, "qmm", i),
                            pq[pb], QO, cosF, sinF, rtmp, rps2a, 2 * swp, extra0=extra0)
                P.dma("SP", (tag, "qst", i), [(tag, "rope", i)],
                      lambda e, QO=QO, ft=ft, swp=swp: e.dma_start(
                          qT_d[ft][:, 1024 * swp:1024 * swp + 1024], QO),
                      stream="qst", idx=i, pool=3)
        for lt in range(4):
            for swp in range(4):
                i = lt * 4 + swp
                for wi in range(2):
                    sw = 2 * swp + wi
                    for c in range(8):
                        deps = []
                        if c == 0 and wi == 0:
                            deps = WQDEP + [(tag, "x2T", cc) for cc in range(8)]
                            if i >= 1:
                                deps.append((tag, "latev", i - 1))
                        P.op("PE", (tag, "lmm", i) if (c == 7 and wi == 1) else None, deps,
                             lambda e, c=c, lt=lt, sw=sw, wi=wi: e.matmul(
                                 pl[:, 512 * wi:512 * wi + 512],
                                 wdb[:, 512 * c + 128 * lt:512 * c + 128 * lt + 128],
                                 x2T[:, S * c + 512 * sw:S * c + 512 * sw + 512],
                                 start=(c == 0), stop=(c == 7)))
                LO = latsb[:, 1024 * (i % 2):1024 * (i % 2) + 1024]
                deps = [(tag, "lmm", i)]
                if i >= 2:
                    deps.append((tag, "latst", i - 2))
                P.op("ACT", (tag, "latev", i), deps,
                     lambda e, LO=LO: e.copy(LO, pl[:]))
                P.dma("SP", (tag, "latst", i), [(tag, "latev", i)],
                      lambda e, LO=LO, lt=lt, swp=swp: e.dma_start(
                          latT_d[lt][:, 1024 * swp:1024 * swp + 1024], LO),
                      stream="latst", idx=i, pool=3)
        P.run()

    with contextlib.ExitStack() as p34:
        sb34 = lambda n, s, d: p34.enter_context(nc.sbuf_tensor(f"{tag}34_{n}", s, d))
        kTs = sb34("kTs", [128, 8 * S], BF16)
        vab = sb34("vab", [128, NT * 1040], BF16)
        with contextlib.ExitStack() as p3:
            sb3 = lambda n, s, d: p3.enter_context(nc.sbuf_tensor(f"{tag}3_{n}", s, d))
            ps3 = lambda n, s: p3.enter_context(nc.psum_tensor(f"{tag}3_{n}", s, F32))
            latw = sb3("latw", [128, 2 * 4 * 1024], BF16)
            wkb = sb3("wkb", [128, 4 * 1024], BF16)
            wvb = sb3("wvb", [128, 4 * 1024], BF16)
            psk = [ps3(f"psk{i}", [128, 1024]) for i in range(2)]
            psv = [ps3(f"psv{i}", [128, 512]) for i in range(2)]
            rps2b = ps3("rps", [128, 1024])
            qstF = [(tag, "qst", 32 - k) for k in (1, 2, 3)]
            latF = [(tag, "latst", 16 - k) for k in (1, 2, 3)]
            P = Plan(ctx, tag + "A2b")
            P.dma("POOL", (tag, "wkb"), qstF + latF,
                  lambda e: e.dma_start(wkb[:].rearrange("p (c f) -> p c f", c=4),
                                        W["wk_up"].rearrange("(c p) f -> p c f", p=128)),
                  stream="watt", pool=1)
            P.dma("POOL", (tag, "wvb"), (),
                  lambda e: e.dma_start(wvb[:].rearrange("p (c f) -> p c f", c=4),
                                        W["wv_up"].rearrange("(c p) f -> p c f", p=128)),
                  stream="watt", pool=1)
            P.op("DVE", (tag, "vone"), qstF + latF, lambda e: e.memset(vab[:], 1.0))
            WKDEP = [(tag, "wkb"), (tag, "wvb")]
            for swp in range(4):
                lb = swp % 2
                for c in range(4):
                    deps = [(tag, "latst", 4 * c + swp)]
                    if swp >= 2:
                        deps.append((tag, "vmm", (8 * (swp - 2) + 7) * 2 + 1))
                    P.dma("SP", (tag, "latld", swp, c), deps,
                          lambda e, lb=lb, c=c, swp=swp: e.dma_start(
                              latw[:, 4096 * lb + 1024 * c:4096 * lb + 1024 * c + 1024],
                              latT_d[c][:, 1024 * swp:1024 * swp + 1024]),
                          stream="latld", idx=4 * swp + c, pool=8)
                for ft in range(8):
                    k_i = swp * 8 + ft
                    pb = k_i % 2
                    for wi in range(2):
                        for c in range(4):
                            deps = []
                            if c == 0 and wi == 0:
                                deps = WKDEP + [(tag, "latld", swp, cc) for cc in range(4)]
                                if k_i >= 2:
                                    deps.append((tag, "krope", k_i - 2))
                            P.op("PE", (tag, "kmm", k_i) if (c == 3 and wi == 1) else None, deps,
                                 lambda e, pb=pb, c=c, ft=ft, lb=lb, wi=wi: e.matmul(
                                     psk[pb][:, 512 * wi:512 * wi + 512],
                                     wkb[:, 1024 * c + 128 * ft:1024 * c + 128 * ft + 128],
                                     latw[:, 4096 * lb + 1024 * c + 512 * wi:
                                             4096 * lb + 1024 * c + 512 * wi + 512],
                                     start=(c == 0), stop=(c == 3)))
                    KO = kTs[:, S * ft + 1024 * swp:S * ft + 1024 * swp + 1024]
                    _emit_rope2(P, tag, (tag, "krope", k_i), k_i, (tag, "kmm", k_i),
                                psk[pb], KO, cosF, sinF, rtmp, rps2b, 2 * swp)
                for stl in range(8):
                    st = 8 * swp + stl
                    for mwin in range(2):
                        i = st * 2 + mwin
                        for c in range(4):
                            deps = []
                            if c == 0:
                                deps = WKDEP + [(tag, "vone")] +                                        [(tag, "latld", swp, cc) for cc in range(4)]
                                if i >= 2:
                                    deps.append((tag, "vev", i - 2))
                            P.op("PE", (tag, "vmm", i) if c == 3 else None, deps,
                                 lambda e, c=c, stl=stl, mwin=mwin, i=i, lb=lb: e.matmul(
                                     psv[i % 2][:],
                                     latw[:, 4096 * lb + 1024 * c + 128 * stl:
                                             4096 * lb + 1024 * c + 128 * stl + 128],
                                     wvb[:, 1024 * c + 512 * mwin:1024 * c + 512 * mwin + 512],
                                     start=(c == 0), stop=(c == 3)))
                        P.op("ACT", (tag, "vev", i), [(tag, "vmm", i)],
                             lambda e, st=st, mwin=mwin, i=i: e.copy(
                                 vab[:, 1040 * st:1040 * st + 1040].rearrange(
                                     "p (h d) -> p h d", d=65)[:, 8 * mwin:8 * mwin + 8, 0:64],
                                 psv[i % 2][:].rearrange("p (h d) -> p h d", d=64)))
            P.run()
        rope_scope.close()

        with contextlib.ExitStack() as p4:
            sb4 = lambda n, s, d: p4.enter_context(nc.sbuf_tensor(f"{tag}4_{n}", s, d))
            ps4 = lambda n, s: p4.enter_context(nc.psum_tensor(f"{tag}4_{n}", s, F32))
            wob = sb4("wob", [128, 8 * 1024], BF16)
            qtb = sb4("qtb", [128, 2 * 512], BF16)
            expb = sb4("expb", [128, 4 * 1024], BF16)
            outT = sb4("outT", [128, 8 * 512], BF16)
            rcb = sb4("rcb", [1, 1024], F32)
            avsb = sb4("avsb", [65, 2 * 1024], F32)
            bcsb = sb4("bcsb", [128, 1024], F32)
            resb = sb4("resb", [128, 2 * M], F32)
            outb = sb4("outb", [128, 2 * M], F32)
            scAB = [ps4(f"sc{i}", [128, 1024]) for i in range(2)]
            avAB = ps4("avAB", [128, 1024])
            bcps = ps4("bcps", [128, 1024])
            P = Plan(ctx, tag + "A3")
            vevF = [(tag, "vev", 2 * NT - 1)]
            P.dma("POOL", (tag, "wob"), vevF,
                  lambda e: e.dma_start(wob[:].rearrange("p (c f) -> p c f", c=8),
                                        W["wo"].rearrange("(c p) f -> p c f", p=128)),
                  stream="watt", pool=1)

            for qw in range(8):
                nkt = 4 * (qw + 1)
                for hp in range(8):
                    j = qw * 8 + hp
                    qb = j % 2
                    hA, hB = 2 * hp, 2 * hp + 1
                    QT = qtb[:, 512 * qb:512 * qb + 512]
                    deps = list(vevF) + [(tag, "qst", 4 * hp + k) for k in range(4)]
                    if j >= 2:
                        deps.append((tag, "avg", j - 2, 4 * ((j - 2) // 8 + 1) - 1))
                    P.dma("SP", (tag, "qld", j), deps,
                          lambda e, QT=QT, hp=hp, qw=qw: e.dma_start(
                              QT, qT_d[hp][:, 512 * qw:512 * qw + 512]),
                          stream="qld", idx=j, pool=3)

                    def emit_sc(kt, j=j, hp=hp, qw=qw, nkt=nkt, QT=QT):
                        SC = scAB[kt % 2]
                        deps = []
                        if kt <= 1:
                            deps = [(tag, "qld", j), (tag, "wob")]
                            if j >= 1:
                                pnkt = 4 * ((j - 1) // 8 + 1)
                                deps.append((tag, "exp", j - 1, pnkt - 2 + kt))
                            else:
                                deps.append((tag, "krope", 31))
                            if kt == 0 and hp == 0 and qw > 0:
                                deps += [(tag, "outev", 4 * qw - 1, 0),
                                         (tag, "outev", 4 * qw - 1, 1),
                                         (tag, "outev", 4 * qw - 2, 0),
                                         (tag, "outev", 4 * qw - 2, 1)]
                        else:
                            deps = [(tag, "exp", j, kt - 2)]
                        P.op("PE", None, deps,
                             lambda e, SC=SC, kt=kt: e.matmul(
                                 SC[:, 0:512],
                                 kTs[0:64, S * hp + 128 * kt:S * hp + 128 * kt + 128],
                                 QT[0:64, :], start=True, stop=True, tile_position=(0, 0)))
                        P.op("PE", (tag, "scmm", j, kt), (),
                             lambda e, SC=SC, kt=kt: e.matmul(
                                 SC[:, 512:1024],
                                 kTs[64:128, S * hp + 128 * kt:S * hp + 128 * kt + 128],
                                 QT[64:128, :], start=True, stop=True, tile_position=(64, 0)))
                        EX = expb[:, 1024 * (kt % 4):1024 * (kt % 4) + 1024]
                        deps = [(tag, "scmm", j, kt)]
                        if kt >= 4:
                            deps.append((tag, "avg", j, kt - 4))
                        elif j >= 1:
                            pnkt = 4 * ((j - 1) // 8 + 1)
                            deps.append((tag, "avg", j - 1, pnkt - 4 + kt))
                        P.op("ACT", (tag, "exp", j, kt), deps,
                             lambda e, EX=EX, SC=SC: e.activation(EX, SC[:], AF.Exp,
                                                                  bias=0.0, scale=0.125))
                        if kt >= nkt - 2:
                            for off in (0, 512):
                                key = (tag, "msk", j, kt) if off == 512 else None
                                P.op("DVE", key, [(tag, "exp", j, kt)] if off == 0 else (),
                                     lambda e, EX=EX, off=off: e.memset(EX[:, off:off + 256], 0.0))

                    def emit_av(kt, j=j, hp=hp, nkt=nkt, hA=hA, hB=hB):
                        EX = expb[:, 1024 * (kt % 4):1024 * (kt % 4) + 1024]
                        deps = [(tag, "msk", j, kt) if kt >= nkt - 2 else (tag, "exp", j, kt)]
                        if kt == 0 and j >= 1:
                            deps.append((tag, "avev", j - 1))
                        P.op("PE", None, deps,
                             lambda e, EX=EX, kt=kt: e.matmul(
                                 avAB[0:65, 0:512],
                                 vab[:, 1040 * kt + 65 * hA:1040 * kt + 65 * hA + 65],
                                 EX[:, 0:512], start=(kt == 0), stop=(kt == nkt - 1)))
                        P.op("PE", (tag, "avg", j, kt), (),
                             lambda e, EX=EX, kt=kt: e.matmul(
                                 avAB[0:65, 512:1024],
                                 vab[:, 1040 * kt + 65 * hB:1040 * kt + 65 * hB + 65],
                                 EX[:, 512:1024], start=(kt == 0), stop=(kt == nkt - 1)))

                    def emit_epilogue(jj):
                        AVS = avsb[:, 1024 * (jj % 2):1024 * (jj % 2) + 1024]
                        for half in range(2):
                            deps = [(tag, "rc", jj)]
                            if jj >= 1 and half == 0:
                                deps.append((tag, "bcev", jj - 1))
                            P.op("PE", (tag, "bcmm", jj, half), deps,
                                 lambda e, half=half: e.matmul(
                                     bcps[:, 512 * half:512 * half + 512], cs["ones1f"][:],
                                     rcb[0:1, 512 * half:512 * half + 512],
                                     start=True, stop=True))
                        deps = [(tag, "bcmm", jj, 1)]
                        if jj >= 1:
                            deps.append((tag, "outTev", jj - 1, 1))
                        P.op("DVE", (tag, "bcev", jj), deps,
                             lambda e: e.tensor_copy(bcsb[:], bcps[:]))
                        OT = outT[:, 512 * (jj % 8):512 * (jj % 8) + 512]
                        deps = [(tag, "bcev", jj)]
                        if jj >= 8:
                            deps.append((tag, "fmm", 4 * (jj // 8 - 1) + 3, 1))
                        P.op("DVE", (tag, "outTev", jj, 0), deps,
                             lambda e, OT=OT, AVS=AVS: e.tensor_tensor(
                                 OT[0:64, :], AVS[0:64, 0:512], bcsb[0:64, 0:512], ALU.mult))
                        P.op("DVE", (tag, "outTev", jj, 1), (),
                             lambda e, OT=OT, AVS=AVS: e.tensor_tensor(
                                 OT[64:128, :], AVS[0:64, 512:1024], bcsb[0:64, 512:1024],
                                 ALU.mult))

                    POS = max(1, min(nkt - 3, 7))
                    emit_sc(0)
                    for kt in range(1, nkt):
                        emit_sc(kt)
                        emit_av(kt - 1)
                        if kt == POS and hp >= 1:
                            emit_epilogue(j - 1)
                    emit_av(nkt - 1)
                    AVS = avsb[:, 1024 * (j % 2):1024 * (j % 2) + 1024]
                    deps = [(tag, "avg", j, nkt - 1)]
                    if j >= 2:
                        deps.append((tag, "outTev", j - 2, 1))
                    P.op("DVE", (tag, "avev", j), deps,
                         lambda e, AVS=AVS: e.tensor_copy(AVS, avAB[0:65, :]))
                    deps = [(tag, "avev", j)]
                    if j >= 1:
                        deps.append((tag, "bcmm", j - 1, 1))
                    P.op("DVE", (tag, "rc", j), deps,
                         lambda e, AVS=AVS: e.reciprocal(rcb[0:1, :], AVS[64:65, :]))
                    if hp == 7:
                        emit_epilogue(j)
                for ssub in range(4):
                    t = 4 * qw + ssub
                    b = t % 2
                    RS = resb[:, M * b:M * b + M]
                    OB = outb[:, M * b:M * b + M]
                    deps = [(tag, "outTev", qw * 8 + 7, 1)] if ssub == 0 else []
                    if t >= 2:
                        deps.append((tag, "outst", t - 2))
                    P.dma("SP", (tag, "res", t), deps + list(vevF),
                          lambda e, RS=RS, t=t: e.dma_start(RS, x_src[128 * t:128 * t + 128, :]),
                          stream="res", idx=t, pool=3)
                    for mw in range(2):
                        fps = scAB[ssub % 2][:, 512 * mw:512 * mw + 512]
                        for hp in range(8):
                            deps = []
                            if ssub == 0 and mw == 0:
                                deps = [(tag, "outTev", qw * 8 + hp, 1)]
                            if hp == 0:
                                deps = deps + [(tag, "exp", qw * 8 + 7, nkt - 1)]
                                if t >= 2:
                                    deps.append((tag, "outev", t - 2, mw))
                            P.op("PE", (tag, "fmm", t, mw) if hp == 7 else None, deps,
                                 lambda e, fps=fps, hp=hp, ssub=ssub, mw=mw: e.matmul(
                                     fps, outT[:, 512 * hp + 128 * ssub:512 * hp + 128 * ssub + 128],
                                     wob[:, 1024 * hp + 512 * mw:1024 * hp + 512 * mw + 512],
                                     start=(hp == 0), stop=(hp == 7)))
                        P.op("DVE", (tag, "outev", t, mw), [(tag, "fmm", t, mw), (tag, "res", t)],
                             lambda e, OB=OB, RS=RS, mw=mw, fps=fps: e.tensor_tensor(
                                 OB[:, 512 * mw:512 * mw + 512], fps,
                                 RS[:, 512 * mw:512 * mw + 512], ALU.add))
                    P.dma("SP", (tag, "outst", t), [(tag, "outev", t, 0), (tag, "outev", t, 1)],
                          lambda e, OB=OB, t=t: e.dma_start(x_dst[128 * t:128 * t + 128, :], OB),
                          stream="outst", idx=t, pool=4)
            P.run()
    emit_fence(ctx)


from concourse.bass_utils import run_bass_kernel_spmd

_BUILD_CACHE = {}


def _build_program():
    if "nc" in _BUILD_CACHE:
        return _BUILD_CACHE["nc"], _BUILD_CACHE["consts"], _BUILD_CACHE["rope"]
    nc = bass.Bass("TRN2", target_bir_lowering=False)
    ctx = Ctx(nc)
    x_d = nc.dram_tensor("x", [S, M], F32, kind="ExternalInput")
    out_d = nc.dram_tensor("out", [S, M], F32, kind="ExternalOutput")
    x1_d = nc.dram_tensor("x1s", [S, M], F32, kind="Internal")
    x2_d = nc.dram_tensor("x2s", [S, M], F32, kind="Internal")
    rw_d = nc.dram_tensor("rw", [M, E], F32, kind="ExternalInput")
    w0_d = nc.dram_tensor("w0", [E, M, H], F32, kind="ExternalInput")
    w1_d = nc.dram_tensor("w1", [E, M, H], F32, kind="ExternalInput")
    wof_d = nc.dram_tensor("wof", [E, H, M], F32, kind="ExternalInput")
    wq_d = nc.dram_tensor("wqp", [M, M], F32, kind="ExternalInput")
    wd_d = nc.dram_tensor("wd", [M, 512], F32, kind="ExternalInput")
    wk_d = nc.dram_tensor("wkp", [512, M], F32, kind="ExternalInput")
    wv_d = nc.dram_tensor("wv", [512, M], F32, kind="ExternalInput")
    woa_d = nc.dram_tensor("woa", [M, M], F32, kind="ExternalInput")

    consts = declare_consts(ctx)
    emit_const_loads(ctx)
    WF = dict(router_w=rw_d.ap(), w0=w0_d.ap(), w1=w1_d.ap(), wo=wof_d.ap())
    WA = dict(wq=wq_d.ap(), w_down=wd_d.ap(), wk_up=wk_d.ap(), wv_up=wv_d.ap(),
              wo=woa_d.ap())
    emit_ffn(ctx, "A", x_d.ap(), x1_d.ap(), WF)
    emit_attention(ctx, "T", x1_d.ap(), x2_d.ap(), WA)
    rope = dict(ctx.rope_inputs)
    emit_ffn(ctx, "B", x2_d.ap(), out_d.ap(), WF)
    emit_finish(ctx)
    _BUILD_CACHE.update(nc=nc, consts=consts, rope=rope)
    return nc, consts, rope


def kernel(L0_attn_x, L0_attn_wq, L0_attn_w_down, L0_attn_wk_up, L0_attn_wv_up,
           L0_attn_wo, L0_ffn_router_w, L0_ffn_w0_kernel, L0_ffn_w1_kernel,
           L0_ffn_wo_kernel):
    x = np.ascontiguousarray(np.asarray(L0_attn_x, dtype=np.float32))
    nc, consts, rope = _build_program()
    perm = attn_perm()
    wqp = np.ascontiguousarray(np.asarray(L0_attn_wq, np.float32)[:, perm])
    wkp = np.ascontiguousarray(np.asarray(L0_attn_wk_up, np.float32)[:, perm])
    base = dict(
        rw=np.ascontiguousarray(np.asarray(L0_ffn_router_w, np.float32)),
        w0=np.ascontiguousarray(np.asarray(L0_ffn_w0_kernel, np.float32)),
        w1=np.ascontiguousarray(np.asarray(L0_ffn_w1_kernel, np.float32)),
        wof=np.ascontiguousarray(np.asarray(L0_ffn_wo_kernel, np.float32)),
        wqp=wqp,
        wd=np.ascontiguousarray(np.asarray(L0_attn_w_down, np.float32)),
        wkp=wkp,
        wv=np.ascontiguousarray(np.asarray(L0_attn_wv_up, np.float32)),
        woa=np.ascontiguousarray(np.asarray(L0_attn_wo, np.float32)),
    )
    for k, v in consts.items():
        base["c_" + k] = v
    base.update(rope)
    in_maps = []
    for b in range(8):
        m = dict(base)
        m["x"] = np.ascontiguousarray(x[b])
        in_maps.append(m)
    res = run_bass_kernel_spmd(nc, in_maps, core_ids=list(range(8)))
    return np.stack([res.results[b]["out"] for b in range(8)], axis=0)


# version 07 (final)
# speedup vs baseline: 1.2976x; 1.1464x over previous
import numpy as np
import ml_dtypes
import contextlib
import concourse.bass as bass
import concourse.mybir as mybir
from concourse.library_config import mlp as MLP_LIB

F32 = mybir.dt.float32
F16 = mybir.dt.float16
BF16 = mybir.dt.bfloat16
AF = mybir.ActivationFunctionType
ALU = mybir.AluOpType
AX = mybir.AxisListType.X
BF = ml_dtypes.bfloat16

S = 4096
M = 1024
E = 8
CAP = 1024
H = 5464
NT = S // 128
NHT = (H + 127) // 128
HSPLIT = 20
NHEAD = 16
DH = 64
GELU_SA = float(np.sqrt(np.float64(0.044715)))
GELU_K = float(2.0 * np.sqrt(2.0 / np.pi))

DMA_OF = {"SP": "DSP", "POOL": "DPL", "ACT": "DACT"}
SER_ENGS = ("DVE", "ACT", "POOL")


class Rec:
    __slots__ = ("eng", "fn", "deps", "key", "inc", "sem_kind", "idx")
    def __init__(self, eng, fn, deps, key, inc, sem_kind, idx):
        self.eng = eng
        self.fn = fn
        self.deps = deps
        self.key = key
        self.inc = inc
        self.sem_kind = sem_kind
        self.idx = idx


class Ctx:
    def __init__(self, nc):
        self.nc = nc
        self.es = contextlib.ExitStack()
        self.T = {}
        self.n = {k: 0 for k in ("PE", "DVE", "ACT", "POOL", "SP", "DSP", "DPL", "DACT")}
        self.sem = {}
        for name in list(self.n):
            self.sem[name] = self.es.enter_context(nc.semaphore("s" + name))

    def dstream(self, name, idx=0, pool=1):
        key = f"D_{name}{idx % pool}"
        if key not in self.sem:
            self.sem[key] = self.es.enter_context(self.nc.semaphore(key))
            self.n[key] = 0
        return key

    def sb(self, name, shape, dt):
        return self.es.enter_context(self.nc.sbuf_tensor(name, shape, dt))


class Plan:
    def __init__(self, ctx, name=""):
        self.ctx = ctx
        self.name = name
        self.recs = []
        self._last = {}

    def op(self, eng, key, deps, fn, ser=True):
        deps = list(deps)
        if ser and eng in SER_ENGS and eng in self._last:
            deps.append(("_idx_", self._last[eng]))
        r = Rec(eng, fn, tuple(deps), key, 1, eng, len(self.recs))
        self.recs.append(r)
        self._last[eng] = r.idx
        return r

    def dma(self, eng, key, deps, fn, stream=None, idx=0, pool=4):
        assert stream is not None
        kind = self.ctx.dstream(stream, idx, pool)
        r = Rec(eng, fn, tuple(deps), key, 16, kind, len(self.recs))
        self.recs.append(r)
        return r

    def run(self):
        ctx = self.ctx
        nc = ctx.nc
        cnt = dict(ctx.n)
        toks = {}
        idx_toks = {}
        keyidx = {}
        for r in self.recs:
            cnt[r.sem_kind] += r.inc
            idx_toks[r.idx] = (ctx.sem[r.sem_kind], cnt[r.sem_kind])
            if r.key is not None:
                toks[r.key] = idx_toks[r.idx]
                keyidx[r.key] = r.idx
        for r in self.recs:
            for k in r.deps:
                if isinstance(k, tuple) and len(k) == 2 and k[0] == "_idx_":
                    continue
                if k in toks:
                    if keyidx[k] >= r.idx and self.recs[keyidx[k]].eng == r.eng:
                        raise RuntimeError(
                            f"plan {self.name}: same-engine future dep {k} at op {r.idx}")
                elif k not in ctx.T:
                    raise KeyError(f"plan {self.name}: unresolved dep {k!r}")

        def resolve(k):
            if isinstance(k, tuple) and len(k) == 2 and k[0] == "_idx_":
                return idx_toks[k[1]]
            return toks[k] if k in toks else ctx.T[k]

        per_eng = {}
        for r in self.recs:
            per_eng.setdefault(r.eng, []).append(r)

        def emit(eng_handle, eng_name):
            for r in per_eng.get(eng_name, ()):
                best = {}
                for k in r.deps:
                    sem, val = resolve(k)
                    if sem.num not in best or best[sem.num][1] < val:
                        best[sem.num] = (sem, val)
                for sem, val in best.values():
                    eng_handle.wait_ge(sem, val)
                inst = r.fn(eng_handle)
                inst.then_inc(ctx.sem[r.sem_kind], r.inc)

        with nc.Block() as blk:
            if "SP" in per_eng:
                @blk.sync
                def _(e):
                    emit(e, "SP")
            if "POOL" in per_eng:
                @blk.gpsimd
                def _(e):
                    emit(e, "POOL")
            if "PE" in per_eng:
                @blk.tensor
                def _(e):
                    emit(e, "PE")
            if "DVE" in per_eng:
                @blk.vector
                def _(e):
                    emit(e, "DVE")
            if "ACT" in per_eng:
                @blk.scalar
                def _(e):
                    emit(e, "ACT")
        ctx.n.update(cnt)
        ctx.T.update(toks)


def make_consts():
    iota8f = np.broadcast_to(np.arange(8, dtype=np.float32), (128, 8)).copy()
    return {
        "iota8f": iota8f,
        "iota8h": iota8f.astype(np.float16),
        "iotam999": iota8f - 999.0,
        "iota1024h": np.broadcast_to(np.arange(1024, dtype=np.float16), (128, 1024)).copy(),
        "triu": (np.arange(128)[:, None] < np.arange(128)[None, :]).astype(BF),
        "onescol": np.ones((128, 1), BF),
        "onescolf": np.ones((128, 1), np.float32),
        "ones1f": np.ones((1, 128), np.float32),
        "ones128f": np.ones((128, 128), np.float32),
        "ident8": np.eye(8, dtype=np.float32),
        "ident128f": np.eye(128, dtype=np.float32),
        "iotapc": (np.arange(128, dtype=np.float32)[:, None]
                   + 128.0 * np.arange(64, dtype=np.float32)[None, :]),
        "eps5": np.full((128, 1), 1e-5, np.float32),
        "iotok": (np.arange(128, dtype=np.int16)[:, None]
                  + 128 * np.arange(NT, dtype=np.int16)[None, :]),
        "sigP": _sig_perm_mat(),
    }


def _sig_perm_mat():
    m = np.zeros((128, 128), BF)
    for i in range(128):
        m[i, 8 * (i % 16) + i // 16] = 1
    return m


def make_rope_consts():
    inv = (1.0 / (np.float32(10000.0) **
                  (np.arange(0, DH, 2, dtype=np.float32) / np.float32(DH)))).astype(np.float32)
    ang = np.outer(np.arange(S, dtype=np.float32), inv).astype(np.float32)
    cos = np.cos(ang).astype(np.float32).T
    sin = np.sin(ang).astype(np.float32).T
    return {"ropeA": np.concatenate([cos, cos, cos, cos], axis=0),
            "ropeB": np.concatenate([sin, sin, sin, sin], axis=0)}

CONST_DTYPES = {
    "iota8f": F32, "iota8h": F16, "iotam999": F32, "iota1024h": F16,
    "triu": BF16, "onescol": BF16, "onescolf": F32, "ones1f": F32, "ones128f": F32,
    "ident8": F32, "ident128f": F32, "iotapc": F32, "eps5": F32,
    "iotok": mybir.dt.int16, "sigP": BF16,
}
ROPE_DTYPES = {"ropeA": F32, "ropeB": F32}


def feat_perm():
    perm = np.zeros(M, dtype=np.int64)
    for ft in range(8):
        hA, hB = 2 * ft, 2 * ft + 1
        base = 128 * ft
        for j in range(32):
            perm[base + j] = hA * 64 + 2 * j
            perm[base + 32 + j] = hB * 64 + 2 * j
            perm[base + 64 + j] = hA * 64 + 2 * j + 1
            perm[base + 96 + j] = hB * 64 + 2 * j + 1
    return perm


def declare_consts(ctx):
    nc = ctx.nc
    ctx.cd = {}
    ctx.cs = {}
    consts = make_consts()
    for k, v in consts.items():
        ctx.cd[k] = nc.dram_tensor("c_" + k, list(v.shape), CONST_DTYPES[k], kind="ExternalInput")
        ctx.cs[k] = ctx.sb("s_" + k, list(v.shape), CONST_DTYPES[k])
    return consts


def emit_fence(ctx, key="fence"):
    nc = ctx.nc
    with nc.Block() as blk:
        @blk.sync
        def _(e):
            for k, sem in ctx.sem.items():
                if k.startswith("D") and ctx.n.get(k, 0) > 0:
                    e.wait_ge(sem, ctx.n[k])
            e.nop().then_inc(ctx.sem["SP"], 1)
    ctx.n["SP"] += 1
    ctx.T[key] = (ctx.sem["SP"], ctx.n["SP"])


def emit_finish(ctx):
    nc = ctx.nc
    with nc.Block() as blk:
        @blk.sync
        def _(e):
            for k, sem in ctx.sem.items():
                if k.startswith("D") and ctx.n.get(k, 0) > 0:
                    e.wait_ge(sem, ctx.n[k])


def emit_pool_init(ctx):
    nc = ctx.nc
    with nc.Block() as blk:
        @blk.gpsimd
        def _(g):
            g.load_library(MLP_LIB)
            ctx.r512 = g.alloc_register("gn512")
            g.reg_mov(ctx.r512, 512)
            ctx.r128 = g.alloc_register("gn128")
            g.reg_mov(ctx.r128, 128)


def emit_const_loads(ctx):
    P = Plan(ctx, "consts")
    for k in ctx.cd:
        dst, src = ctx.cs[k], ctx.cd[k]
        P.dma("SP", "consts", (), lambda e, d=dst, s=src: e.dma_start(d[:], s[:]), stream="misc", pool=1)
    P.run()
    ctx.T["fence"] = ctx.T["consts"]


def emit_ffn(ctx, tag, x_src, x_dst, W):
    nc = ctx.nc
    I16 = mybir.dt.int16
    I32 = mybir.dt.int32
    xhi_d = nc.dram_tensor(f"{tag}_xhi", [S, M], BF16, kind="Internal")
    xlo_d = nc.dram_tensor(f"{tag}_xlo", [S, M], BF16, kind="Internal")
    imap_d = nc.dram_tensor(f"{tag}_imap", [16 * 520], I16, kind="Internal")
    gate_d = nc.dram_tensor(f"{tag}_gate", [8320], F32, kind="Internal")
    cmbi_d = nc.dram_tensor(f"{tag}_cmbi", [2, NT, 128], I16, kind="Internal")
    eo_d = nc.dram_tensor(f"{tag}_eo", [8256, M], BF16, kind="Internal")

    with contextlib.ExitStack() as ph:
        sbp = lambda n, s, d: ph.enter_context(nc.sbuf_tensor(f"{tag}_{n}", s, d))
        colbuf = sbp("colbuf", [128, 32 * NT], F32)
        m1kb = sbp("m1kb", [128, 8 * NT], F32)
        m2kb = sbp("m2kb", [128, 8 * NT], F32)

        with contextlib.ExitStack() as p1:
            sb1 = lambda n, s, d: p1.enter_context(nc.sbuf_tensor(f"{tag}1_{n}", s, d))
            ps1 = lambda n, s: p1.enter_context(nc.psum_tensor(f"{tag}1_{n}", s, F32))
            B = dict(
                m1b=sb1("m1b", [128, 8 * NT], BF16),
                m2b=sb1("m2b", [128, 8 * NT], BF16),
                lg=sb1("lg", [128, 8 * NT], F32),
                rawb=sb1("rawb", [128, 8 * NT], F32),
                p1m=sb1("p1m", [128, 8 * NT], F32),
                p2m=sb1("p2m", [128, 8 * NT], F32),
                run1=sb1("run1", [1, 8], F32),
                run2=sb1("run2", [1, 8], F32),
                scr=sb1("scr", [128, 8], F32),
                scr2=sb1("scr2", [128, 8], F32),
                sl32=sb1("sl32", [128, 2 * NT], I32),
                pk32=sb1("pk32", [128, 2 * NT], I32),
                d32=sb1("d32", [128, 2], I32),
                sl16=sb1("sl16", [128, 2 * NT], I16),
                gcol=sb1("gcol", [128, 2 * NT], F32),
                zi16=sb1("zi16", [128, 66], I16),
                zf32=sb1("zf32", [128, 65], F32),
            )
            with contextlib.ExitStack() as p1g:
                ps1g = lambda n, s: p1g.enter_context(nc.psum_tensor(f"{tag}1_{n}", s, F32))
                B["pp"] = ps1g("pp", [128, 8])
                B["pcs"] = ps1g("pcs", [1, 8])
                B["ptp"] = ps1g("ptp", [128, 8])
                _emit_f1(ctx, tag, x_src, xhi_d, xlo_d, W, colbuf, B, sb1, ps1g)
                _emit_gating(ctx, tag, colbuf, m1kb, m2kb, B)
            _emit_scatters(ctx, tag, imap_d, gate_d, cmbi_d, B)

        _emit_experts(ctx, tag, W, xhi_d, imap_d, gate_d, eo_d)
        _emit_combine(ctx, tag, eo_d, cmbi_d, x_src, x_dst)
    emit_fence(ctx)


def _emit_f1(ctx, tag, x_src, xhi_d, xlo_d, W, colbuf, B, sb1, ps1):
    cs = ctx.cs
    xt = sb1("xt", [128, 2 * M], F32)
    st6 = sb1("st6", [128, 2 * 16], F32)
    mv = sb1("mv", [128, 4 * NT], F32)
    xln = sb1("xln", [128, 2 * M], F32)
    xhi = sb1("xhi", [128, 2 * M], BF16)
    xlo = sb1("xlo", [128, 2 * M], BF16)
    xhiT = sb1("xhiT", [128, 2 * 512], BF16)
    xloT = sb1("xloT", [128, 2 * 512], BF16)
    lgT = sb1("lgT", [8, 512], F32)
    rwh = sb1("rwh", [128, 64], BF16)
    rwl = sb1("rwl", [128, 64], BF16)
    rwf = sb1("rwf", [128, 64], F32)
    pl8 = ps1("pl8", [8, 512])
    lg = B["lg"]
    ptp = B["ptp"]

    P = Plan(ctx, tag + "F1a")
    P.dma("POOL", (tag, "rwh"), ["consts", "fence"],
          lambda e: e.dma_start(rwh[:].rearrange("p (c e) -> p c e", c=8),
                                W["router_w"].rearrange("(c p) e -> p c e", p=128)),
          stream="miscP", pool=1)
    P.dma("SP", (tag, "rwf"), ["consts", "fence"],
          lambda e: e.dma_start(rwf[:].rearrange("p (c e) -> p c e", c=8),
                                W["router_w"].rearrange("(c p) e -> p c e", p=128)), stream="misc", pool=1)
    P.op("DVE", (tag, "cbz"), ["fence"], lambda e: e.memset(colbuf[:], 0.0))
    P.op("DVE", (tag, "rwl"), [(tag, "rwh"), (tag, "rwf")],
         lambda e: e.tensor_tensor(rwl[:], rwf[:], rwh[:], ALU.subtract))
    for t in range(NT):
        b = t % 2
        X = xt[:, M * b:M * b + M]
        S6 = st6[:, 16 * b:16 * b + 12]
        MV2 = st6[:, 16 * b + 12:16 * b + 14]
        VV = st6[:, 16 * b + 13:16 * b + 14]
        XL = xln[:, M * b:M * b + M]
        XH = xhi[:, M * b:M * b + M]
        XO = xlo[:, M * b:M * b + M]
        deps = ["fence"] + ([(tag, "xlnf", t - 2)] if t >= 2 else [])
        P.dma("SP", (tag, "xt", t), deps,
              lambda e, X=X, t=t: e.dma_start(X, x_src[128 * t:128 * t + 128, :]), stream="xt", idx=t, pool=3)
        deps = [(tag, "xt", t)] + ([(tag, "rstd", t - 2)] if t >= 2 else [])
        P.op("DVE", (tag, "bns", t), deps,
             lambda e, S6=S6, X=X: e.bn_stats(S6[:, 0:6], X[:, 0:512]))
        P.op("DVE", None, (),
             lambda e, S6=S6, X=X: e.bn_stats(S6[:, 6:12], X[:, 512:1024]))
        P.op("DVE", (tag, "bn", t), (), lambda e, MV2=MV2, S6=S6: e.bn_aggr(MV2, S6))
        P.op("DVE", (tag, "mn", t), (),
             lambda e, t=t, MV2=MV2: e.tensor_copy(mv[:, 4 * t:4 * t + 1], MV2[:, 0:1]))
        P.op("ACT", None, [(tag, "bn", t)],
             lambda e, VV=VV: e.activation(VV, VV, AF.Ln, bias=cs['eps5'][:, 0:1], scale=1.0))
        P.op("ACT", (tag, "rstd", t), (),
             lambda e, t=t, VV=VV: e.activation(mv[:, 4 * t + 1:4 * t + 2], VV, AF.Exp,
                                                bias=0.0, scale=-0.5))
        deps = [(tag, "rstd", t)]
        if t >= 2:
            deps += [(tag, "sthi", t - 2), (tag, "stlo", t - 2)]
        P.op("DVE", (tag, "xlnf", t), deps,
             lambda e, XL=XL, X=X, t=t: e.tensor_scalar(
                 XL, X, mv[:, 4 * t:4 * t + 1], mv[:, 4 * t + 1:4 * t + 2],
                 ALU.subtract, ALU.mult))
        P.op("DVE", (tag, "xhi", t), (), lambda e, XH=XH, XL=XL: e.tensor_copy(XH, XL))
        P.op("DVE", (tag, "xlo", t), (),
             lambda e, XO=XO, XL=XL, XH=XH: e.tensor_tensor(XO, XL, XH, ALU.subtract))
        P.dma("SP", (tag, "sthi", t), [(tag, "xhi", t)],
              lambda e, XH=XH, t=t: e.dma_start(xhi_d[128 * t:128 * t + 128, :], XH), stream="sthi", idx=t, pool=3)
        P.dma("SP", (tag, "stlo", t), [(tag, "xlo", t)],
              lambda e, XO=XO, t=t: e.dma_start(xlo_d[128 * t:128 * t + 128, :], XO), stream="stlo", idx=t, pool=3)
    P.run()

    P = Plan(ctx, tag + "F1b")
    for w in range(8):
        for c in range(8):
            i = w * 8 + c
            b = i % 2
            hiT = xhiT[:, 512 * b:512 * b + 512]
            loT = xloT[:, 512 * b:512 * b + 512]
            deps = [(tag, "sthi", NT - k) for k in (1, 2, 3, 4)] + \
                   [(tag, "stlo", NT - k) for k in (1, 2, 3, 4)]
            if i >= 2:
                deps.append((tag, "lgmm", i - 2))
            P.dma("SP", (tag, "ldTh", i), deps,
                  lambda e, hiT=hiT, w=w, c=c: e.dma_start(
                      hiT, xhi_d[512 * w:512 * w + 512, 128 * c:128 * c + 128], transpose=True), stream="ldT", idx=i, pool=3)
            P.dma("SP", (tag, "ldT", i), (),
                  lambda e, loT=loT, w=w, c=c: e.dma_start(
                      loT, xlo_d[512 * w:512 * w + 512, 128 * c:128 * c + 128], transpose=True), stream="ldT", idx=i, pool=3)
            deps = [(tag, "ldT", i), (tag, "ldTh", i), (tag, "rwl")]
            if c == 0 and w > 0:
                deps.append((tag, "lgTc", w - 1))
            P.op("PE", None, deps,
                 lambda e, c=c, hiT=hiT: e.matmul(pl8[:], rwh[:, 8 * c:8 * c + 8], hiT,
                                                  start=(c == 0), stop=False))
            P.op("PE", None, (),
                 lambda e, c=c, loT=loT: e.matmul(pl8[:], rwh[:, 8 * c:8 * c + 8], loT,
                                                  start=False, stop=False))
            P.op("PE", (tag, "lgmm", i), (),
                 lambda e, c=c, hiT=hiT: e.matmul(pl8[:], rwl[:, 8 * c:8 * c + 8], hiT,
                                                  start=False, stop=(c == 7)))
        deps = [(tag, "lgmm", w * 8 + 7)]
        if w > 0:
            deps.append((tag, "lgtr", 4 * w - 1))
        P.op("ACT", (tag, "lgTc", w), deps, lambda e: e.copy(lgT[:], pl8[:]))
        for j in range(4):
            t_idx = 4 * w + j
            deps = [(tag, "lgTc", w)]
            if t_idx > 0:
                deps.append((tag, "lgn", t_idx - 1))
            P.op("PE", (tag, "lgtr", t_idx), deps,
                 lambda e, j=j: e.transpose(ptp[:], lgT[:, 128 * j:128 * j + 128], cs["ident8"][:]))
            P.op("ACT", (tag, "lgn", t_idx), [(tag, "lgtr", t_idx)],
                 lambda e, t_idx=t_idx: e.copy(lg[:, 8 * t_idx:8 * t_idx + 8], ptp[:]))
    P.run()


def _emit_gating(ctx, tag, colbuf, m1kb, m2kb, B):
    cs = ctx.cs
    C = float(CAP)
    lg, rawb = B["lg"], B["rawb"]
    m1b, m2b = B["m1b"], B["m2b"]
    p1m, p2m = B["p1m"], B["p2m"]
    run1, run2 = B["run1"], B["run2"]
    scr, scr2 = B["scr"], B["scr2"]
    pp, pcs = B["pp"], B["pcs"]
    sl32, pk32, d32, sl16, gcol = B["sl32"], B["pk32"], B["d32"], B["sl16"], B["gcol"]

    P = Plan(ctx, tag + "G1")
    for t in range(NT):
        L = lg[:, 8 * t:8 * t + 8]
        R = rawb[:, 8 * t:8 * t + 8]
        cb = colbuf[:, 32 * t:32 * t + 32]
        P.op("DVE", (tag, "negmax", t), [(tag, "cbz"), (tag, "lgn", t)],
             lambda e, cb=cb, L=L: e.reduce_max(cb[:, 0:1], L, axis=AX, negate=True))
        P.op("ACT", (tag, "exp", t), [(tag, "negmax", t)],
             lambda e, R=R, L=L, cb=cb: e.activation(R, L, AF.Exp, bias=cb[:, 0:1], scale=1.0,
                                                     accum_out=cb[:, 1:2]))
        P.op("DVE", None, [(tag, "exp", t)],
             lambda e, cb=cb: e.reciprocal(cb[:, 2:3], cb[:, 1:2]))
        P.op("DVE", None, (), lambda e, R=R, cb=cb: e.tensor_scalar(R, R, cb[:, 2:3], None, ALU.mult))
        P.op("DVE", (tag, "g1", t), (), lambda e, cb=cb, R=R: e.reduce_max(cb[:, 3:4], R, axis=AX))
        P.op("DVE", None, (), lambda e, R=R, cb=cb: e.tensor_scalar(scr[:], R, cb[:, 3:4], None, ALU.is_equal))
        P.op("DVE", None, (), lambda e: e.tensor_tensor(scr[:], scr[:], cs["iotam999"][:], ALU.mult))
        P.op("DVE", None, (), lambda e: e.tensor_scalar(scr[:], scr[:], 999.0, None, ALU.add))
        P.op("DVE", None, (), lambda e, cb=cb: e.tensor_reduce(cb[:, 4:5], scr[:], axis=AX, op=ALU.min))
        P.op("DVE", (tag, "m1", t), (),
             lambda e, t=t, cb=cb: e.tensor_scalar(m1b[:, 8 * t:8 * t + 8], cs["iota8h"][:],
                                                   cb[:, 4:5], None, ALU.is_equal))
        P.op("DVE", None, (), lambda e, t=t, R=R: e.tensor_tensor(scr2[:], m1b[:, 8 * t:8 * t + 8], R, ALU.mult))
        P.op("DVE", None, (), lambda e, R=R: e.tensor_tensor(scr2[:], R, scr2[:], ALU.subtract))
        P.op("DVE", (tag, "g2", t), (), lambda e, cb=cb: e.reduce_max(cb[:, 5:6], scr2[:], axis=AX))
        P.op("DVE", None, (), lambda e, cb=cb: e.tensor_scalar(scr[:], scr2[:], cb[:, 5:6], None, ALU.is_equal))
        P.op("DVE", None, (), lambda e: e.tensor_tensor(scr[:], scr[:], cs["iotam999"][:], ALU.mult))
        P.op("DVE", None, (), lambda e: e.tensor_scalar(scr[:], scr[:], 999.0, None, ALU.add))
        P.op("DVE", None, (), lambda e, cb=cb: e.tensor_reduce(cb[:, 6:7], scr[:], axis=AX, op=ALU.min))
        P.op("DVE", (tag, "m2", t), (),
             lambda e, t=t, cb=cb: e.tensor_scalar(m2b[:, 8 * t:8 * t + 8], cs["iota8h"][:],
                                                   cb[:, 6:7], None, ALU.is_equal))
    P.run()

    for mx in (1, 2):
        mk = m1kb if mx == 1 else m2kb
        pm = p1m if mx == 1 else p2m
        runx = run1 if mx == 1 else run2
        mb_ = m1b if mx == 1 else m2b
        P = Plan(ctx, tag + f"G{mx + 1}")
        if mx == 1:
            P.op("DVE", (tag, "run1", 0), (), lambda e: e.memset(run1[:], 0.0))
        else:
            for t in range(NT):
                P.op("PE", (tag, "m1cnt", t), [(tag, "p1ev", NT - 1), (tag, "run1", NT)],
                     lambda e, t=t: e.matmul(pcs[:], cs["onescolf"][:], m1kb[:, 8 * t:8 * t + 8],
                                             start=(t == 0), stop=(t == NT - 1)))
            P.op("DVE", (tag, "run2", 0), [(tag, "m1cnt", NT - 1)],
                 lambda e: e.tensor_copy(run2[:], pcs[0:1, :]))
        for t in range(NT):
            cb = colbuf[:, 32 * t:32 * t + 32]
            deps = [(tag, f"run{mx}", t), (tag, "m2", NT - 1)]
            if t > 0:
                deps.append((tag, f"p{mx}ev", t - 1))
            P.op("PE", None, deps,
                 lambda e, t=t, mb_=mb_: e.matmul(pp[:], cs["triu"][:], mb_[:, 8 * t:8 * t + 8],
                                                  start=True, stop=False))
            P.op("PE", (tag, f"ppmm{mx}", t), (),
                 lambda e, runx=runx: e.matmul(pp[:], cs["ones1f"][:], runx[:], start=False, stop=True))
            P.op("PE", (tag, f"csmm{mx}", t), [(tag, f"run{mx}", t)],
                 lambda e, t=t, mb_=mb_: e.matmul(pcs[:], cs["onescol"][:], mb_[:, 8 * t:8 * t + 8],
                                                  start=True, stop=True))
            P.op("DVE", None, [(tag, f"ppmm{mx}", t)],
                 lambda e, t=t, pm=pm: e.tensor_copy(pm[:, 8 * t:8 * t + 8], pp[:]))
            P.op("DVE", (tag, f"p{mx}ev", t), (),
                 lambda e, t=t, mk=mk, mb_=mb_, cb=cb: e.scalar_tensor_tensor(
                     mk[:, 8 * t:8 * t + 8], pp[:], C, mb_[:, 8 * t:8 * t + 8],
                     ALU.is_lt, ALU.mult, accum_out=cb[:, 6 + mx:7 + mx]))
            P.op("DVE", (tag, f"run{mx}", t + 1), [(tag, f"csmm{mx}", t)],
                 lambda e, runx=runx: e.tensor_tensor(runx[:], runx[:], pcs[0:1, :], ALU.add))
        P.run()

    P = Plan(ctx, tag + "G5")
    for t in range(NT):
        cb = colbuf[:, 32 * t:32 * t + 32]
        P.op("DVE", None, [(tag, "p2ev", NT - 1), (tag, "run2", NT)],
             lambda e, t=t, cb=cb: e.scalar_tensor_tensor(
                 scr[:], p1m[:, 8 * t:8 * t + 8], 1.0, m1kb[:, 8 * t:8 * t + 8],
                 ALU.mult, ALU.mult, accum_out=cb[:, 9:10]))
        P.op("DVE", None, (),
             lambda e, t=t, cb=cb: e.scalar_tensor_tensor(
                 scr[:], p2m[:, 8 * t:8 * t + 8], 1.0, m2kb[:, 8 * t:8 * t + 8],
                 ALU.mult, ALU.mult, accum_out=cb[:, 10:11]))
        for k in range(2):
            KPT = cb[:, 7 + k:8 + k]
            POS = cb[:, 9 + k:10 + k]
            EIX = cb[:, 4:5] if k == 0 else cb[:, 6:7]
            GRW = cb[:, 3:4] if k == 0 else cb[:, 5:6]
            DRP = cb[:, 21 + k:22 + k]
            SLF = cb[:, 17 + k:18 + k]
            DFL = cb[:, 19 + k:20 + k]
            MOD = cb[:, 23 + k:24 + k]
            PKF = cb[:, 25 + k:26 + k]
            c = 2 * t + k
            gdep = [(tag, "g1" if k == 0 else "g2", t)]
            P.op("DVE", None, gdep, lambda e, DRP=DRP, KPT=KPT: e.tensor_scalar(
                DRP, KPT, 1.0, -99999.0, ALU.subtract, ALU.mult))
            P.op("DVE", None, (), lambda e, SLF=SLF, EIX=EIX, POS=POS: e.scalar_tensor_tensor(
                SLF, EIX, 1024.0, POS, ALU.mult, ALU.add))
            P.op("DVE", None, (), lambda e, SLF=SLF, DRP=DRP: e.tensor_tensor(
                SLF, SLF, DRP, ALU.add))
            P.op("DVE", None, (), lambda e, SLF=SLF, t=t, k=k: e.tensor_scalar(
                SLF, SLF, float(8192 + 2 * t + k), None, ALU.min, ALU.bypass))
            P.op("DVE", None, (), lambda e, GRW=GRW, c=c: e.tensor_scalar(
                gcol[:, c:c + 1], GRW, 0.5, None, ALU.mult, ALU.bypass))
            P.op("DVE", None, (), lambda e, SLF=SLF, c=c: e.tensor_copy(
                sl32[:, c:c + 1], SLF))
            P.op("DVE", None, (), lambda e, SLF=SLF, c=c: e.tensor_copy(
                sl16[:, c:c + 1], SLF))
            P.op("DVE", None, (), lambda e, c=c, k=k: e.tensor_scalar(
                d32[:, k:k + 1], sl32[:, c:c + 1], 4, None,
                ALU.arith_shift_right, ALU.bypass))
            P.op("DVE", None, (), lambda e, DFL=DFL, k=k: e.tensor_copy(
                DFL, d32[:, k:k + 1]))
            P.op("DVE", None, (), lambda e, MOD=MOD, DFL=DFL, SLF=SLF: e.scalar_tensor_tensor(
                MOD, DFL, -16.0, SLF, ALU.mult, ALU.add))
            P.op("DVE", None, (), lambda e, PKF=PKF, MOD=MOD, DFL=DFL: e.scalar_tensor_tensor(
                PKF, MOD, 520.0, DFL, ALU.mult, ALU.add))
            P.op("DVE", (tag, "idx", t) if k == 1 else None, (),
                 lambda e, PKF=PKF, c=c: e.tensor_copy(pk32[:, c:c + 1], PKF))
    P.run()


def _emit_scatters(ctx, tag, imap_d, gate_d, cmbi_d, B):
    cs = ctx.cs
    sl32, pk32, sl16, gcol, zi16 = B["sl32"], B["pk32"], B["sl16"], B["gcol"], B["zi16"]
    zf32 = B["zf32"]
    P = Plan(ctx, tag + "F2")
    P.op("DVE", (tag, "zi"), ["fence"], lambda e: e.memset(zi16[:], 0))
    P.op("DVE", (tag, "zf"), (), lambda e: e.memset(zf32[:], 0.0))
    P.dma("POOL", (tag, "imz"), [(tag, "zi")],
          lambda e: e.dma_start(imap_d[:].rearrange("(p f) -> p f", p=128),
                                zi16[:, 0:65]),
          stream="imz", pool=1)
    P.dma("POOL", (tag, "gz"), [(tag, "zf")],
          lambda e: e.dma_start(gate_d[:].rearrange("(p f) -> p f", p=128),
                                zf32[:]),
          stream="gz", pool=1)
    for t in range(NT):
        for k in range(2):
            c = 2 * t + k
            P.dma("POOL", (tag, "sctk", t, k), [(tag, "idx", t), (tag, "imz")],
                  lambda e, c=c, t=t: e.indirect_dma_start(
                      imap_d[:].rearrange("(r x) -> r x", x=1),
                      bass.IndirectOffsetOnAxis(pk32[:, c:c + 1], 0),
                      cs["iotok"][:, t:t + 1], None),
                  stream="sctk", idx=c, pool=4)
            P.dma("POOL", (tag, "scg", t, k), [(tag, "idx", t), (tag, "gz")],
                  lambda e, c=c: e.indirect_dma_start(
                      gate_d[:].rearrange("(r x) -> r x", x=1),
                      bass.IndirectOffsetOnAxis(sl32[:, c:c + 1], 0),
                      gcol[:, c:c + 1], None),
                  stream="scg", idx=c, pool=4)
            P.dma("SP", (tag, "cmbist", t, k), [(tag, "idx", t)],
                  lambda e, c=c, t=t, k=k: e.dma_start(
                      cmbi_d[k, t].rearrange("(p x) -> p x", x=1), sl16[:, c:c + 1]),
                  stream="cmbi", idx=c, pool=4)
    P.run()


def _emit_experts(ctx, tag, W, xhi_d, imap_d, gate_d, eo_d):
    nc = ctx.nc
    HC = 512
    halves = [(0, HSPLIT), (HSPLIT, NHT)]
    with contextlib.ExitStack() as p3:
        sb3 = lambda n, s, d: p3.enter_context(nc.sbuf_tensor(f"{tag}3_{n}", s, d))
        ps3 = lambda n, s: p3.enter_context(nc.psum_tensor(f"{tag}3_{n}", s, F32))
        hidT = sb3("hidT", [128, 23 * 1024], BF16)
        wores = sb3("wores", [128, 23 * 1024], BF16)
        eoacc = sb3("eoacc", [128, 8 * 1024], F32)
        eib = sb3("eib", [128, 16 * 512], BF16)
        wixb = sb3("wixb", [128, 2 * 32], mybir.dt.int16)
        glb = sb3("glb", [128, 16], F32)
        w0b = sb3("w0b", [128, 2 * 8 * HC], BF16)
        w1b = sb3("w1b", [128, 2 * 8 * HC], BF16)
        sqb = sb3("sqb", [128, 1024], F32)
        u2b = sb3("u2b", [128, 1024], F32)
        gb = sb3("gb", [128, 1024], F32)
        eosb = sb3("eosb", [128, 2 * 1024], BF16)
        ph0 = [ps3(f"ph0{i}", [128, 1024]) for i in range(2)]
        ph1 = [ps3(f"ph1{i}", [128, 1024]) for i in range(2)]

        wmmC_key = {}
        g = 0
        for ex_ in range(8):
            for hh_, (ht0_, ht1_) in enumerate(halves):
                hbase_ = 128 * ht0_
                nchunk_ = (128 * ht1_ - hbase_ + HC - 1) // HC
                for lc_ in range(nchunk_):
                    last_ht = min(ht1_ - 1, (hbase_ + (lc_ + 1) * HC - 1) // 128)
                    wmmC_key[g] = (tag, "wmmht", ex_, last_ht)
                    g += 1

        P = Plan(ctx, tag + "F3")
        P.op("DVE", (tag, "wixz"), ["fence"], lambda e: e.memset(wixb[:], 0))
        gw = [0]
        prev_tg = [None]
        SCTKF = [(tag, "sctk", NT - 1, 0), (tag, "sctk", NT - 1, 1),
                 (tag, "sctk", NT - 2, 0), (tag, "sctk", NT - 2, 1)]
        SCGF = [(tag, "scg", NT - 1, 0), (tag, "scg", NT - 1, 1),
                (tag, "scg", NT - 2, 0), (tag, "scg", NT - 2, 1)]
        CMBF = [(tag, "cmbist", NT - 1, 0), (tag, "cmbist", NT - 1, 1),
                (tag, "cmbist", NT - 2, 0), (tag, "cmbist", NT - 2, 1)]
        STHIF = [(tag, "sthi", NT - 1), (tag, "sthi", NT - 2), (tag, "sthi", NT - 3)]
        for ex in range(8):
            deps = list(SCGF)
            if ex >= 2:
                deps.append((tag, "eoev", ex - 2, 7, 1))
            for ct_ in range(8):
                P.dma("SP", (tag, "gld", ex) if ct_ == 7 else None, deps if ct_ == 0 else [],
                      lambda e, ex=ex, ct_=ct_: e.dma_start(
                          glb[:, 8 * (ex % 2) + ct_:8 * (ex % 2) + ct_ + 1],
                          gate_d[1024 * ex + 128 * ct_:1024 * ex + 128 * ct_ + 128]
                          .rearrange("(p x) -> p x", x=1)),
                      stream="gld", idx=0, pool=1)
            for wi in range(2):
                w = 2 * ex + wi
                for b2 in range(2):
                    deps = []
                    if w >= 2:
                        deps.append((tag, "eild", ex - 1, wi))
                    if w == 0 and b2 == 0:
                        deps += SCTKF + CMBF
                    if w <= 1:
                        deps.append((tag, "wixz"))
                    P.dma("SP", (tag, "wixld", w, b2), deps,
                          lambda e, w=w, b2=b2: e.dma_start(
                              wixb[16 * b2:16 * b2 + 16, 32 * (w % 2):32 * (w % 2) + 32],
                              imap_d[:].rearrange("(q j) -> q j", q=16)[:, 32 * w:32 * w + 32]),
                          stream="wix", idx=w, pool=2)
                deps = [(tag, "wixld", w, 0), (tag, "wixld", w, 1), (tag, "wixz")]
                if ex > 0:
                    deps.append((tag, "wmmht", ex - 1, NHT - 1))
                else:
                    deps += STHIF
                P.dma("POOL", (tag, "eild", ex, wi), deps,
                      lambda e, wi=wi, w=w: e.dma_gather(
                          eib[:, wi * 4096:wi * 4096 + 4096].rearrange(
                              "p (a f) -> p a f", a=8),
                          xhi_d[:], wixb[:, 32 * (w % 2):32 * (w % 2) + 32],
                          512, ctx.r512, M, transpose=True),
                      stream="eild", idx=w, pool=2)
            for hh, (ht0, ht1) in enumerate(halves):
                hbase = 128 * ht0
                nchunk = (128 * ht1 - hbase + HC - 1) // HC
                for lc in range(nchunk):
                    cw = min(HC, H - hbase - HC * lc)
                    b = gw[0] % 2
                    deps = []
                    if gw[0] >= 2:
                        deps.append(wmmC_key[gw[0] - 2])
                    if gw[0] < 2:
                        deps += [(tag, "cmbist", NT - 1, 1), (tag, "imz")]
                    P.dma("POOL", (tag, "w0", ex, hh, lc), deps,
                          lambda e, b=b, cw=cw, ex=ex, h0=hbase + HC * lc: e.dma_start(
                              w0b[:, b * 8 * HC:b * 8 * HC + 8 * cw].rearrange("p (c h) -> p c h", c=8),
                              W["w0"][ex].rearrange("(c p) h -> p c h", p=128)[:, :, h0:h0 + cw]),
                          stream="w01", idx=gw[0], pool=3)
                    P.dma("POOL", (tag, "w1", ex, hh, lc), (),
                          lambda e, b=b, cw=cw, ex=ex, h0=hbase + HC * lc: e.dma_start(
                              w1b[:, b * 8 * HC:b * 8 * HC + 8 * cw].rearrange("p (c h) -> p c h", c=8),
                              W["w1"][ex].rearrange("(c p) h -> p c h", p=128)[:, :, h0:h0 + cw]),
                          stream="w01", idx=gw[0], pool=3)
                    gw[0] += 1
                for ht in range(ht0, ht1):
                    hrows = min(128, H - 128 * ht)
                    deps = []
                    if (ex, hh) > (0, 0):
                        pex, phh = (ex, hh - 1) if hh == 1 else (ex - 1, 1)
                        deps.append((tag, "odone", pex, phh))
                    else:
                        deps += [(tag, "cmbist", NT - 1, 1), (tag, "imz")]
                    P.dma("POOL", (tag, "wo", ex, ht), deps,
                          lambda e, hrows=hrows, ex=ex, ht=ht, ht0=ht0: e.dma_start(
                              wores[0:hrows, 1024 * (ht - ht0):1024 * (ht - ht0) + 1024],
                              W["wo"][ex][128 * ht:128 * ht + hrows, :]),
                          stream="woh", idx=2 * ex + hh, pool=2)
                for ht in range(ht0, ht1):
                    lc = (128 * ht - hbase) // HC
                    cw = min(HC, H - hbase - HC * lc)
                    off = 128 * ht - hbase - HC * lc
                    hrows = min(128, H - 128 * ht)
                    b = (gw[0] - nchunk + lc) % 2
                    pb = ht % 2
                    for half in range(2):
                        for mt in range(8):
                            deps = []
                            if mt == 0:
                                deps = [(tag, "w0", ex, hh, lc), (tag, "w1", ex, hh, lc),
                                        (tag, "eild", ex, 0), (tag, "eild", ex, 1)]
                                gi = ex * NHT + ht
                                if gi >= 2 and half == 0:
                                    pht = ht - 2
                                    pe_ = ex
                                    if pht < 0:
                                        pe_ -= 1
                                        pht += NHT
                                    deps.append((tag, "hid", pe_, pht))
                                if ht - ht0 < 2 and half == 0 and (ex, hh) > (0, 0):
                                    pex, phh = (ex, 0) if hh == 1 else (ex - 1, 1)
                                    for ctp in (6, 7):
                                        for mwp in (0, 1):
                                            deps.append((tag, "eoh", pex, phh, ctp, mwp))
                            P.op("PE", None, deps,
                                 lambda e, pb=pb, hrows=hrows, half=half, b=b, mt=mt, cw=cw,
                                        off=off: e.matmul(
                                     ph0[pb][0:hrows, 512 * half:512 * half + 512],
                                     w0b[:, b * 8 * HC + mt * cw + off:
                                            b * 8 * HC + mt * cw + off + hrows],
                                     eib[:, half * 4096 + 512 * mt:half * 4096 + 512 * mt + 512],
                                     start=(mt == 0), stop=(mt == 7)))
                        for mt in range(8):
                            key = (tag, "wmmht", ex, ht) if (half == 1 and mt == 7) else None
                            P.op("PE", key, (),
                                 lambda e, pb=pb, hrows=hrows, half=half, b=b, mt=mt, cw=cw,
                                        off=off: e.matmul(
                                     ph1[pb][0:hrows, 512 * half:512 * half + 512],
                                     w1b[:, b * 8 * HC + mt * cw + off:
                                            b * 8 * HC + mt * cw + off + hrows],
                                     eib[:, half * 4096 + 512 * mt:half * 4096 + 512 * mt + 512],
                                     start=(mt == 0), stop=(mt == 7)))
                    if ht == ht1 - 1:
                        ctxkey = (tag, "wmmC", gw[0] - nchunk + lc)
                    SQ = sqb[0:hrows, :]
                    U2 = u2b[0:hrows, :]
                    G = gb[0:hrows, :]
                    HD = hidT[0:hrows, 1024 * (ht - ht0):1024 * (ht - ht0) + 1024]
                    deps_sq = [(tag, "wmmht", ex, ht)]
                    if prev_tg[0] is not None:
                        deps_sq.append(prev_tg[0])
                    P.op("ACT", (tag, "sq", ex, ht), deps_sq,
                         lambda e, SQ=SQ, pb=pb, hrows=hrows: e.activation(
                             SQ, ph0[pb][0:hrows, :], AF.Square, bias=0.0, scale=GELU_SA))
                    P.op("DVE", (tag, "u2", ex, ht), [(tag, "sq", ex, ht)],
                         lambda e, U2=U2, SQ=SQ, pb=pb, hrows=hrows: e.scalar_tensor_tensor(
                             U2, SQ, 1.0, ph0[pb][0:hrows, :], ALU.add, ALU.mult))
                    P.op("ACT", (tag, "sg", ex, ht), [(tag, "u2", ex, ht)],
                         lambda e, SQ=SQ, U2=U2: e.activation(SQ, U2, AF.Sigmoid,
                                                              bias=0.0, scale=GELU_K))
                    P.op("DVE", (tag, "tg", ex, ht), [(tag, "sg", ex, ht)],
                         lambda e, G=G, SQ=SQ, pb=pb, hrows=hrows: e.tensor_tensor(
                             G, SQ, ph0[pb][0:hrows, :], ALU.mult))
                    prev_tg[0] = (tag, "tg", ex, ht)
                    deps = []
                    if (ex, hh) > (0, 0):
                        pex, phh = (ex, 0) if hh == 1 else (ex - 1, 1)
                        deps.append((tag, "odone", pex, phh))
                    P.op("DVE", (tag, "hid", ex, ht), deps,
                         lambda e, G=G, pb=pb, hrows=hrows, HD=HD: e.tensor_tensor(
                             HD, G, ph1[pb][0:hrows, :], ALU.mult))
                for ct in range(8):
                    pb = ct % 2
                    for mw in range(2):
                        for ht in range(ht0, ht1):
                            hrows = min(128, H - 128 * ht)
                            deps = []
                            if ht == ht0:
                                deps = [(tag, "wo", ex, ht1 - 1),
                                        (tag, "hid", ex, ht1 - 1), (tag, "hid", ex, ht1 - 2)]
                                if ct >= 2:
                                    deps += [(tag, "eoh", ex, hh, ct - 2, 0),
                                             (tag, "eoh", ex, hh, ct - 2, 1)]
                            key = (tag, "omm", ex, hh, ct, mw) if ht == ht1 - 1 else None
                            P.op("PE", key, deps,
                                 lambda e, pb=pb, mw=mw, hrows=hrows, ht=ht, ct=ct,
                                        ht0=ht0: e.matmul(
                                     ph0[pb][0:128, 512 * mw:512 * mw + 512],
                                     hidT[0:hrows, 1024 * (ht - ht0) + 128 * ct:
                                              1024 * (ht - ht0) + 128 * ct + 128],
                                     wores[0:hrows, 1024 * (ht - ht0) + 512 * mw:
                                               1024 * (ht - ht0) + 512 * mw + 512],
                                     start=(ht == ht0), stop=(ht == ht1 - 1),
                                     skip_group_check=True))
                        EA = eoacc[:, 1024 * ct + 512 * mw:1024 * ct + 512 * mw + 512]
                        if hh == 0:
                            P.op("ACT", (tag, "eoh", ex, 0, ct, mw),
                                 [(tag, "omm", ex, 0, ct, 1)],
                                 lambda e, EA=EA, pb=pb, mw=mw: e.copy(
                                     EA, ph0[pb][0:128, 512 * mw:512 * mw + 512]))
                        else:
                            P.op("DVE", (tag, "eoh", ex, 1, ct, mw),
                                 [(tag, "omm", ex, 1, ct, 1), (tag, "eoh", ex, 0, ct, mw)],
                                 lambda e, EA=EA, pb=pb, mw=mw: e.tensor_tensor(
                                     EA, EA, ph0[pb][0:128, 512 * mw:512 * mw + 512], ALU.add))
                            deps = [(tag, "eoh", ex, 1, ct, mw), (tag, "gld", ex)]
                            if ct >= 2:
                                deps.append((tag, "eost", ex, ct - 2))
                            elif ex > 0:
                                deps.append((tag, "eost", ex - 1, ct + 6))
                            P.op("ACT", (tag, "eoev", ex, ct, mw), deps,
                                 lambda e, ct=ct, mw=mw, ex=ex, EA=EA: e.activation(
                                     eosb[:, (ct % 2) * 1024 + 512 * mw:
                                              (ct % 2) * 1024 + 512 * mw + 512],
                                     EA, AF.Copy, bias=0.0,
                                     scale=glb[:, 8 * (ex % 2) + ct:8 * (ex % 2) + ct + 1]))
                    if hh == 1:
                        P.dma("SP", (tag, "eost", ex, ct),
                              [(tag, "eoev", ex, ct, 0), (tag, "eoev", ex, ct, 1)],
                              lambda e, ex=ex, ct=ct: e.dma_start(
                                  eo_d[1024 * ex + 128 * ct:1024 * ex + 128 * ct + 128, :],
                                  eosb[:, (ct % 2) * 1024:(ct % 2) * 1024 + 1024]),
                              stream="eost", idx=ex * 8 + ct, pool=3)
                P.op("PE", (tag, "odone", ex, hh), [(tag, "omm", ex, hh, 7, 1)],
                     lambda e: e.nop(), ser=False)
        P.run()


def _emit_combine(ctx, tag, eo_d, cmbi_d, x_src, x_dst):
    nc = ctx.nc
    cs = ctx.cs
    with contextlib.ExitStack() as p4:
        sb4 = lambda n, s, d: p4.enter_context(nc.sbuf_tensor(f"{tag}4_{n}", s, d))
        cixb = sb4("cixb", [128, 4 * 8], mybir.dt.int16)
        gbuf = sb4("gbuf", [128, 4 * M], BF16)
        resb = sb4("resb", [128, 2 * M], F32)
        outb = sb4("outb", [128, 2 * M], F32)
        zb16 = sb4("zb16", [128, M], BF16)
        pcb = [p4.enter_context(nc.psum_tensor(f"{tag}4_pcb{i}", [128, M], F32))
               for i in range(2)]
        P = Plan(ctx, tag + "F4")
        eoF = [(tag, "eost", 7, 7), (tag, "eost", 7, 6), (tag, "eost", 7, 5)]
        P.op("DVE", (tag, "zb16"), ["fence"], lambda e: e.memset(zb16[:], 0.0))
        P.op("DVE", (tag, "cixz"), (), lambda e: e.memset(cixb[:], 0))
        P.dma("SP", (tag, "eoz"), [(tag, "zb16")],
              lambda e: e.dma_start(eo_d[8192:8256, :], zb16[0:64, :]),
              stream="eoz", pool=1)
        for t in range(NT):
            b = t % 2
            for k in range(2):
                slot = 2 * b + k
                CMBF = [(tag, "cmbist", NT - 1, 1), (tag, "cmbist", NT - 1, 0),
                        (tag, "cmbist", NT - 2, 1), (tag, "cmbist", NT - 2, 0)]
                for b2 in range(2):
                    deps = list(CMBF)
                    if t < 2:
                        deps.append((tag, "cixz"))
                    else:
                        deps.append((tag, "cgl", t - 2, k))
                    P.dma("SP", (tag, "cixld", t, k, b2), deps,
                          lambda e, t=t, k=k, b2=b2, slot=slot: e.dma_start(
                              cixb[16 * b2:16 * b2 + 16, 8 * slot:8 * slot + 8],
                              cmbi_d[k, t].rearrange("(q c) -> q c", q=16)),
                          stream="cix", idx=2 * t + k, pool=4)
                deps = [(tag, "cixld", t, k, 0), (tag, "cixld", t, k, 1), (tag, "cixz")]
                if t < 2:
                    deps += eoF + [(tag, "eoz")]
                else:
                    deps.append((tag, "pmm", t - 2, 1))
                P.dma("POOL", (tag, "cgl", t, k), deps,
                      lambda e, slot=slot: e.dma_gather(
                          gbuf[:].rearrange("p (s f) -> p s f", s=4)[:, slot:slot + 1, :],
                          eo_d[:], cixb[:, 8 * slot:8 * slot + 8],
                          128, ctx.r128, M),
                      stream="cgl", idx=2 * t + k, pool=4)
            RS = resb[:, M * b:M * b + M]
            OB = outb[:, M * b:M * b + M]
            deps = [(tag, "outst", t - 2)] if t >= 2 else ["fence"]
            P.dma("SP", (tag, "res", t), deps,
                  lambda e, RS=RS, t=t: e.dma_start(RS, x_src[128 * t:128 * t + 128, :]),
                  stream="res", idx=t, pool=3)
            for mw in range(2):
                for k in range(2):
                    deps = []
                    if mw == 0 and k == 0:
                        deps = [(tag, "cgl", t, 0), (tag, "cgl", t, 1)]
                        if t >= 2:
                            deps.append((tag, "outev", t - 2))
                    P.op("PE", (tag, "pmm", t, mw) if k == 1 else None, deps,
                         lambda e, t=t, b=b, mw=mw, k=k: e.matmul(
                             pcb[t % 2][:, 512 * mw:512 * mw + 512],
                             cs["sigP"][:],
                             gbuf[:, M * (2 * b + k) + 512 * mw:
                                     M * (2 * b + k) + 512 * mw + 512],
                             start=(k == 0), stop=(k == 1)))
            P.op("DVE", (tag, "outev", t), [(tag, "pmm", t, 0), (tag, "pmm", t, 1),
                                            (tag, "res", t)],
                 lambda e, OB=OB, RS=RS, t=t: e.tensor_tensor(
                     OB, pcb[t % 2][:], RS, ALU.add))
            P.dma("SP", (tag, "outst", t), [(tag, "outev", t)],
                  lambda e, OB=OB, t=t: e.dma_start(x_dst[128 * t:128 * t + 128, :], OB),
                  stream="outst", idx=t, pool=4)
        P.run()


def attn_perm():
    perm = np.zeros(M, dtype=np.int64)
    for h in range(NHEAD):
        for j in range(32):
            perm[64 * h + j] = 64 * h + 2 * j
            perm[64 * h + 32 + j] = 64 * h + 2 * j + 1
    return perm




def _emit_rope2(P, tag, key, i, prev, psrc, dst, tabC, tabS, rtmp, rps, sw0, extra0=()):
    pre = P._last.get("DVE")
    base = [prev] + list(extra0)
    if pre is not None:
        base.append(("_idx_", pre))
    CO = tabC[:, 512 * sw0:512 * sw0 + 1024]
    SI = tabS[:, 512 * sw0:512 * sw0 + 1024]
    P.op("DVE", None, base,
         lambda e: e.tensor_tensor(rps[:], psrc[:], CO, ALU.mult))
    P.op("DVE", None, (),
         lambda e: e.tensor_tensor(rtmp[:], psrc[:], SI, ALU.mult))
    for B in (0, 64):
        P.op("DVE", None, (), lambda e, B=B: e.tensor_tensor(
            dst[B:B + 32, :], rps[B:B + 32, :], rtmp[B + 32:B + 64, :], ALU.subtract))
        P.op("DVE", key if B == 64 else None, (), lambda e, B=B: e.tensor_tensor(
            dst[B + 32:B + 64, :], rtmp[B:B + 32, :], rps[B + 32:B + 64, :], ALU.add))


def emit_attention(ctx, tag, x_src, x_dst, W):
    nc = ctx.nc
    cs = ctx.cs
    xln_d = nc.dram_tensor(f"{tag}_xln", [S, M], BF16, kind="Internal")
    qT_d = nc.dram_tensor(f"{tag}_qT", [8, 128, S], BF16, kind="Internal")
    latT_d = nc.dram_tensor(f"{tag}_latT", [4, 128, S], BF16, kind="Internal")
    ropes = make_rope_consts()
    cosF_d = nc.dram_tensor(f"{tag}_cosF", [128, S], BF16, kind="ExternalInput")
    sinF_d = nc.dram_tensor(f"{tag}_sinF", [128, S], BF16, kind="ExternalInput")
    ctx.rope_inputs = {
        f"{tag}_cosF": ropes["ropeA"].astype(ml_dtypes.bfloat16),
        f"{tag}_sinF": ropes["ropeB"].astype(ml_dtypes.bfloat16)}

    rope_scope = contextlib.ExitStack()
    sbr = lambda n, s, d: rope_scope.enter_context(
        nc.sbuf_tensor(f"{tag}_{n}", s, d, side="right"))
    cosF = sbr("cosFs", [128, S], BF16)
    sinF = sbr("sinFs", [128, S], BF16)
    rtmp = sbr("rtmp", [128, 1024], F32)

    with contextlib.ExitStack() as p1:
        sb1 = lambda n, s, d: p1.enter_context(nc.sbuf_tensor(f"{tag}1_{n}", s, d))
        xt = sb1("xt", [128, 2 * M], F32)
        st6 = sb1("st6", [128, 2 * 16], F32)
        mvb = sb1("mvb", [128, 4 * NT], F32)
        xlnb = sb1("xlnb", [128, 2 * M], BF16)
        P = Plan(ctx, tag + "A1")
        P.dma("SP", (tag, "cosF"), ["fence"], lambda e: e.dma_start(cosF[:], cosF_d[:]),
              stream="rope", pool=1)
        P.dma("SP", (tag, "sinF"), ["fence"], lambda e: e.dma_start(sinF[:], sinF_d[:]),
              stream="rope", pool=1)
        for t in range(NT):
            b = t % 2
            X = xt[:, M * b:M * b + M]
            S6 = st6[:, 16 * b:16 * b + 12]
            MV2 = st6[:, 16 * b + 12:16 * b + 14]
            VV = st6[:, 16 * b + 13:16 * b + 14]
            XL = xlnb[:, M * b:M * b + M]
            deps = ["fence"] + ([(tag, "xlnf", t - 2)] if t >= 2 else [])
            P.dma("SP", (tag, "xt", t), deps,
                  lambda e, X=X, t=t: e.dma_start(X, x_src[128 * t:128 * t + 128, :]),
                  stream="xt", idx=t, pool=3)
            deps = [(tag, "xt", t)] + ([(tag, "rstd", t - 2)] if t >= 2 else [])
            P.op("DVE", (tag, "bns", t), deps,
                 lambda e, S6=S6, X=X: e.bn_stats(S6[:, 0:6], X[:, 0:512]))
            P.op("DVE", None, (), lambda e, S6=S6, X=X: e.bn_stats(S6[:, 6:12], X[:, 512:1024]))
            P.op("DVE", (tag, "bn", t), (), lambda e, MV2=MV2, S6=S6: e.bn_aggr(MV2, S6))
            P.op("DVE", (tag, "mn", t), (),
                 lambda e, t=t, MV2=MV2: e.tensor_copy(mvb[:, 4 * t:4 * t + 1], MV2[:, 0:1]))
            P.op("ACT", None, [(tag, "bn", t)],
                 lambda e, VV=VV: e.activation(VV, VV, AF.Ln, bias=cs['eps5'][:, 0:1], scale=1.0))
            P.op("ACT", (tag, "rstd", t), (),
                 lambda e, t=t, VV=VV: e.activation(mvb[:, 4 * t + 1:4 * t + 2], VV, AF.Exp,
                                                    bias=0.0, scale=-0.5))
            deps = [(tag, "rstd", t)]
            if t >= 2:
                deps.append((tag, "stln", t - 2))
            P.op("DVE", (tag, "xlnf", t), deps,
                 lambda e, XL=XL, X=X, t=t: e.tensor_scalar(
                     XL, X, mvb[:, 4 * t:4 * t + 1], mvb[:, 4 * t + 1:4 * t + 2],
                     ALU.subtract, ALU.mult))
            P.dma("SP", (tag, "stln", t), [(tag, "xlnf", t)],
                  lambda e, XL=XL, t=t: e.dma_start(xln_d[128 * t:128 * t + 128, :], XL),
                  stream="sthi", idx=t, pool=3)
        P.run()

    with contextlib.ExitStack() as p2:
        sb2 = lambda n, s, d: p2.enter_context(nc.sbuf_tensor(f"{tag}2_{n}", s, d))
        ps2 = lambda n, s: p2.enter_context(nc.psum_tensor(f"{tag}2_{n}", s, F32))
        x2T = sb2("x2T", [128, 8 * S], BF16)
        wqb = sb2("wqb", [128, 8 * 1024], BF16)
        wdb = sb2("wdb", [128, 8 * 512], BF16)
        qsb = sb2("qsb", [128, 2 * 1024], BF16)
        latsb = sb2("latsb", [128, 2 * 1024], BF16)
        pq = [ps2(f"pq{i}", [128, 1024]) for i in range(2)]
        pl = ps2("pl", [128, 1024])
        rps2a = ps2("rps", [128, 1024])
        P = Plan(ctx, tag + "A2a")
        P.dma("POOL", (tag, "wqb"), [(tag, "stln", NT - 1)],
              lambda e: e.dma_start(wqb[:].rearrange("p (c f) -> p c f", c=8),
                                    W["wq"].rearrange("(c p) f -> p c f", p=128)),
              stream="watt", pool=1)
        P.dma("POOL", (tag, "wdb"), (),
              lambda e: e.dma_start(wdb[:].rearrange("p (c l) -> p c l", c=8),
                                    W["w_down"].rearrange("(c p) l -> p c l", p=128)),
              stream="watt", pool=1)
        for c in range(8):
            deps = [(tag, "stln", NT - k) for k in (1, 2, 3)]
            P.dma("SP", (tag, "x2T", c), deps,
                  lambda e, c=c: e.dma_start(x2T[:, S * c:S * c + S],
                                             xln_d[:, 128 * c:128 * c + 128], transpose=True),
                  stream="ldT", idx=c, pool=3)
        WQDEP = [(tag, "wqb"), (tag, "wdb")]
        for ft in range(8):
            for swp in range(4):
                i = ft * 4 + swp
                pb = i % 2
                for wi in range(2):
                    sw = 2 * swp + wi
                    for c in range(8):
                        deps = []
                        if c == 0 and wi == 0:
                            deps = WQDEP + [(tag, "x2T", cc) for cc in range(8)]
                            if i >= 2:
                                deps.append((tag, "rope", i - 2))
                        P.op("PE", (tag, "qmm", i) if (c == 7 and wi == 1) else None, deps,
                             lambda e, pb=pb, c=c, ft=ft, sw=sw, wi=wi: e.matmul(
                                 pq[pb][:, 512 * wi:512 * wi + 512],
                                 wqb[:, 1024 * c + 128 * ft:1024 * c + 128 * ft + 128],
                                 x2T[:, S * c + 512 * sw:S * c + 512 * sw + 512],
                                 start=(c == 0), stop=(c == 7)))
                QO = qsb[:, 1024 * pb:1024 * pb + 1024]
                extra0 = [(tag, "cosF"), (tag, "sinF")]
                if i >= 2:
                    extra0.append((tag, "qst", i - 2))
                _emit_rope2(P, tag, (tag, "rope", i), i, (tag, "qmm", i),
                            pq[pb], QO, cosF, sinF, rtmp, rps2a, 2 * swp, extra0=extra0)
                P.dma("SP", (tag, "qst", i), [(tag, "rope", i)],
                      lambda e, QO=QO, ft=ft, swp=swp: e.dma_start(
                          qT_d[ft][:, 1024 * swp:1024 * swp + 1024], QO),
                      stream="qst", idx=i, pool=3)
        for lt in range(4):
            for swp in range(4):
                i = lt * 4 + swp
                for wi in range(2):
                    sw = 2 * swp + wi
                    for c in range(8):
                        deps = []
                        if c == 0 and wi == 0:
                            deps = WQDEP + [(tag, "x2T", cc) for cc in range(8)]
                            if i >= 1:
                                deps.append((tag, "latev", i - 1))
                        P.op("PE", (tag, "lmm", i) if (c == 7 and wi == 1) else None, deps,
                             lambda e, c=c, lt=lt, sw=sw, wi=wi: e.matmul(
                                 pl[:, 512 * wi:512 * wi + 512],
                                 wdb[:, 512 * c + 128 * lt:512 * c + 128 * lt + 128],
                                 x2T[:, S * c + 512 * sw:S * c + 512 * sw + 512],
                                 start=(c == 0), stop=(c == 7)))
                LO = latsb[:, 1024 * (i % 2):1024 * (i % 2) + 1024]
                deps = [(tag, "lmm", i)]
                if i >= 2:
                    deps.append((tag, "latst", i - 2))
                P.op("ACT", (tag, "latev", i), deps,
                     lambda e, LO=LO: e.copy(LO, pl[:]))
                P.dma("SP", (tag, "latst", i), [(tag, "latev", i)],
                      lambda e, LO=LO, lt=lt, swp=swp: e.dma_start(
                          latT_d[lt][:, 1024 * swp:1024 * swp + 1024], LO),
                      stream="latst", idx=i, pool=3)
        P.run()

    with contextlib.ExitStack() as p34:
        sb34 = lambda n, s, d: p34.enter_context(nc.sbuf_tensor(f"{tag}34_{n}", s, d))
        kTs = sb34("kTs", [128, 8 * S], BF16)
        vab = sb34("vab", [128, NT * 1040], BF16)
        with contextlib.ExitStack() as p3:
            sb3 = lambda n, s, d: p3.enter_context(nc.sbuf_tensor(f"{tag}3_{n}", s, d))
            ps3 = lambda n, s: p3.enter_context(nc.psum_tensor(f"{tag}3_{n}", s, F32))
            latw = sb3("latw", [128, 2 * 4 * 1024], BF16)
            wkb = sb3("wkb", [128, 4 * 1024], BF16)
            wvb = sb3("wvb", [128, 4 * 1024], BF16)
            psk = [ps3(f"psk{i}", [128, 1024]) for i in range(2)]
            psv = [ps3(f"psv{i}", [128, 512]) for i in range(2)]
            rps2b = ps3("rps", [128, 1024])
            qstF = [(tag, "qst", 32 - k) for k in (1, 2, 3)]
            latF = [(tag, "latst", 16 - k) for k in (1, 2, 3)]
            P = Plan(ctx, tag + "A2b")
            P.dma("POOL", (tag, "wkb"), qstF + latF,
                  lambda e: e.dma_start(wkb[:].rearrange("p (c f) -> p c f", c=4),
                                        W["wk_up"].rearrange("(c p) f -> p c f", p=128)),
                  stream="watt", pool=1)
            P.dma("POOL", (tag, "wvb"), (),
                  lambda e: e.dma_start(wvb[:].rearrange("p (c f) -> p c f", c=4),
                                        W["wv_up"].rearrange("(c p) f -> p c f", p=128)),
                  stream="watt", pool=1)
            P.op("DVE", (tag, "vone"), qstF + latF, lambda e: e.memset(vab[:], 1.0))
            WKDEP = [(tag, "wkb"), (tag, "wvb")]
            for swp in range(4):
                lb = swp % 2
                for c in range(4):
                    deps = [(tag, "latst", 4 * c + swp)]
                    if swp >= 2:
                        deps.append((tag, "vmm", (8 * (swp - 2) + 7) * 2 + 1))
                    P.dma("SP", (tag, "latld", swp, c), deps,
                          lambda e, lb=lb, c=c, swp=swp: e.dma_start(
                              latw[:, 4096 * lb + 1024 * c:4096 * lb + 1024 * c + 1024],
                              latT_d[c][:, 1024 * swp:1024 * swp + 1024]),
                          stream="latld", idx=4 * swp + c, pool=8)
                for ft in range(8):
                    k_i = swp * 8 + ft
                    pb = k_i % 2
                    for wi in range(2):
                        for c in range(4):
                            deps = []
                            if c == 0 and wi == 0:
                                deps = WKDEP + [(tag, "latld", swp, cc) for cc in range(4)]
                                if k_i >= 2:
                                    deps.append((tag, "krope", k_i - 2))
                            P.op("PE", (tag, "kmm", k_i) if (c == 3 and wi == 1) else None, deps,
                                 lambda e, pb=pb, c=c, ft=ft, lb=lb, wi=wi: e.matmul(
                                     psk[pb][:, 512 * wi:512 * wi + 512],
                                     wkb[:, 1024 * c + 128 * ft:1024 * c + 128 * ft + 128],
                                     latw[:, 4096 * lb + 1024 * c + 512 * wi:
                                             4096 * lb + 1024 * c + 512 * wi + 512],
                                     start=(c == 0), stop=(c == 3)))
                    KO = kTs[:, S * ft + 1024 * swp:S * ft + 1024 * swp + 1024]
                    _emit_rope2(P, tag, (tag, "krope", k_i), k_i, (tag, "kmm", k_i),
                                psk[pb], KO, cosF, sinF, rtmp, rps2b, 2 * swp)
                for stl in range(8):
                    st = 8 * swp + stl
                    for mwin in range(2):
                        i = st * 2 + mwin
                        for c in range(4):
                            deps = []
                            if c == 0:
                                deps = WKDEP + [(tag, "vone")] +                                        [(tag, "latld", swp, cc) for cc in range(4)]
                                if i >= 2:
                                    deps.append((tag, "vev", i - 2))
                            P.op("PE", (tag, "vmm", i) if c == 3 else None, deps,
                                 lambda e, c=c, stl=stl, mwin=mwin, i=i, lb=lb: e.matmul(
                                     psv[i % 2][:],
                                     latw[:, 4096 * lb + 1024 * c + 128 * stl:
                                             4096 * lb + 1024 * c + 128 * stl + 128],
                                     wvb[:, 1024 * c + 512 * mwin:1024 * c + 512 * mwin + 512],
                                     start=(c == 0), stop=(c == 3)))
                        P.op("ACT", (tag, "vev", i), [(tag, "vmm", i)],
                             lambda e, st=st, mwin=mwin, i=i: e.copy(
                                 vab[:, 1040 * st:1040 * st + 1040].rearrange(
                                     "p (h d) -> p h d", d=65)[:, 8 * mwin:8 * mwin + 8, 0:64],
                                 psv[i % 2][:].rearrange("p (h d) -> p h d", d=64)))
            P.run()
        rope_scope.close()

        with contextlib.ExitStack() as p4:
            sb4 = lambda n, s, d: p4.enter_context(nc.sbuf_tensor(f"{tag}4_{n}", s, d))
            ps4 = lambda n, s: p4.enter_context(nc.psum_tensor(f"{tag}4_{n}", s, F32))
            wob = sb4("wob", [128, 8 * 1024], BF16)
            qtb = sb4("qtb", [128, 2 * 512], BF16)
            expb = sb4("expb", [128, 4 * 1024], BF16)
            outT = sb4("outT", [128, 8 * 512], BF16)
            rcb = sb4("rcb", [1, 1024], F32)
            avsb = sb4("avsb", [65, 2 * 1024], F32)
            bcsb = sb4("bcsb", [128, 1024], F32)
            resb = sb4("resb", [128, 2 * M], F32)
            outb = sb4("outb", [128, 2 * M], F32)
            scAB = [ps4(f"sc{i}", [128, 1024]) for i in range(2)]
            avAB = ps4("avAB", [128, 1024])
            bcps = ps4("bcps", [128, 1024])
            P = Plan(ctx, tag + "A3")
            vevF = [(tag, "vev", 2 * NT - 1)]
            P.dma("POOL", (tag, "wob"), vevF,
                  lambda e: e.dma_start(wob[:].rearrange("p (c f) -> p c f", c=8),
                                        W["wo"].rearrange("(c p) f -> p c f", p=128)),
                  stream="watt", pool=1)

            for qw in range(8):
                nkt = 4 * (qw + 1)
                for hp in range(8):
                    j = qw * 8 + hp
                    qb = j % 2
                    hA, hB = 2 * hp, 2 * hp + 1
                    QT = qtb[:, 512 * qb:512 * qb + 512]
                    deps = list(vevF) + [(tag, "qst", 4 * hp + k) for k in range(4)]
                    if j >= 2:
                        deps.append((tag, "avg", j - 2, 4 * ((j - 2) // 8 + 1) - 1))
                    P.dma("SP", (tag, "qld", j), deps,
                          lambda e, QT=QT, hp=hp, qw=qw: e.dma_start(
                              QT, qT_d[hp][:, 512 * qw:512 * qw + 512]),
                          stream="qld", idx=j, pool=3)

                    def emit_sc(kt, j=j, hp=hp, qw=qw, nkt=nkt, QT=QT):
                        SC = scAB[kt % 2]
                        deps = []
                        if kt <= 1:
                            deps = [(tag, "qld", j), (tag, "wob")]
                            if j >= 1:
                                pnkt = 4 * ((j - 1) // 8 + 1)
                                deps.append((tag, "exp", j - 1, pnkt - 2 + kt))
                            else:
                                deps.append((tag, "krope", 31))
                            if kt == 0 and hp == 0 and qw > 0:
                                deps += [(tag, "outev", 4 * qw - 1, 0),
                                         (tag, "outev", 4 * qw - 1, 1),
                                         (tag, "outev", 4 * qw - 2, 0),
                                         (tag, "outev", 4 * qw - 2, 1)]
                        else:
                            deps = [(tag, "exp", j, kt - 2)]
                        P.op("PE", None, deps,
                             lambda e, SC=SC, kt=kt: e.matmul(
                                 SC[:, 0:512],
                                 kTs[0:64, S * hp + 128 * kt:S * hp + 128 * kt + 128],
                                 QT[0:64, :], start=True, stop=True, tile_position=(0, 0)))
                        P.op("PE", (tag, "scmm", j, kt), (),
                             lambda e, SC=SC, kt=kt: e.matmul(
                                 SC[:, 512:1024],
                                 kTs[64:128, S * hp + 128 * kt:S * hp + 128 * kt + 128],
                                 QT[64:128, :], start=True, stop=True, tile_position=(64, 0)))
                        EX = expb[:, 1024 * (kt % 4):1024 * (kt % 4) + 1024]
                        deps = [(tag, "scmm", j, kt)]
                        if kt >= 4:
                            deps.append((tag, "avg", j, kt - 4))
                        elif j >= 1:
                            pnkt = 4 * ((j - 1) // 8 + 1)
                            deps.append((tag, "avg", j - 1, pnkt - 4 + kt))
                        P.op("ACT", (tag, "exp", j, kt), deps,
                             lambda e, EX=EX, SC=SC: e.activation(EX, SC[:], AF.Exp,
                                                                  bias=0.0, scale=0.125))
                        if kt >= nkt - 2:
                            for off in (0, 512):
                                key = (tag, "msk", j, kt) if off == 512 else None
                                P.op("DVE", key, [(tag, "exp", j, kt)] if off == 0 else (),
                                     lambda e, EX=EX, off=off: e.memset(EX[:, off:off + 256], 0.0))

                    def emit_av(kt, j=j, hp=hp, nkt=nkt, hA=hA, hB=hB):
                        EX = expb[:, 1024 * (kt % 4):1024 * (kt % 4) + 1024]
                        deps = [(tag, "msk", j, kt) if kt >= nkt - 2 else (tag, "exp", j, kt)]
                        if kt == 0 and j >= 1:
                            deps.append((tag, "avev", j - 1))
                        P.op("PE", None, deps,
                             lambda e, EX=EX, kt=kt: e.matmul(
                                 avAB[0:65, 0:512],
                                 vab[:, 1040 * kt + 65 * hA:1040 * kt + 65 * hA + 65],
                                 EX[:, 0:512], start=(kt == 0), stop=(kt == nkt - 1)))
                        P.op("PE", (tag, "avg", j, kt), (),
                             lambda e, EX=EX, kt=kt: e.matmul(
                                 avAB[0:65, 512:1024],
                                 vab[:, 1040 * kt + 65 * hB:1040 * kt + 65 * hB + 65],
                                 EX[:, 512:1024], start=(kt == 0), stop=(kt == nkt - 1)))

                    def emit_epilogue(jj):
                        AVS = avsb[:, 1024 * (jj % 2):1024 * (jj % 2) + 1024]
                        for half in range(2):
                            deps = [(tag, "rc", jj)]
                            if jj >= 1 and half == 0:
                                deps.append((tag, "bcev", jj - 1))
                            P.op("PE", (tag, "bcmm", jj, half), deps,
                                 lambda e, half=half: e.matmul(
                                     bcps[:, 512 * half:512 * half + 512], cs["ones1f"][:],
                                     rcb[0:1, 512 * half:512 * half + 512],
                                     start=True, stop=True))
                        deps = [(tag, "bcmm", jj, 1)]
                        if jj >= 1:
                            deps.append((tag, "outTev", jj - 1, 1))
                        P.op("DVE", (tag, "bcev", jj), deps,
                             lambda e: e.tensor_copy(bcsb[:], bcps[:]))
                        OT = outT[:, 512 * (jj % 8):512 * (jj % 8) + 512]
                        deps = [(tag, "bcev", jj)]
                        if jj >= 8:
                            deps.append((tag, "fmm", 4 * (jj // 8 - 1) + 3, 1))
                        P.op("DVE", (tag, "outTev", jj, 0), deps,
                             lambda e, OT=OT, AVS=AVS: e.tensor_tensor(
                                 OT[0:64, :], AVS[0:64, 0:512], bcsb[0:64, 0:512], ALU.mult))
                        P.op("DVE", (tag, "outTev", jj, 1), (),
                             lambda e, OT=OT, AVS=AVS: e.tensor_tensor(
                                 OT[64:128, :], AVS[0:64, 512:1024], bcsb[0:64, 512:1024],
                                 ALU.mult))

                    POS = max(1, min(nkt - 3, 7))
                    emit_sc(0)
                    for kt in range(1, nkt):
                        emit_sc(kt)
                        emit_av(kt - 1)
                        if kt == POS and hp >= 1:
                            emit_epilogue(j - 1)
                    emit_av(nkt - 1)
                    AVS = avsb[:, 1024 * (j % 2):1024 * (j % 2) + 1024]
                    deps = [(tag, "avg", j, nkt - 1)]
                    if j >= 2:
                        deps.append((tag, "outTev", j - 2, 1))
                    P.op("DVE", (tag, "avev", j), deps,
                         lambda e, AVS=AVS: e.tensor_copy(AVS, avAB[0:65, :]))
                    deps = [(tag, "avev", j)]
                    if j >= 1:
                        deps.append((tag, "bcmm", j - 1, 1))
                    P.op("DVE", (tag, "rc", j), deps,
                         lambda e, AVS=AVS: e.reciprocal(rcb[0:1, :], AVS[64:65, :]))
                    if hp == 7:
                        emit_epilogue(j)
                for ssub in range(4):
                    t = 4 * qw + ssub
                    b = t % 2
                    RS = resb[:, M * b:M * b + M]
                    OB = outb[:, M * b:M * b + M]
                    deps = [(tag, "outTev", qw * 8 + 7, 1)] if ssub == 0 else []
                    if t >= 2:
                        deps.append((tag, "outst", t - 2))
                    P.dma("SP", (tag, "res", t), deps + list(vevF),
                          lambda e, RS=RS, t=t: e.dma_start(RS, x_src[128 * t:128 * t + 128, :]),
                          stream="res", idx=t, pool=3)
                    for mw in range(2):
                        fps = scAB[ssub % 2][:, 512 * mw:512 * mw + 512]
                        for hp in range(8):
                            deps = []
                            if ssub == 0 and mw == 0:
                                deps = [(tag, "outTev", qw * 8 + hp, 1)]
                            if hp == 0:
                                deps = deps + [(tag, "exp", qw * 8 + 7, nkt - 1)]
                                if t >= 2:
                                    deps.append((tag, "outev", t - 2, mw))
                            P.op("PE", (tag, "fmm", t, mw) if hp == 7 else None, deps,
                                 lambda e, fps=fps, hp=hp, ssub=ssub, mw=mw: e.matmul(
                                     fps, outT[:, 512 * hp + 128 * ssub:512 * hp + 128 * ssub + 128],
                                     wob[:, 1024 * hp + 512 * mw:1024 * hp + 512 * mw + 512],
                                     start=(hp == 0), stop=(hp == 7)))
                        P.op("DVE", (tag, "outev", t, mw), [(tag, "fmm", t, mw), (tag, "res", t)],
                             lambda e, OB=OB, RS=RS, mw=mw, fps=fps: e.tensor_tensor(
                                 OB[:, 512 * mw:512 * mw + 512], fps,
                                 RS[:, 512 * mw:512 * mw + 512], ALU.add))
                    P.dma("SP", (tag, "outst", t), [(tag, "outev", t, 0), (tag, "outev", t, 1)],
                          lambda e, OB=OB, t=t: e.dma_start(x_dst[128 * t:128 * t + 128, :], OB),
                          stream="outst", idx=t, pool=4)
            P.run()
    emit_fence(ctx)


from concourse.bass_utils import run_bass_kernel_spmd

_BUILD_CACHE = {}


def _build_program():
    if "nc" in _BUILD_CACHE:
        return _BUILD_CACHE["nc"], _BUILD_CACHE["consts"], _BUILD_CACHE["rope"]
    nc = bass.Bass("TRN2", target_bir_lowering=False)
    ctx = Ctx(nc)
    x_d = nc.dram_tensor("x", [S, M], F32, kind="ExternalInput")
    out_d = nc.dram_tensor("out", [S, M], F32, kind="ExternalOutput")
    x1_d = nc.dram_tensor("x1s", [S, M], F32, kind="Internal")
    x2_d = nc.dram_tensor("x2s", [S, M], F32, kind="Internal")
    rw_d = nc.dram_tensor("rw", [M, E], F32, kind="ExternalInput")
    w0_d = nc.dram_tensor("w0", [E, M, H], F32, kind="ExternalInput")
    w1_d = nc.dram_tensor("w1", [E, M, H], F32, kind="ExternalInput")
    wof_d = nc.dram_tensor("wof", [E, H, M], F32, kind="ExternalInput")
    wq_d = nc.dram_tensor("wqp", [M, M], F32, kind="ExternalInput")
    wd_d = nc.dram_tensor("wd", [M, 512], F32, kind="ExternalInput")
    wk_d = nc.dram_tensor("wkp", [512, M], F32, kind="ExternalInput")
    wv_d = nc.dram_tensor("wv", [512, M], F32, kind="ExternalInput")
    woa_d = nc.dram_tensor("woa", [M, M], F32, kind="ExternalInput")

    consts = declare_consts(ctx)
    emit_pool_init(ctx)
    emit_const_loads(ctx)
    WF = dict(router_w=rw_d.ap(), w0=w0_d.ap(), w1=w1_d.ap(), wo=wof_d.ap())
    WA = dict(wq=wq_d.ap(), w_down=wd_d.ap(), wk_up=wk_d.ap(), wv_up=wv_d.ap(),
              wo=woa_d.ap())
    emit_ffn(ctx, "A", x_d.ap(), x1_d.ap(), WF)
    emit_attention(ctx, "T", x1_d.ap(), x2_d.ap(), WA)
    rope = dict(ctx.rope_inputs)
    emit_ffn(ctx, "B", x2_d.ap(), out_d.ap(), WF)
    emit_finish(ctx)
    mybir.codegen_inst_isa_subclasses(nc)
    _BUILD_CACHE.update(nc=nc, consts=consts, rope=rope)
    return nc, consts, rope


def kernel(L0_attn_x, L0_attn_wq, L0_attn_w_down, L0_attn_wk_up, L0_attn_wv_up,
           L0_attn_wo, L0_ffn_router_w, L0_ffn_w0_kernel, L0_ffn_w1_kernel,
           L0_ffn_wo_kernel):
    x = np.ascontiguousarray(np.asarray(L0_attn_x, dtype=np.float32))
    nc, consts, rope = _build_program()
    perm = attn_perm()
    wqp = np.ascontiguousarray(np.asarray(L0_attn_wq, np.float32)[:, perm])
    wkp = np.ascontiguousarray(np.asarray(L0_attn_wk_up, np.float32)[:, perm])
    base = dict(
        rw=np.ascontiguousarray(np.asarray(L0_ffn_router_w, np.float32)),
        w0=np.ascontiguousarray(np.asarray(L0_ffn_w0_kernel, np.float32)),
        w1=np.ascontiguousarray(np.asarray(L0_ffn_w1_kernel, np.float32)),
        wof=np.ascontiguousarray(np.asarray(L0_ffn_wo_kernel, np.float32)),
        wqp=wqp,
        wd=np.ascontiguousarray(np.asarray(L0_attn_w_down, np.float32)),
        wkp=wkp,
        wv=np.ascontiguousarray(np.asarray(L0_attn_wv_up, np.float32)),
        woa=np.ascontiguousarray(np.asarray(L0_attn_wo, np.float32)),
    )
    for k, v in consts.items():
        base["c_" + k] = v
    base.update(rope)
    in_maps = []
    for b in range(8):
        m = dict(base)
        m["x"] = np.ascontiguousarray(x[b])
        in_maps.append(m)
    res = run_bass_kernel_spmd(nc, in_maps, core_ids=list(range(8)))
    return np.stack([res.results[b]["out"] for b in range(8)], axis=0)


# version 10
# speedup vs baseline: 1.3031x; 1.0042x over previous
import numpy as np
import ml_dtypes
import contextlib
import concourse.bass as bass
import concourse.mybir as mybir
from concourse.library_config import mlp as MLP_LIB

F32 = mybir.dt.float32
F16 = mybir.dt.float16
BF16 = mybir.dt.bfloat16
AF = mybir.ActivationFunctionType
ALU = mybir.AluOpType
AX = mybir.AxisListType.X
BF = ml_dtypes.bfloat16

S = 4096
M = 1024
E = 8
CAP = 1024
H = 5464
NT = S // 128
NHT = (H + 127) // 128
HSPLIT = 21
NHEAD = 16
DH = 64
GELU_SA = float(np.sqrt(np.float64(0.044715)))
GELU_K = float(2.0 * np.sqrt(2.0 / np.pi))

DMA_OF = {"SP": "DSP", "POOL": "DPL", "ACT": "DACT"}
SER_ENGS = ("DVE", "ACT", "POOL")


class Rec:
    __slots__ = ("eng", "fn", "deps", "key", "inc", "sem_kind", "idx")
    def __init__(self, eng, fn, deps, key, inc, sem_kind, idx):
        self.eng = eng
        self.fn = fn
        self.deps = deps
        self.key = key
        self.inc = inc
        self.sem_kind = sem_kind
        self.idx = idx


class Ctx:
    def __init__(self, nc):
        self.nc = nc
        self.es = contextlib.ExitStack()
        self.T = {}
        self.n = {k: 0 for k in ("PE", "DVE", "ACT", "POOL", "SP", "DSP", "DPL", "DACT")}
        self.sem = {}
        for name in list(self.n):
            self.sem[name] = self.es.enter_context(nc.semaphore("s" + name))

    def dstream(self, name, idx=0, pool=1):
        key = f"D_{name}{idx % pool}"
        if key not in self.sem:
            self.sem[key] = self.es.enter_context(self.nc.semaphore(key))
            self.n[key] = 0
        return key

    def sb(self, name, shape, dt):
        return self.es.enter_context(self.nc.sbuf_tensor(name, shape, dt))


class Plan:
    def __init__(self, ctx, name=""):
        self.ctx = ctx
        self.name = name
        self.recs = []
        self._last = {}

    def op(self, eng, key, deps, fn, ser=True):
        deps = list(deps)
        if ser and eng in SER_ENGS and eng in self._last:
            deps.append(("_idx_", self._last[eng]))
        r = Rec(eng, fn, tuple(deps), key, 1, eng, len(self.recs))
        self.recs.append(r)
        self._last[eng] = r.idx
        return r

    def dma(self, eng, key, deps, fn, stream=None, idx=0, pool=4):
        assert stream is not None
        kind = self.ctx.dstream(stream, idx, pool)
        r = Rec(eng, fn, tuple(deps), key, 16, kind, len(self.recs))
        self.recs.append(r)
        return r

    def run(self):
        ctx = self.ctx
        nc = ctx.nc
        cnt = dict(ctx.n)
        toks = {}
        idx_toks = {}
        keyidx = {}
        for r in self.recs:
            cnt[r.sem_kind] += r.inc
            idx_toks[r.idx] = (ctx.sem[r.sem_kind], cnt[r.sem_kind])
            if r.key is not None:
                toks[r.key] = idx_toks[r.idx]
                keyidx[r.key] = r.idx
        for r in self.recs:
            for k in r.deps:
                if isinstance(k, tuple) and len(k) == 2 and k[0] == "_idx_":
                    continue
                if k in toks:
                    if keyidx[k] >= r.idx and self.recs[keyidx[k]].eng == r.eng:
                        raise RuntimeError(
                            f"plan {self.name}: same-engine future dep {k} at op {r.idx}")
                elif k not in ctx.T:
                    raise KeyError(f"plan {self.name}: unresolved dep {k!r}")

        def resolve(k):
            if isinstance(k, tuple) and len(k) == 2 and k[0] == "_idx_":
                return idx_toks[k[1]]
            return toks[k] if k in toks else ctx.T[k]

        per_eng = {}
        for r in self.recs:
            per_eng.setdefault(r.eng, []).append(r)

        def emit(eng_handle, eng_name):
            for r in per_eng.get(eng_name, ()):
                best = {}
                for k in r.deps:
                    sem, val = resolve(k)
                    if sem.num not in best or best[sem.num][1] < val:
                        best[sem.num] = (sem, val)
                for sem, val in best.values():
                    eng_handle.wait_ge(sem, val)
                inst = r.fn(eng_handle)
                inst.then_inc(ctx.sem[r.sem_kind], r.inc)

        with nc.Block() as blk:
            if "SP" in per_eng:
                @blk.sync
                def _(e):
                    emit(e, "SP")
            if "POOL" in per_eng:
                @blk.gpsimd
                def _(e):
                    emit(e, "POOL")
            if "PE" in per_eng:
                @blk.tensor
                def _(e):
                    emit(e, "PE")
            if "DVE" in per_eng:
                @blk.vector
                def _(e):
                    emit(e, "DVE")
            if "ACT" in per_eng:
                @blk.scalar
                def _(e):
                    emit(e, "ACT")
        ctx.n.update(cnt)
        ctx.T.update(toks)


def make_consts():
    iota8f = np.broadcast_to(np.arange(8, dtype=np.float32), (128, 8)).copy()
    return {
        "iota8f": iota8f,
        "iota8h": iota8f.astype(np.float16),
        "iotam999": iota8f - 999.0,
        "iota1024h": np.broadcast_to(np.arange(1024, dtype=np.float16), (128, 1024)).copy(),
        "triu": (np.arange(128)[:, None] < np.arange(128)[None, :]).astype(BF),
        "onescol": np.ones((128, 1), BF),
        "onescolf": np.ones((128, 1), np.float32),
        "ones1f": np.ones((1, 128), np.float32),
        "ones128f": np.ones((128, 128), np.float32),
        "ident8": np.eye(8, dtype=np.float32),
        "ident128f": np.eye(128, dtype=np.float32),
        "iotapc": (np.arange(128, dtype=np.float32)[:, None]
                   + 128.0 * np.arange(64, dtype=np.float32)[None, :]),
        "eps5": np.full((128, 1), 1e-5, np.float32),
        "iotok": (np.arange(128, dtype=np.int16)[:, None]
                  + 128 * np.arange(NT, dtype=np.int16)[None, :]),
        "sigP": _sig_perm_mat(),
    }


def _sig_perm_mat():
    m = np.zeros((128, 128), BF)
    for i in range(128):
        m[i, 8 * (i % 16) + i // 16] = 1
    return m


def make_rope_consts():
    inv = (1.0 / (np.float32(10000.0) **
                  (np.arange(0, DH, 2, dtype=np.float32) / np.float32(DH)))).astype(np.float32)
    ang = np.outer(np.arange(S, dtype=np.float32), inv).astype(np.float32)
    cos = np.cos(ang).astype(np.float32).T
    sin = np.sin(ang).astype(np.float32).T
    return {"ropeA": np.concatenate([cos, cos, cos, cos], axis=0),
            "ropeB": np.concatenate([sin, sin, sin, sin], axis=0)}

CONST_DTYPES = {
    "iota8f": F32, "iota8h": F16, "iotam999": F32, "iota1024h": F16,
    "triu": BF16, "onescol": BF16, "onescolf": F32, "ones1f": F32, "ones128f": F32,
    "ident8": F32, "ident128f": F32, "iotapc": F32, "eps5": F32,
    "iotok": mybir.dt.int16, "sigP": BF16,
}
ROPE_DTYPES = {"ropeA": F32, "ropeB": F32}


def feat_perm():
    perm = np.zeros(M, dtype=np.int64)
    for ft in range(8):
        hA, hB = 2 * ft, 2 * ft + 1
        base = 128 * ft
        for j in range(32):
            perm[base + j] = hA * 64 + 2 * j
            perm[base + 32 + j] = hB * 64 + 2 * j
            perm[base + 64 + j] = hA * 64 + 2 * j + 1
            perm[base + 96 + j] = hB * 64 + 2 * j + 1
    return perm


def declare_consts(ctx):
    nc = ctx.nc
    ctx.cd = {}
    ctx.cs = {}
    consts = make_consts()
    for k, v in consts.items():
        ctx.cd[k] = nc.dram_tensor("c_" + k, list(v.shape), CONST_DTYPES[k], kind="ExternalInput")
        ctx.cs[k] = ctx.sb("s_" + k, list(v.shape), CONST_DTYPES[k])
    return consts


def emit_fence(ctx, key="fence"):
    nc = ctx.nc
    with nc.Block() as blk:
        @blk.sync
        def _(e):
            for k, sem in ctx.sem.items():
                if k.startswith("D") and ctx.n.get(k, 0) > 0:
                    e.wait_ge(sem, ctx.n[k])
            e.nop().then_inc(ctx.sem["SP"], 1)
    ctx.n["SP"] += 1
    ctx.T[key] = (ctx.sem["SP"], ctx.n["SP"])


def emit_finish(ctx):
    nc = ctx.nc
    with nc.Block() as blk:
        @blk.sync
        def _(e):
            for k, sem in ctx.sem.items():
                if k.startswith("D") and ctx.n.get(k, 0) > 0:
                    e.wait_ge(sem, ctx.n[k])


def emit_pool_init(ctx):
    nc = ctx.nc
    with nc.Block() as blk:
        @blk.gpsimd
        def _(g):
            g.load_library(MLP_LIB)
            ctx.r512 = g.alloc_register("gn512")
            g.reg_mov(ctx.r512, 512)
            ctx.r128 = g.alloc_register("gn128")
            g.reg_mov(ctx.r128, 128)


def emit_const_loads(ctx):
    P = Plan(ctx, "consts")
    for k in ctx.cd:
        dst, src = ctx.cs[k], ctx.cd[k]
        P.dma("SP", "consts", (), lambda e, d=dst, s=src: e.dma_start(d[:], s[:]), stream="misc", pool=1)
    P.run()
    ctx.T["fence"] = ctx.T["consts"]


def emit_ffn(ctx, tag, x_src, x_dst, W, first=True):
    nc = ctx.nc
    I16 = mybir.dt.int16
    I32 = mybir.dt.int32
    xhi_d = nc.dram_tensor(f"{tag}_xhi", [S, M], BF16, kind="Internal")
    xlo_d = nc.dram_tensor(f"{tag}_xlo", [S, M], BF16, kind="Internal")
    imap_d = nc.dram_tensor(f"{tag}_imap", [16 * 520], I16, kind="Internal")
    gate_d = nc.dram_tensor(f"{tag}_gate", [8320], F32, kind="Internal")
    cmbi_d = nc.dram_tensor(f"{tag}_cmbi", [2, NT, 128], I16, kind="Internal")
    eo_d = nc.dram_tensor(f"{tag}_eo", [8256, M], BF16, kind="Internal")

    with contextlib.ExitStack() as ph:
        sbp = lambda n, s, d: ph.enter_context(nc.sbuf_tensor(f"{tag}_{n}", s, d))
        colbuf = sbp("colbuf", [128, 32 * NT], F32)
        m1kb = sbp("m1kb", [128, 8 * NT], F32)
        m2kb = sbp("m2kb", [128, 8 * NT], F32)

        with contextlib.ExitStack() as p1:
            sb1 = lambda n, s, d: p1.enter_context(nc.sbuf_tensor(f"{tag}1_{n}", s, d))
            ps1 = lambda n, s: p1.enter_context(nc.psum_tensor(f"{tag}1_{n}", s, F32))
            B = dict(
                m1b=sb1("m1b", [128, 8 * NT], BF16),
                m2b=sb1("m2b", [128, 8 * NT], BF16),
                lg=sb1("lg", [128, 8 * NT], F32),
                rawb=sb1("rawb", [128, 8 * NT], F32),
                p1m=sb1("p1m", [128, 8 * NT], F32),
                p2m=sb1("p2m", [128, 8 * NT], F32),
                run1=sb1("run1", [1, 8], F32),
                run2=sb1("run2", [1, 8], F32),
                scr=sb1("scr", [128, 8], F32),
                scr2=sb1("scr2", [128, 8], F32),
                sl32=sb1("sl32", [128, 2 * NT], I32),
                pk32=sb1("pk32", [128, 2 * NT], I32),
                d32=sb1("d32", [128, 2], I32),
                sl16=sb1("sl16", [128, 2 * NT], I16),
                gcol=sb1("gcol", [128, 2 * NT], F32),
                zi16=sb1("zi16", [128, 66], I16),
                zf32=sb1("zf32", [128, 65], F32),
            )
            with contextlib.ExitStack() as p1g:
                ps1g = lambda n, s: p1g.enter_context(nc.psum_tensor(f"{tag}1_{n}", s, F32))
                B["pp"] = ps1g("pp", [128, 8])
                B["pcs"] = ps1g("pcs", [1, 8])
                B["ptp"] = ps1g("ptp", [128, 8])
                _emit_f1(ctx, tag, x_src, xhi_d, xlo_d, W, colbuf, B, sb1, ps1g)
                _emit_gating(ctx, tag, colbuf, m1kb, m2kb, B)
            _emit_scatters(ctx, tag, imap_d, gate_d, cmbi_d, B)

        _emit_experts(ctx, tag, W, xhi_d, imap_d, gate_d, eo_d, first)
        _emit_combine(ctx, tag, eo_d, cmbi_d, x_src, x_dst)
    emit_fence(ctx)


def _emit_f1(ctx, tag, x_src, xhi_d, xlo_d, W, colbuf, B, sb1, ps1):
    cs = ctx.cs
    xt = sb1("xt", [128, 2 * M], F32)
    st6 = sb1("st6", [128, 2 * 16], F32)
    mv = sb1("mv", [128, 4 * NT], F32)
    xln = sb1("xln", [128, 2 * M], F32)
    xhi = sb1("xhi", [128, 2 * M], BF16)
    xlo = sb1("xlo", [128, 2 * M], BF16)
    xhiT = sb1("xhiT", [128, 2 * 512], BF16)
    xloT = sb1("xloT", [128, 2 * 512], BF16)
    lgT = sb1("lgT", [8, 512], F32)
    rwh = sb1("rwh", [128, 64], BF16)
    rwl = sb1("rwl", [128, 64], BF16)
    rwf = sb1("rwf", [128, 64], F32)
    pl8 = ps1("pl8", [8, 512])
    lg = B["lg"]
    ptp = B["ptp"]

    P = Plan(ctx, tag + "F1a")
    P.dma("POOL", (tag, "rwh"), ["consts", "fence"],
          lambda e: e.dma_start(rwh[:].rearrange("p (c e) -> p c e", c=8),
                                W["router_w"].rearrange("(c p) e -> p c e", p=128)),
          stream="miscP", pool=1)
    P.dma("SP", (tag, "rwf"), ["consts", "fence"],
          lambda e: e.dma_start(rwf[:].rearrange("p (c e) -> p c e", c=8),
                                W["router_w"].rearrange("(c p) e -> p c e", p=128)), stream="misc", pool=1)
    P.op("DVE", (tag, "cbz"), ["fence"], lambda e: e.memset(colbuf[:], 0.0))
    P.op("DVE", (tag, "rwl"), [(tag, "rwh"), (tag, "rwf")],
         lambda e: e.tensor_tensor(rwl[:], rwf[:], rwh[:], ALU.subtract))
    for t in range(NT):
        b = t % 2
        X = xt[:, M * b:M * b + M]
        S6 = st6[:, 16 * b:16 * b + 12]
        MV2 = st6[:, 16 * b + 12:16 * b + 14]
        VV = st6[:, 16 * b + 13:16 * b + 14]
        XL = xln[:, M * b:M * b + M]
        XH = xhi[:, M * b:M * b + M]
        XO = xlo[:, M * b:M * b + M]
        deps = ["fence"] + ([(tag, "xlnf", t - 2)] if t >= 2 else [])
        P.dma("SP", (tag, "xt", t), deps,
              lambda e, X=X, t=t: e.dma_start(X, x_src[128 * t:128 * t + 128, :]), stream="xt", idx=t, pool=3)
        deps = [(tag, "xt", t)] + ([(tag, "rstd", t - 2)] if t >= 2 else [])
        P.op("DVE", (tag, "bns", t), deps,
             lambda e, S6=S6, X=X: e.bn_stats(S6[:, 0:6], X[:, 0:512]))
        P.op("DVE", None, (),
             lambda e, S6=S6, X=X: e.bn_stats(S6[:, 6:12], X[:, 512:1024]))
        P.op("DVE", (tag, "bn", t), (), lambda e, MV2=MV2, S6=S6: e.bn_aggr(MV2, S6))
        P.op("DVE", (tag, "mn", t), (),
             lambda e, t=t, MV2=MV2: e.tensor_copy(mv[:, 4 * t:4 * t + 1], MV2[:, 0:1]))
        P.op("ACT", None, [(tag, "bn", t)],
             lambda e, VV=VV: e.activation(VV, VV, AF.Ln, bias=cs['eps5'][:, 0:1], scale=1.0))
        P.op("ACT", (tag, "rstd", t), (),
             lambda e, t=t, VV=VV: e.activation(mv[:, 4 * t + 1:4 * t + 2], VV, AF.Exp,
                                                bias=0.0, scale=-0.5))
        deps = [(tag, "rstd", t)]
        if t >= 2:
            deps += [(tag, "sthi", t - 2), (tag, "stlo", t - 2)]
        P.op("DVE", (tag, "xlnf", t), deps,
             lambda e, XL=XL, X=X, t=t: e.tensor_scalar(
                 XL, X, mv[:, 4 * t:4 * t + 1], mv[:, 4 * t + 1:4 * t + 2],
                 ALU.subtract, ALU.mult))
        P.op("DVE", (tag, "xhi", t), (), lambda e, XH=XH, XL=XL: e.tensor_copy(XH, XL))
        P.op("DVE", (tag, "xlo", t), (),
             lambda e, XO=XO, XL=XL, XH=XH: e.tensor_tensor(XO, XL, XH, ALU.subtract))
        P.dma("SP", (tag, "sthi", t), [(tag, "xhi", t)],
              lambda e, XH=XH, t=t: e.dma_start(xhi_d[128 * t:128 * t + 128, :], XH), stream="sthi", idx=t, pool=3)
        P.dma("SP", (tag, "stlo", t), [(tag, "xlo", t)],
              lambda e, XO=XO, t=t: e.dma_start(xlo_d[128 * t:128 * t + 128, :], XO), stream="stlo", idx=t, pool=3)
    P.run()

    P = Plan(ctx, tag + "F1b")
    for w in range(8):
        for c in range(8):
            i = w * 8 + c
            b = i % 2
            hiT = xhiT[:, 512 * b:512 * b + 512]
            loT = xloT[:, 512 * b:512 * b + 512]
            deps = [(tag, "sthi", NT - k) for k in (1, 2, 3, 4)] + \
                   [(tag, "stlo", NT - k) for k in (1, 2, 3, 4)]
            if i >= 2:
                deps.append((tag, "lgmm", i - 2))
            P.dma("SP", (tag, "ldTh", i), deps,
                  lambda e, hiT=hiT, w=w, c=c: e.dma_start(
                      hiT, xhi_d[512 * w:512 * w + 512, 128 * c:128 * c + 128], transpose=True), stream="ldT", idx=i, pool=3)
            P.dma("SP", (tag, "ldT", i), (),
                  lambda e, loT=loT, w=w, c=c: e.dma_start(
                      loT, xlo_d[512 * w:512 * w + 512, 128 * c:128 * c + 128], transpose=True), stream="ldT", idx=i, pool=3)
            deps = [(tag, "ldT", i), (tag, "ldTh", i), (tag, "rwl")]
            if c == 0 and w > 0:
                deps.append((tag, "lgTc", w - 1))
            P.op("PE", None, deps,
                 lambda e, c=c, hiT=hiT: e.matmul(pl8[:], rwh[:, 8 * c:8 * c + 8], hiT,
                                                  start=(c == 0), stop=False))
            P.op("PE", None, (),
                 lambda e, c=c, loT=loT: e.matmul(pl8[:], rwh[:, 8 * c:8 * c + 8], loT,
                                                  start=False, stop=False))
            P.op("PE", (tag, "lgmm", i), (),
                 lambda e, c=c, hiT=hiT: e.matmul(pl8[:], rwl[:, 8 * c:8 * c + 8], hiT,
                                                  start=False, stop=(c == 7)))
        deps = [(tag, "lgmm", w * 8 + 7)]
        if w > 0:
            deps.append((tag, "lgtr", 4 * w - 1))
        P.op("ACT", (tag, "lgTc", w), deps, lambda e: e.copy(lgT[:], pl8[:]))
        for j in range(4):
            t_idx = 4 * w + j
            deps = [(tag, "lgTc", w)]
            if t_idx > 0:
                deps.append((tag, "lgn", t_idx - 1))
            P.op("PE", (tag, "lgtr", t_idx), deps,
                 lambda e, j=j: e.transpose(ptp[:], lgT[:, 128 * j:128 * j + 128], cs["ident8"][:]))
            P.op("ACT", (tag, "lgn", t_idx), [(tag, "lgtr", t_idx)],
                 lambda e, t_idx=t_idx: e.copy(lg[:, 8 * t_idx:8 * t_idx + 8], ptp[:]))
    P.run()


def _emit_gating(ctx, tag, colbuf, m1kb, m2kb, B):
    cs = ctx.cs
    C = float(CAP)
    lg, rawb = B["lg"], B["rawb"]
    m1b, m2b = B["m1b"], B["m2b"]
    p1m, p2m = B["p1m"], B["p2m"]
    run1, run2 = B["run1"], B["run2"]
    scr, scr2 = B["scr"], B["scr2"]
    pp, pcs = B["pp"], B["pcs"]
    sl32, pk32, d32, sl16, gcol = B["sl32"], B["pk32"], B["d32"], B["sl16"], B["gcol"]

    P = Plan(ctx, tag + "G1")
    for t in range(NT):
        L = lg[:, 8 * t:8 * t + 8]
        R = rawb[:, 8 * t:8 * t + 8]
        cb = colbuf[:, 32 * t:32 * t + 32]
        P.op("DVE", (tag, "negmax", t), [(tag, "cbz"), (tag, "lgn", t)],
             lambda e, cb=cb, L=L: e.reduce_max(cb[:, 0:1], L, axis=AX, negate=True))
        P.op("ACT", (tag, "exp", t), [(tag, "negmax", t)],
             lambda e, R=R, L=L, cb=cb: e.activation(R, L, AF.Exp, bias=cb[:, 0:1], scale=1.0,
                                                     accum_out=cb[:, 1:2]))
        P.op("DVE", None, [(tag, "exp", t)],
             lambda e, cb=cb: e.reciprocal(cb[:, 2:3], cb[:, 1:2]))
        P.op("DVE", None, (), lambda e, R=R, cb=cb: e.tensor_scalar(R, R, cb[:, 2:3], None, ALU.mult))
        P.op("DVE", (tag, "g1", t), (), lambda e, cb=cb, R=R: e.reduce_max(cb[:, 3:4], R, axis=AX))
        P.op("DVE", None, (), lambda e, R=R, cb=cb: e.tensor_scalar(scr[:], R, cb[:, 3:4], None, ALU.is_equal))
        P.op("DVE", None, (), lambda e: e.tensor_tensor(scr[:], scr[:], cs["iotam999"][:], ALU.mult))
        P.op("DVE", None, (), lambda e: e.tensor_scalar(scr[:], scr[:], 999.0, None, ALU.add))
        P.op("DVE", None, (), lambda e, cb=cb: e.tensor_reduce(cb[:, 4:5], scr[:], axis=AX, op=ALU.min))
        P.op("DVE", (tag, "m1", t), (),
             lambda e, t=t, cb=cb: e.tensor_scalar(m1b[:, 8 * t:8 * t + 8], cs["iota8h"][:],
                                                   cb[:, 4:5], None, ALU.is_equal))
        P.op("DVE", None, (), lambda e, t=t, R=R: e.tensor_tensor(scr2[:], m1b[:, 8 * t:8 * t + 8], R, ALU.mult))
        P.op("DVE", None, (), lambda e, R=R: e.tensor_tensor(scr2[:], R, scr2[:], ALU.subtract))
        P.op("DVE", (tag, "g2", t), (), lambda e, cb=cb: e.reduce_max(cb[:, 5:6], scr2[:], axis=AX))
        P.op("DVE", None, (), lambda e, cb=cb: e.tensor_scalar(scr[:], scr2[:], cb[:, 5:6], None, ALU.is_equal))
        P.op("DVE", None, (), lambda e: e.tensor_tensor(scr[:], scr[:], cs["iotam999"][:], ALU.mult))
        P.op("DVE", None, (), lambda e: e.tensor_scalar(scr[:], scr[:], 999.0, None, ALU.add))
        P.op("DVE", None, (), lambda e, cb=cb: e.tensor_reduce(cb[:, 6:7], scr[:], axis=AX, op=ALU.min))
        P.op("DVE", (tag, "m2", t), (),
             lambda e, t=t, cb=cb: e.tensor_scalar(m2b[:, 8 * t:8 * t + 8], cs["iota8h"][:],
                                                   cb[:, 6:7], None, ALU.is_equal))
    P.run()

    for mx in (1, 2):
        mk = m1kb if mx == 1 else m2kb
        pm = p1m if mx == 1 else p2m
        runx = run1 if mx == 1 else run2
        mb_ = m1b if mx == 1 else m2b
        P = Plan(ctx, tag + f"G{mx + 1}")
        if mx == 1:
            P.op("DVE", (tag, "run1", 0), (), lambda e: e.memset(run1[:], 0.0))
        else:
            for t in range(NT):
                P.op("PE", (tag, "m1cnt", t), [(tag, "p1ev", NT - 1), (tag, "run1", NT)],
                     lambda e, t=t: e.matmul(pcs[:], cs["onescolf"][:], m1kb[:, 8 * t:8 * t + 8],
                                             start=(t == 0), stop=(t == NT - 1)))
            P.op("DVE", (tag, "run2", 0), [(tag, "m1cnt", NT - 1)],
                 lambda e: e.tensor_copy(run2[:], pcs[0:1, :]))
        for t in range(NT):
            cb = colbuf[:, 32 * t:32 * t + 32]
            deps = [(tag, f"run{mx}", t), (tag, "m2", NT - 1)]
            if t > 0:
                deps.append((tag, f"p{mx}ev", t - 1))
            P.op("PE", None, deps,
                 lambda e, t=t, mb_=mb_: e.matmul(pp[:], cs["triu"][:], mb_[:, 8 * t:8 * t + 8],
                                                  start=True, stop=False))
            P.op("PE", (tag, f"ppmm{mx}", t), (),
                 lambda e, runx=runx: e.matmul(pp[:], cs["ones1f"][:], runx[:], start=False, stop=True))
            P.op("PE", (tag, f"csmm{mx}", t), [(tag, f"run{mx}", t)],
                 lambda e, t=t, mb_=mb_: e.matmul(pcs[:], cs["onescol"][:], mb_[:, 8 * t:8 * t + 8],
                                                  start=True, stop=True))
            P.op("DVE", None, [(tag, f"ppmm{mx}", t)],
                 lambda e, t=t, pm=pm: e.tensor_copy(pm[:, 8 * t:8 * t + 8], pp[:]))
            P.op("DVE", (tag, f"p{mx}ev", t), (),
                 lambda e, t=t, mk=mk, mb_=mb_, cb=cb: e.scalar_tensor_tensor(
                     mk[:, 8 * t:8 * t + 8], pp[:], C, mb_[:, 8 * t:8 * t + 8],
                     ALU.is_lt, ALU.mult, accum_out=cb[:, 6 + mx:7 + mx]))
            P.op("DVE", (tag, f"run{mx}", t + 1), [(tag, f"csmm{mx}", t)],
                 lambda e, runx=runx: e.tensor_tensor(runx[:], runx[:], pcs[0:1, :], ALU.add))
        P.run()

    P = Plan(ctx, tag + "G5")
    for t in range(NT):
        cb = colbuf[:, 32 * t:32 * t + 32]
        P.op("DVE", None, [(tag, "p2ev", NT - 1), (tag, "run2", NT)],
             lambda e, t=t, cb=cb: e.scalar_tensor_tensor(
                 scr[:], p1m[:, 8 * t:8 * t + 8], 1.0, m1kb[:, 8 * t:8 * t + 8],
                 ALU.mult, ALU.mult, accum_out=cb[:, 9:10]))
        P.op("DVE", None, (),
             lambda e, t=t, cb=cb: e.scalar_tensor_tensor(
                 scr[:], p2m[:, 8 * t:8 * t + 8], 1.0, m2kb[:, 8 * t:8 * t + 8],
                 ALU.mult, ALU.mult, accum_out=cb[:, 10:11]))
        for k in range(2):
            KPT = cb[:, 7 + k:8 + k]
            POS = cb[:, 9 + k:10 + k]
            EIX = cb[:, 4:5] if k == 0 else cb[:, 6:7]
            GRW = cb[:, 3:4] if k == 0 else cb[:, 5:6]
            DRP = cb[:, 21 + k:22 + k]
            SLF = cb[:, 17 + k:18 + k]
            DFL = cb[:, 19 + k:20 + k]
            MOD = cb[:, 23 + k:24 + k]
            PKF = cb[:, 25 + k:26 + k]
            c = 2 * t + k
            gdep = [(tag, "g1" if k == 0 else "g2", t)]
            P.op("DVE", None, gdep, lambda e, DRP=DRP, KPT=KPT: e.tensor_scalar(
                DRP, KPT, 1.0, -99999.0, ALU.subtract, ALU.mult))
            P.op("DVE", None, (), lambda e, SLF=SLF, EIX=EIX, POS=POS: e.scalar_tensor_tensor(
                SLF, EIX, 1024.0, POS, ALU.mult, ALU.add))
            P.op("DVE", None, (), lambda e, SLF=SLF, DRP=DRP: e.tensor_tensor(
                SLF, SLF, DRP, ALU.add))
            P.op("DVE", None, (), lambda e, SLF=SLF, t=t, k=k: e.tensor_scalar(
                SLF, SLF, float(8192 + 2 * t + k), None, ALU.min, ALU.bypass))
            P.op("DVE", None, (), lambda e, GRW=GRW, c=c: e.tensor_scalar(
                gcol[:, c:c + 1], GRW, 0.5, None, ALU.mult, ALU.bypass))
            P.op("DVE", None, (), lambda e, SLF=SLF, c=c: e.tensor_copy(
                sl32[:, c:c + 1], SLF))
            P.op("DVE", None, (), lambda e, SLF=SLF, c=c: e.tensor_copy(
                sl16[:, c:c + 1], SLF))
            P.op("DVE", None, (), lambda e, c=c, k=k: e.tensor_scalar(
                d32[:, k:k + 1], sl32[:, c:c + 1], 4, None,
                ALU.arith_shift_right, ALU.bypass))
            P.op("DVE", None, (), lambda e, DFL=DFL, k=k: e.tensor_copy(
                DFL, d32[:, k:k + 1]))
            P.op("DVE", None, (), lambda e, MOD=MOD, DFL=DFL, SLF=SLF: e.scalar_tensor_tensor(
                MOD, DFL, -16.0, SLF, ALU.mult, ALU.add))
            P.op("DVE", None, (), lambda e, PKF=PKF, MOD=MOD, DFL=DFL: e.scalar_tensor_tensor(
                PKF, MOD, 520.0, DFL, ALU.mult, ALU.add))
            P.op("DVE", (tag, "idx", t) if k == 1 else None, (),
                 lambda e, PKF=PKF, c=c: e.tensor_copy(pk32[:, c:c + 1], PKF))
    P.run()


def _emit_scatters(ctx, tag, imap_d, gate_d, cmbi_d, B):
    cs = ctx.cs
    sl32, pk32, sl16, gcol, zi16 = B["sl32"], B["pk32"], B["sl16"], B["gcol"], B["zi16"]
    zf32 = B["zf32"]
    P = Plan(ctx, tag + "F2")
    P.op("DVE", (tag, "zi"), ["fence"], lambda e: e.memset(zi16[:], 0))
    P.op("DVE", (tag, "zf"), (), lambda e: e.memset(zf32[:], 0.0))
    P.dma("POOL", (tag, "imz"), [(tag, "zi")],
          lambda e: e.dma_start(imap_d[:].rearrange("(p f) -> p f", p=128),
                                zi16[:, 0:65]),
          stream="imz", pool=1)
    P.dma("POOL", (tag, "gz"), [(tag, "zf")],
          lambda e: e.dma_start(gate_d[:].rearrange("(p f) -> p f", p=128),
                                zf32[:]),
          stream="gz", pool=1)
    for t in range(NT):
        for k in range(2):
            c = 2 * t + k
            P.dma("POOL", (tag, "sctk", t, k), [(tag, "idx", t), (tag, "imz")],
                  lambda e, c=c, t=t: e.indirect_dma_start(
                      imap_d[:].rearrange("(r x) -> r x", x=1),
                      bass.IndirectOffsetOnAxis(pk32[:, c:c + 1], 0),
                      cs["iotok"][:, t:t + 1], None),
                  stream="sctk", idx=c, pool=4)
            P.dma("POOL", (tag, "scg", t, k), [(tag, "idx", t), (tag, "gz")],
                  lambda e, c=c: e.indirect_dma_start(
                      gate_d[:].rearrange("(r x) -> r x", x=1),
                      bass.IndirectOffsetOnAxis(sl32[:, c:c + 1], 0),
                      gcol[:, c:c + 1], None),
                  stream="scg", idx=c, pool=4)
            P.dma("SP", (tag, "cmbist", t, k), [(tag, "idx", t)],
                  lambda e, c=c, t=t, k=k: e.dma_start(
                      cmbi_d[k, t].rearrange("(p x) -> p x", x=1), sl16[:, c:c + 1]),
                  stream="cmbi", idx=c, pool=4)
    P.run()


def _emit_experts(ctx, tag, W, xhi_d, imap_d, gate_d, eo_d, first):
    nc = ctx.nc
    HC = 512
    halves = [(0, HSPLIT), (HSPLIT, NHT)]
    with contextlib.ExitStack() as p3:
        sb3 = lambda n, s, d: p3.enter_context(nc.sbuf_tensor(f"{tag}3_{n}", s, d))
        ps3 = lambda n, s: p3.enter_context(nc.psum_tensor(f"{tag}3_{n}", s, F32))
        hidT = sb3("hidT", [128, 22 * 1024], BF16)
        wores = sb3("wores", [128, 22 * 1024], BF16)
        eoacc = sb3("eoacc", [128, 8 * 1024], F32)
        eib = sb3("eib", [128, 2 * 16 * 512], BF16)
        wixb = sb3("wixb", [128, 2 * 32], mybir.dt.int16)
        glb = sb3("glb", [128, 16], F32)
        w0b = sb3("w0b", [128, 2 * 8 * HC], BF16)
        w1b = sb3("w1b", [128, 2 * 8 * HC], BF16)
        gb = sb3("gb", [128, 2 * 1024], F32)
        eosb = sb3("eosb", [128, 2 * 1024], BF16)
        ph0 = [ps3(f"ph0{i}", [128, 1024]) for i in range(2)]
        ph1 = [ps3(f"ph1{i}", [128, 1024]) for i in range(2)]

        wmmC_key = {}
        g = 0
        for ex_ in range(8):
            for hh_, (ht0_, ht1_) in enumerate(halves):
                hbase_ = 128 * ht0_
                nchunk_ = (128 * ht1_ - hbase_ + HC - 1) // HC
                for lc_ in range(nchunk_):
                    last_ht = min(ht1_ - 1, (hbase_ + (lc_ + 1) * HC - 1) // 128)
                    wmmC_key[g] = (tag, "wmmht", ex_, last_ht)
                    g += 1

        P = Plan(ctx, tag + "F3")
        P.op("DVE", (tag, "wixz"), ["fence"], lambda e: e.memset(wixb[:], 0))
        gw = [0]
        prev_tg = [None]
        hid_hist = []
        SCTKF = [(tag, "sctk", NT - 1, 0), (tag, "sctk", NT - 1, 1),
                 (tag, "sctk", NT - 2, 0), (tag, "sctk", NT - 2, 1)]
        SCGF = [(tag, "scg", NT - 1, 0), (tag, "scg", NT - 1, 1),
                (tag, "scg", NT - 2, 0), (tag, "scg", NT - 2, 1)]
        CMBF = [(tag, "cmbist", NT - 1, 0), (tag, "cmbist", NT - 1, 1),
                (tag, "cmbist", NT - 2, 0), (tag, "cmbist", NT - 2, 1)]
        STHIF = [(tag, "sthi", NT - 1), (tag, "sthi", NT - 2), (tag, "sthi", NT - 3)]
        for ex in range(8):
            deps = list(SCGF)
            if ex >= 2:
                deps.append((tag, "eoev", ex - 2, 7, 1))
            for ct_ in range(8):
                P.dma("SP", (tag, "gld", ex) if ct_ == 7 else None, deps if ct_ == 0 else [],
                      lambda e, ex=ex, ct_=ct_: e.dma_start(
                          glb[:, 8 * (ex % 2) + ct_:8 * (ex % 2) + ct_ + 1],
                          gate_d[1024 * ex + 128 * ct_:1024 * ex + 128 * ct_ + 128]
                          .rearrange("(p x) -> p x", x=1)),
                      stream="gld", idx=0, pool=1)
            for wi in range(2):
                w = 2 * ex + wi
                for b2 in range(2):
                    deps = []
                    if w >= 2:
                        deps.append((tag, "eild", ex - 1, wi))
                    if w == 0 and b2 == 0:
                        deps += SCTKF + CMBF
                    if w <= 1:
                        deps.append((tag, "wixz"))
                    P.dma("SP", (tag, "wixld", w, b2), deps,
                          lambda e, w=w, b2=b2: e.dma_start(
                              wixb[16 * b2:16 * b2 + 16, 32 * (w % 2):32 * (w % 2) + 32],
                              imap_d[:].rearrange("(q j) -> q j", q=16)[:, 32 * w:32 * w + 32]),
                          stream="wix", idx=w, pool=2)
                deps = [(tag, "wixld", w, 0), (tag, "wixld", w, 1), (tag, "wixz")]
                if ex > 1:
                    deps.append((tag, "wmmht", ex - 2, NHT - 1))
                else:
                    deps += STHIF
                P.dma("POOL", (tag, "eild", ex, wi), deps,
                      lambda e, wi=wi, w=w, ex=ex: e.dma_gather(
                          eib[:, (ex % 2) * 8192 + wi * 4096:
                                 (ex % 2) * 8192 + wi * 4096 + 4096].rearrange(
                              "p (a f) -> p a f", a=8),
                          xhi_d[:], wixb[:, 32 * (w % 2):32 * (w % 2) + 32],
                          512, ctx.r512, M, transpose=True),
                      stream="eild", idx=w, pool=2)
            for hh, (ht0, ht1) in enumerate(halves):
                hbase = 128 * ht0
                nchunk = (128 * ht1 - hbase + HC - 1) // HC
                for lc in range(nchunk):
                    cw = min(HC, H - hbase - HC * lc)
                    g = gw[0]
                    b = g % 2
                    h0 = hbase + HC * lc
                    deps = []
                    if g >= 2:
                        deps.append(wmmC_key[g - 2])
                        if first:
                            deps += [(tag, "w0t", g - 2), (tag, "w1t", g - 2)]
                    if g < 2:
                        deps += [(tag, "cmbist", NT - 1, 1), (tag, "imz")]
                    if first:
                        P.dma("POOL", (tag, "w0", ex, hh, lc), deps,
                              lambda e, b=b, cw=cw, ex=ex, h0=h0: e.dma_start(
                                  w0b[:, b * 8 * HC:b * 8 * HC + 8 * cw].rearrange("p (c h) -> p c h", c=8),
                                  W["w0"][ex].rearrange("(c p) h -> p c h", p=128)[:, :, h0:h0 + cw]),
                              stream="w01", idx=g, pool=3)
                        P.dma("POOL", (tag, "w1", ex, hh, lc), (),
                              lambda e, b=b, cw=cw, ex=ex, h0=h0: e.dma_start(
                                  w1b[:, b * 8 * HC:b * 8 * HC + 8 * cw].rearrange("p (c h) -> p c h", c=8),
                                  W["w1"][ex].rearrange("(c p) h -> p c h", p=128)[:, :, h0:h0 + cw]),
                              stream="w01", idx=g, pool=3)
                        P.dma("SP", (tag, "w0t", g), [(tag, "w0", ex, hh, lc)],
                              lambda e, b=b, cw=cw, ex=ex, h0=h0: e.dma_start(
                                  W["w0c"][ex][:, :, h0:h0 + cw],
                                  w0b[:, b * 8 * HC:b * 8 * HC + 8 * cw].rearrange("p (c h) -> p c h", c=8)),
                              stream="latld", idx=g, pool=8)
                        P.dma("SP", (tag, "w1t", g), [(tag, "w1", ex, hh, lc)],
                              lambda e, b=b, cw=cw, ex=ex, h0=h0: e.dma_start(
                                  W["w1c"][ex][:, :, h0:h0 + cw],
                                  w1b[:, b * 8 * HC:b * 8 * HC + 8 * cw].rearrange("p (c h) -> p c h", c=8)),
                              stream="latld", idx=g, pool=8)
                    else:
                        P.dma("POOL", (tag, "w0", ex, hh, lc), deps,
                              lambda e, b=b, cw=cw, ex=ex, h0=h0: e.dma_start(
                                  w0b[:, b * 8 * HC:b * 8 * HC + 8 * cw].rearrange("p (c h) -> p c h", c=8),
                                  W["w0c"][ex][:, :, h0:h0 + cw]),
                              stream="w01", idx=g, pool=3)
                        if g < 2:
                            deps = list(deps) + [(tag, "gz")] + SCTKF + SCGF
                        P.dma("SP", (tag, "w1", ex, hh, lc), list(deps),
                              lambda e, b=b, cw=cw, ex=ex, h0=h0: e.dma_start(
                                  w1b[:, b * 8 * HC:b * 8 * HC + 8 * cw].rearrange("p (c h) -> p c h", c=8),
                                  W["w1c"][ex][:, :, h0:h0 + cw]),
                              stream="stlo", idx=g, pool=3)
                    gw[0] += 1
                for ht in range(ht0, ht1):
                    hrows = min(128, H - 128 * ht)
                    deps = []
                    if (ex, hh) > (0, 0):
                        pex, phh = (ex, hh - 1) if hh == 1 else (ex - 1, 1)
                        deps.append((tag, "odone", pex, phh))
                        if first:
                            pht0 = 0 if phh == 0 else HSPLIT
                            psz = HSPLIT if phh == 0 else NHT - HSPLIT
                            pos = ht - ht0
                            if pos < psz:
                                deps.append((tag, "wot", pex, pht0 + pos))
                    else:
                        deps += [(tag, "cmbist", NT - 1, 1), (tag, "imz")]
                    if first:
                        P.dma("POOL", (tag, "wo", ex, ht), deps,
                              lambda e, hrows=hrows, ex=ex, ht=ht, ht0=ht0: e.dma_start(
                                  wores[0:hrows, 1024 * (ht - ht0):1024 * (ht - ht0) + 1024],
                                  W["wo"][ex][128 * ht:128 * ht + hrows, :]),
                              stream="woh", idx=2 * ex + hh, pool=2)
                        P.dma("SP", (tag, "wot", ex, ht), [(tag, "wo", ex, ht)],
                              lambda e, hrows=hrows, ex=ex, ht=ht, ht0=ht0: e.dma_start(
                                  W["woc"][ex][128 * ht:128 * ht + hrows, :],
                                  wores[0:hrows, 1024 * (ht - ht0):1024 * (ht - ht0) + 1024]),
                              stream="cix", idx=ht, pool=4)
                    else:
                        eng = "SP" if ht % 2 == 0 else "ACT"
                        if (ex, hh) == (0, 0):
                            deps = list(deps) + [(tag, "gz")] + SCTKF + SCGF
                        P.dma(eng, (tag, "wo", ex, ht), deps,
                              lambda e, hrows=hrows, ex=ex, ht=ht, ht0=ht0: e.dma_start(
                                  wores[0:hrows, 1024 * (ht - ht0):1024 * (ht - ht0) + 1024],
                                  W["woc"][ex][128 * ht:128 * ht + hrows, :]),
                              stream="sctk" if ht % 2 == 0 else "scg", idx=ht, pool=4)
                for ht in range(ht0, ht1):
                    lc = (128 * ht - hbase) // HC
                    cw = min(HC, H - hbase - HC * lc)
                    off = 128 * ht - hbase - HC * lc
                    hrows = min(128, H - 128 * ht)
                    b = (gw[0] - nchunk + lc) % 2
                    pb = ht % 2
                    for half in range(2):
                        for mt in range(8):
                            deps = []
                            if mt == 0:
                                deps = [(tag, "w0", ex, hh, lc), (tag, "w1", ex, hh, lc),
                                        (tag, "eild", ex, 0), (tag, "eild", ex, 1)]
                                gi = ex * NHT + ht
                                if gi >= 2 and half == 0:
                                    pht = ht - 2
                                    pe_ = ex
                                    if pht < 0:
                                        pe_ -= 1
                                        pht += NHT
                                    deps.append((tag, "hid", pe_, pht))
                                if ht - ht0 < 2 and half == 0 and (ex, hh) > (0, 0):
                                    pex, phh = (ex, 0) if hh == 1 else (ex - 1, 1)
                                    for ctp in (6, 7):
                                        for mwp in (0, 1):
                                            deps.append((tag, "eoh", pex, phh, ctp, mwp))
                            P.op("PE", None, deps,
                                 lambda e, pb=pb, hrows=hrows, half=half, b=b, mt=mt, cw=cw,
                                        off=off, ex=ex: e.matmul(
                                     ph0[pb][0:hrows, 512 * half:512 * half + 512],
                                     w0b[:, b * 8 * HC + mt * cw + off:
                                            b * 8 * HC + mt * cw + off + hrows],
                                     eib[:, (ex % 2) * 8192 + half * 4096 + 512 * mt:
                                            (ex % 2) * 8192 + half * 4096 + 512 * mt + 512],
                                     start=(mt == 0), stop=(mt == 7)))
                        for mt in range(8):
                            key = (tag, "wmmht", ex, ht) if (half == 1 and mt == 7) else None
                            P.op("PE", key, (),
                                 lambda e, pb=pb, hrows=hrows, half=half, b=b, mt=mt, cw=cw,
                                        off=off, ex=ex: e.matmul(
                                     ph1[pb][0:hrows, 512 * half:512 * half + 512],
                                     w1b[:, b * 8 * HC + mt * cw + off:
                                            b * 8 * HC + mt * cw + off + hrows],
                                     eib[:, (ex % 2) * 8192 + half * 4096 + 512 * mt:
                                            (ex % 2) * 8192 + half * 4096 + 512 * mt + 512],
                                     start=(mt == 0), stop=(mt == 7)))
                    if ht == ht1 - 1:
                        ctxkey = (tag, "wmmC", gw[0] - nchunk + lc)
                    SQ = gb[0:hrows, 0:1024]
                    G = gb[0:hrows, 1024:2048]
                    HD = hidT[0:hrows, 1024 * (ht - ht0):1024 * (ht - ht0) + 1024]
                    deps_sq = [(tag, "wmmht", ex, ht)]
                    if prev_tg[0] is not None:
                        deps_sq.append(prev_tg[0])
                    P.op("ACT", (tag, "sq", ex, ht), deps_sq,
                         lambda e, SQ=SQ, pb=pb, hrows=hrows: e.activation(
                             SQ, ph0[pb][0:hrows, :], AF.Square, bias=0.0, scale=GELU_SA))
                    P.op("DVE", (tag, "u2", ex, ht), [(tag, "sq", ex, ht)],
                         lambda e, SQ=SQ, pb=pb, hrows=hrows: e.scalar_tensor_tensor(
                             SQ, SQ, 1.0, ph0[pb][0:hrows, :], ALU.add, ALU.mult))
                    P.op("ACT", (tag, "sg", ex, ht), [(tag, "u2", ex, ht)],
                         lambda e, SQ=SQ: e.activation(SQ, SQ, AF.Sigmoid,
                                                       bias=0.0, scale=GELU_K))
                    P.op("DVE", (tag, "tg", ex, ht), [(tag, "sg", ex, ht)],
                         lambda e, G=G, SQ=SQ, pb=pb, hrows=hrows: e.tensor_tensor(
                             G, SQ, ph0[pb][0:hrows, :], ALU.mult))
                    prev_tg[0] = (tag, "tg", ex, ht)
                    deps = []
                    if (ex, hh) > (0, 0):
                        pex, phh = (ex, 0) if hh == 1 else (ex - 1, 1)
                        deps.append((tag, "odone", pex, phh))
                    P.op("DVE", (tag, "hid", ex, ht), deps,
                         lambda e, G=G, pb=pb, hrows=hrows, HD=HD: e.tensor_tensor(
                             HD, G, ph1[pb][0:hrows, :], ALU.mult))
                for ct in range(8):
                    pb = ct % 2
                    for mw in range(2):
                        for ht in range(ht0, ht1):
                            hrows = min(128, H - 128 * ht)
                            deps = []
                            if ht == ht0:
                                deps = [(tag, "wo", ex, ht1 - 1), (tag, "wo", ex, ht1 - 2),
                                        (tag, "hid", ex, ht1 - 1), (tag, "hid", ex, ht1 - 2)]
                                if ct >= 2:
                                    deps += [(tag, "eoh", ex, hh, ct - 2, 0),
                                             (tag, "eoh", ex, hh, ct - 2, 1)]
                            key = (tag, "omm", ex, hh, ct, mw) if ht == ht1 - 1 else None
                            P.op("PE", key, deps,
                                 lambda e, pb=pb, mw=mw, hrows=hrows, ht=ht, ct=ct,
                                        ht0=ht0: e.matmul(
                                     ph0[pb][0:128, 512 * mw:512 * mw + 512],
                                     hidT[0:hrows, 1024 * (ht - ht0) + 128 * ct:
                                              1024 * (ht - ht0) + 128 * ct + 128],
                                     wores[0:hrows, 1024 * (ht - ht0) + 512 * mw:
                                               1024 * (ht - ht0) + 512 * mw + 512],
                                     start=(ht == ht0), stop=(ht == ht1 - 1),
                                     skip_group_check=True))
                        EA = eoacc[:, 1024 * ct + 512 * mw:1024 * ct + 512 * mw + 512]
                        if hh == 0:
                            P.op("ACT", (tag, "eoh", ex, 0, ct, mw),
                                 [(tag, "omm", ex, 0, ct, 1)],
                                 lambda e, EA=EA, pb=pb, mw=mw: e.copy(
                                     EA, ph0[pb][0:128, 512 * mw:512 * mw + 512]))
                        else:
                            P.op("DVE", (tag, "eoh", ex, 1, ct, mw),
                                 [(tag, "omm", ex, 1, ct, 1), (tag, "eoh", ex, 0, ct, mw)],
                                 lambda e, EA=EA, pb=pb, mw=mw: e.tensor_tensor(
                                     EA, EA, ph0[pb][0:128, 512 * mw:512 * mw + 512], ALU.add))
                            deps = [(tag, "eoh", ex, 1, ct, mw), (tag, "gld", ex)]
                            if ct >= 2:
                                deps.append((tag, "eost", ex, ct - 2))
                            elif ex > 0:
                                deps.append((tag, "eost", ex - 1, ct + 6))
                            P.op("ACT", (tag, "eoev", ex, ct, mw), deps,
                                 lambda e, ct=ct, mw=mw, ex=ex, EA=EA: e.activation(
                                     eosb[:, (ct % 2) * 1024 + 512 * mw:
                                              (ct % 2) * 1024 + 512 * mw + 512],
                                     EA, AF.Copy, bias=0.0,
                                     scale=glb[:, 8 * (ex % 2) + ct:8 * (ex % 2) + ct + 1]))
                    if hh == 1:
                        P.dma("SP", (tag, "eost", ex, ct),
                              [(tag, "eoev", ex, ct, 0), (tag, "eoev", ex, ct, 1)],
                              lambda e, ex=ex, ct=ct: e.dma_start(
                                  eo_d[1024 * ex + 128 * ct:1024 * ex + 128 * ct + 128, :],
                                  eosb[:, (ct % 2) * 1024:(ct % 2) * 1024 + 1024]),
                              stream="eost", idx=ex * 8 + ct, pool=3)
                P.op("PE", (tag, "odone", ex, hh), [(tag, "omm", ex, hh, 7, 1)],
                     lambda e: e.nop(), ser=False)
        P.run()


def _emit_combine(ctx, tag, eo_d, cmbi_d, x_src, x_dst):
    nc = ctx.nc
    cs = ctx.cs
    with contextlib.ExitStack() as p4:
        sb4 = lambda n, s, d: p4.enter_context(nc.sbuf_tensor(f"{tag}4_{n}", s, d))
        cixb = sb4("cixb", [128, 4 * 8], mybir.dt.int16)
        gbuf = sb4("gbuf", [128, 4 * M], BF16)
        resb = sb4("resb", [128, 2 * M], F32)
        outb = sb4("outb", [128, 2 * M], F32)
        zb16 = sb4("zb16", [128, M], BF16)
        pcb = [p4.enter_context(nc.psum_tensor(f"{tag}4_pcb{i}", [128, M], F32))
               for i in range(2)]
        P = Plan(ctx, tag + "F4")
        eoF = [(tag, "eost", 7, 7), (tag, "eost", 7, 6), (tag, "eost", 7, 5)]
        P.op("DVE", (tag, "zb16"), ["fence"], lambda e: e.memset(zb16[:], 0.0))
        P.op("DVE", (tag, "cixz"), (), lambda e: e.memset(cixb[:], 0))
        P.dma("SP", (tag, "eoz"), [(tag, "zb16")],
              lambda e: e.dma_start(eo_d[8192:8256, :], zb16[0:64, :]),
              stream="eoz", pool=1)
        for t in range(NT):
            b = t % 2
            for k in range(2):
                slot = 2 * b + k
                CMBF = [(tag, "cmbist", NT - 1, 1), (tag, "cmbist", NT - 1, 0),
                        (tag, "cmbist", NT - 2, 1), (tag, "cmbist", NT - 2, 0)]
                for b2 in range(2):
                    deps = list(CMBF)
                    if t < 2:
                        deps.append((tag, "cixz"))
                    else:
                        deps.append((tag, "cgl", t - 2, k))
                    P.dma("SP", (tag, "cixld", t, k, b2), deps,
                          lambda e, t=t, k=k, b2=b2, slot=slot: e.dma_start(
                              cixb[16 * b2:16 * b2 + 16, 8 * slot:8 * slot + 8],
                              cmbi_d[k, t].rearrange("(q c) -> q c", q=16)),
                          stream="cix", idx=2 * t + k, pool=4)
                deps = [(tag, "cixld", t, k, 0), (tag, "cixld", t, k, 1), (tag, "cixz")]
                if t < 2:
                    deps += eoF + [(tag, "eoz")]
                else:
                    deps.append((tag, "pmm", t - 2, 1))
                P.dma("POOL", (tag, "cgl", t, k), deps,
                      lambda e, slot=slot: e.dma_gather(
                          gbuf[:].rearrange("p (s f) -> p s f", s=4)[:, slot:slot + 1, :],
                          eo_d[:], cixb[:, 8 * slot:8 * slot + 8],
                          128, ctx.r128, M),
                      stream="cgl", idx=2 * t + k, pool=4)
            RS = resb[:, M * b:M * b + M]
            OB = outb[:, M * b:M * b + M]
            deps = [(tag, "outst", t - 2)] if t >= 2 else ["fence"]
            P.dma("SP", (tag, "res", t), deps,
                  lambda e, RS=RS, t=t: e.dma_start(RS, x_src[128 * t:128 * t + 128, :]),
                  stream="res", idx=t, pool=3)
            for mw in range(2):
                for k in range(2):
                    deps = []
                    if mw == 0 and k == 0:
                        deps = [(tag, "cgl", t, 0), (tag, "cgl", t, 1)]
                        if t >= 2:
                            deps.append((tag, "outev", t - 2))
                    P.op("PE", (tag, "pmm", t, mw) if k == 1 else None, deps,
                         lambda e, t=t, b=b, mw=mw, k=k: e.matmul(
                             pcb[t % 2][:, 512 * mw:512 * mw + 512],
                             cs["sigP"][:],
                             gbuf[:, M * (2 * b + k) + 512 * mw:
                                     M * (2 * b + k) + 512 * mw + 512],
                             start=(k == 0), stop=(k == 1)))
            P.op("DVE", (tag, "outev", t), [(tag, "pmm", t, 0), (tag, "pmm", t, 1),
                                            (tag, "res", t)],
                 lambda e, OB=OB, RS=RS, t=t: e.tensor_tensor(
                     OB, pcb[t % 2][:], RS, ALU.add))
            P.dma("SP", (tag, "outst", t), [(tag, "outev", t)],
                  lambda e, OB=OB, t=t: e.dma_start(x_dst[128 * t:128 * t + 128, :], OB),
                  stream="outst", idx=t, pool=4)
        P.run()


def attn_perm():
    perm = np.zeros(M, dtype=np.int64)
    for h in range(NHEAD):
        for j in range(32):
            perm[64 * h + j] = 64 * h + 2 * j
            perm[64 * h + 32 + j] = 64 * h + 2 * j + 1
    return perm




def _emit_rope2(P, tag, key, i, prev, psrc, dst, tabC, tabS, rtmp, rps, sw0, extra0=()):
    pre = P._last.get("DVE")
    base = [prev] + list(extra0)
    if pre is not None:
        base.append(("_idx_", pre))
    CO = tabC[:, 512 * sw0:512 * sw0 + 1024]
    SI = tabS[:, 512 * sw0:512 * sw0 + 1024]
    P.op("DVE", None, base,
         lambda e: e.tensor_tensor(rps[:], psrc[:], CO, ALU.mult))
    P.op("DVE", None, (),
         lambda e: e.tensor_tensor(rtmp[:], psrc[:], SI, ALU.mult))
    for B in (0, 64):
        P.op("DVE", None, (), lambda e, B=B: e.tensor_tensor(
            dst[B:B + 32, :], rps[B:B + 32, :], rtmp[B + 32:B + 64, :], ALU.subtract))
        P.op("DVE", key if B == 64 else None, (), lambda e, B=B: e.tensor_tensor(
            dst[B + 32:B + 64, :], rtmp[B:B + 32, :], rps[B + 32:B + 64, :], ALU.add))


def emit_attention(ctx, tag, x_src, x_dst, W):
    nc = ctx.nc
    cs = ctx.cs
    xln_d = nc.dram_tensor(f"{tag}_xln", [S, M], BF16, kind="Internal")
    qT_d = nc.dram_tensor(f"{tag}_qT", [8, 128, S], BF16, kind="Internal")
    latT_d = nc.dram_tensor(f"{tag}_latT", [4, 128, S], BF16, kind="Internal")
    ropes = make_rope_consts()
    cosF_d = nc.dram_tensor(f"{tag}_cosF", [128, S], BF16, kind="ExternalInput")
    sinF_d = nc.dram_tensor(f"{tag}_sinF", [128, S], BF16, kind="ExternalInput")
    ctx.rope_inputs = {
        f"{tag}_cosF": ropes["ropeA"].astype(ml_dtypes.bfloat16),
        f"{tag}_sinF": ropes["ropeB"].astype(ml_dtypes.bfloat16)}

    rope_scope = contextlib.ExitStack()
    sbr = lambda n, s, d: rope_scope.enter_context(
        nc.sbuf_tensor(f"{tag}_{n}", s, d, side="right"))
    cosF = sbr("cosFs", [128, S], BF16)
    sinF = sbr("sinFs", [128, S], BF16)
    rtmp = sbr("rtmp", [128, 1024], F32)

    with contextlib.ExitStack() as p1:
        sb1 = lambda n, s, d: p1.enter_context(nc.sbuf_tensor(f"{tag}1_{n}", s, d))
        xt = sb1("xt", [128, 2 * M], F32)
        st6 = sb1("st6", [128, 2 * 16], F32)
        mvb = sb1("mvb", [128, 4 * NT], F32)
        xlnb = sb1("xlnb", [128, 2 * M], BF16)
        P = Plan(ctx, tag + "A1")
        P.dma("SP", (tag, "cosF"), ["fence"], lambda e: e.dma_start(cosF[:], cosF_d[:]),
              stream="rope", pool=1)
        P.dma("SP", (tag, "sinF"), ["fence"], lambda e: e.dma_start(sinF[:], sinF_d[:]),
              stream="rope", pool=1)
        for t in range(NT):
            b = t % 2
            X = xt[:, M * b:M * b + M]
            S6 = st6[:, 16 * b:16 * b + 12]
            MV2 = st6[:, 16 * b + 12:16 * b + 14]
            VV = st6[:, 16 * b + 13:16 * b + 14]
            XL = xlnb[:, M * b:M * b + M]
            deps = ["fence"] + ([(tag, "xlnf", t - 2)] if t >= 2 else [])
            P.dma("SP", (tag, "xt", t), deps,
                  lambda e, X=X, t=t: e.dma_start(X, x_src[128 * t:128 * t + 128, :]),
                  stream="xt", idx=t, pool=3)
            deps = [(tag, "xt", t)] + ([(tag, "rstd", t - 2)] if t >= 2 else [])
            P.op("DVE", (tag, "bns", t), deps,
                 lambda e, S6=S6, X=X: e.bn_stats(S6[:, 0:6], X[:, 0:512]))
            P.op("DVE", None, (), lambda e, S6=S6, X=X: e.bn_stats(S6[:, 6:12], X[:, 512:1024]))
            P.op("DVE", (tag, "bn", t), (), lambda e, MV2=MV2, S6=S6: e.bn_aggr(MV2, S6))
            P.op("DVE", (tag, "mn", t), (),
                 lambda e, t=t, MV2=MV2: e.tensor_copy(mvb[:, 4 * t:4 * t + 1], MV2[:, 0:1]))
            P.op("ACT", None, [(tag, "bn", t)],
                 lambda e, VV=VV: e.activation(VV, VV, AF.Ln, bias=cs['eps5'][:, 0:1], scale=1.0))
            P.op("ACT", (tag, "rstd", t), (),
                 lambda e, t=t, VV=VV: e.activation(mvb[:, 4 * t + 1:4 * t + 2], VV, AF.Exp,
                                                    bias=0.0, scale=-0.5))
            deps = [(tag, "rstd", t)]
            if t >= 2:
                deps.append((tag, "stln", t - 2))
            P.op("DVE", (tag, "xlnf", t), deps,
                 lambda e, XL=XL, X=X, t=t: e.tensor_scalar(
                     XL, X, mvb[:, 4 * t:4 * t + 1], mvb[:, 4 * t + 1:4 * t + 2],
                     ALU.subtract, ALU.mult))
            P.dma("SP", (tag, "stln", t), [(tag, "xlnf", t)],
                  lambda e, XL=XL, t=t: e.dma_start(xln_d[128 * t:128 * t + 128, :], XL),
                  stream="sthi", idx=t, pool=3)
        P.run()

    with contextlib.ExitStack() as p2:
        sb2 = lambda n, s, d: p2.enter_context(nc.sbuf_tensor(f"{tag}2_{n}", s, d))
        ps2 = lambda n, s: p2.enter_context(nc.psum_tensor(f"{tag}2_{n}", s, F32))
        x2T = sb2("x2T", [128, 8 * S], BF16)
        wqb = sb2("wqb", [128, 8 * 1024], BF16)
        wdb = sb2("wdb", [128, 8 * 512], BF16)
        qsb = sb2("qsb", [128, 2 * 1024], BF16)
        latsb = sb2("latsb", [128, 2 * 1024], BF16)
        pq = [ps2(f"pq{i}", [128, 1024]) for i in range(2)]
        pl = ps2("pl", [128, 1024])
        rps2a = ps2("rps", [128, 1024])
        P = Plan(ctx, tag + "A2a")
        P.dma("POOL", (tag, "wqb"), [(tag, "stln", NT - 1)],
              lambda e: e.dma_start(wqb[:].rearrange("p (c f) -> p c f", c=8),
                                    W["wq"].rearrange("(c p) f -> p c f", p=128)),
              stream="watt", pool=1)
        P.dma("POOL", (tag, "wdb"), (),
              lambda e: e.dma_start(wdb[:].rearrange("p (c l) -> p c l", c=8),
                                    W["w_down"].rearrange("(c p) l -> p c l", p=128)),
              stream="watt", pool=1)
        for c in range(8):
            deps = [(tag, "stln", NT - k) for k in (1, 2, 3)]
            P.dma("SP", (tag, "x2T", c), deps,
                  lambda e, c=c: e.dma_start(x2T[:, S * c:S * c + S],
                                             xln_d[:, 128 * c:128 * c + 128], transpose=True),
                  stream="ldT", idx=c, pool=3)
        WQDEP = [(tag, "wqb"), (tag, "wdb")]
        for ft in range(8):
            for swp in range(4):
                i = ft * 4 + swp
                pb = i % 2
                for wi in range(2):
                    sw = 2 * swp + wi
                    for c in range(8):
                        deps = []
                        if c == 0 and wi == 0:
                            deps = WQDEP + [(tag, "x2T", cc) for cc in range(8)]
                            if i >= 2:
                                deps.append((tag, "rope", i - 2))
                        P.op("PE", (tag, "qmm", i) if (c == 7 and wi == 1) else None, deps,
                             lambda e, pb=pb, c=c, ft=ft, sw=sw, wi=wi: e.matmul(
                                 pq[pb][:, 512 * wi:512 * wi + 512],
                                 wqb[:, 1024 * c + 128 * ft:1024 * c + 128 * ft + 128],
                                 x2T[:, S * c + 512 * sw:S * c + 512 * sw + 512],
                                 start=(c == 0), stop=(c == 7)))
                QO = qsb[:, 1024 * pb:1024 * pb + 1024]
                extra0 = [(tag, "cosF"), (tag, "sinF")]
                if i >= 2:
                    extra0.append((tag, "qst", i - 2))
                _emit_rope2(P, tag, (tag, "rope", i), i, (tag, "qmm", i),
                            pq[pb], QO, cosF, sinF, rtmp, rps2a, 2 * swp, extra0=extra0)
                P.dma("SP", (tag, "qst", i), [(tag, "rope", i)],
                      lambda e, QO=QO, ft=ft, swp=swp: e.dma_start(
                          qT_d[ft][:, 1024 * swp:1024 * swp + 1024], QO),
                      stream="qst", idx=i, pool=3)
        for lt in range(4):
            for swp in range(4):
                i = lt * 4 + swp
                for wi in range(2):
                    sw = 2 * swp + wi
                    for c in range(8):
                        deps = []
                        if c == 0 and wi == 0:
                            deps = WQDEP + [(tag, "x2T", cc) for cc in range(8)]
                            if i >= 1:
                                deps.append((tag, "latev", i - 1))
                        P.op("PE", (tag, "lmm", i) if (c == 7 and wi == 1) else None, deps,
                             lambda e, c=c, lt=lt, sw=sw, wi=wi: e.matmul(
                                 pl[:, 512 * wi:512 * wi + 512],
                                 wdb[:, 512 * c + 128 * lt:512 * c + 128 * lt + 128],
                                 x2T[:, S * c + 512 * sw:S * c + 512 * sw + 512],
                                 start=(c == 0), stop=(c == 7)))
                LO = latsb[:, 1024 * (i % 2):1024 * (i % 2) + 1024]
                deps = [(tag, "lmm", i)]
                if i >= 2:
                    deps.append((tag, "latst", i - 2))
                P.op("ACT", (tag, "latev", i), deps,
                     lambda e, LO=LO: e.copy(LO, pl[:]))
                P.dma("SP", (tag, "latst", i), [(tag, "latev", i)],
                      lambda e, LO=LO, lt=lt, swp=swp: e.dma_start(
                          latT_d[lt][:, 1024 * swp:1024 * swp + 1024], LO),
                      stream="latst", idx=i, pool=3)
        P.run()

    with contextlib.ExitStack() as p34:
        sb34 = lambda n, s, d: p34.enter_context(nc.sbuf_tensor(f"{tag}34_{n}", s, d))
        kTs = sb34("kTs", [128, 8 * S], BF16)
        vab = sb34("vab", [128, NT * 1040], BF16)
        with contextlib.ExitStack() as p3:
            sb3 = lambda n, s, d: p3.enter_context(nc.sbuf_tensor(f"{tag}3_{n}", s, d))
            ps3 = lambda n, s: p3.enter_context(nc.psum_tensor(f"{tag}3_{n}", s, F32))
            latw = sb3("latw", [128, 2 * 4 * 1024], BF16)
            wkb = sb3("wkb", [128, 4 * 1024], BF16)
            wvb = sb3("wvb", [128, 4 * 1024], BF16)
            psk = [ps3(f"psk{i}", [128, 1024]) for i in range(2)]
            psv = [ps3(f"psv{i}", [128, 512]) for i in range(2)]
            rps2b = ps3("rps", [128, 1024])
            qstF = [(tag, "qst", 32 - k) for k in (1, 2, 3)]
            latF = [(tag, "latst", 16 - k) for k in (1, 2, 3)]
            P = Plan(ctx, tag + "A2b")
            P.dma("POOL", (tag, "wkb"), qstF + latF,
                  lambda e: e.dma_start(wkb[:].rearrange("p (c f) -> p c f", c=4),
                                        W["wk_up"].rearrange("(c p) f -> p c f", p=128)),
                  stream="watt", pool=1)
            P.dma("POOL", (tag, "wvb"), (),
                  lambda e: e.dma_start(wvb[:].rearrange("p (c f) -> p c f", c=4),
                                        W["wv_up"].rearrange("(c p) f -> p c f", p=128)),
                  stream="watt", pool=1)
            P.op("DVE", (tag, "vone"), qstF + latF, lambda e: e.memset(vab[:], 1.0))
            WKDEP = [(tag, "wkb"), (tag, "wvb")]
            for swp in range(4):
                lb = swp % 2
                for c in range(4):
                    deps = [(tag, "latst", 4 * c + swp)]
                    if swp >= 2:
                        deps.append((tag, "vmm", (8 * (swp - 2) + 7) * 2 + 1))
                    P.dma("SP", (tag, "latld", swp, c), deps,
                          lambda e, lb=lb, c=c, swp=swp: e.dma_start(
                              latw[:, 4096 * lb + 1024 * c:4096 * lb + 1024 * c + 1024],
                              latT_d[c][:, 1024 * swp:1024 * swp + 1024]),
                          stream="latld", idx=4 * swp + c, pool=8)
                for ft in range(8):
                    k_i = swp * 8 + ft
                    pb = k_i % 2
                    for wi in range(2):
                        for c in range(4):
                            deps = []
                            if c == 0 and wi == 0:
                                deps = WKDEP + [(tag, "latld", swp, cc) for cc in range(4)]
                                if k_i >= 2:
                                    deps.append((tag, "krope", k_i - 2))
                            P.op("PE", (tag, "kmm", k_i) if (c == 3 and wi == 1) else None, deps,
                                 lambda e, pb=pb, c=c, ft=ft, lb=lb, wi=wi: e.matmul(
                                     psk[pb][:, 512 * wi:512 * wi + 512],
                                     wkb[:, 1024 * c + 128 * ft:1024 * c + 128 * ft + 128],
                                     latw[:, 4096 * lb + 1024 * c + 512 * wi:
                                             4096 * lb + 1024 * c + 512 * wi + 512],
                                     start=(c == 0), stop=(c == 3)))
                    KO = kTs[:, S * ft + 1024 * swp:S * ft + 1024 * swp + 1024]
                    _emit_rope2(P, tag, (tag, "krope", k_i), k_i, (tag, "kmm", k_i),
                                psk[pb], KO, cosF, sinF, rtmp, rps2b, 2 * swp)
                for stl in range(8):
                    st = 8 * swp + stl
                    for mwin in range(2):
                        i = st * 2 + mwin
                        for c in range(4):
                            deps = []
                            if c == 0:
                                deps = WKDEP + [(tag, "vone")] +                                        [(tag, "latld", swp, cc) for cc in range(4)]
                                if i >= 2:
                                    deps.append((tag, "vev", i - 2))
                            P.op("PE", (tag, "vmm", i) if c == 3 else None, deps,
                                 lambda e, c=c, stl=stl, mwin=mwin, i=i, lb=lb: e.matmul(
                                     psv[i % 2][:],
                                     latw[:, 4096 * lb + 1024 * c + 128 * stl:
                                             4096 * lb + 1024 * c + 128 * stl + 128],
                                     wvb[:, 1024 * c + 512 * mwin:1024 * c + 512 * mwin + 512],
                                     start=(c == 0), stop=(c == 3)))
                        P.op("ACT", (tag, "vev", i), [(tag, "vmm", i)],
                             lambda e, st=st, mwin=mwin, i=i: e.copy(
                                 vab[:, 1040 * st:1040 * st + 1040].rearrange(
                                     "p (h d) -> p h d", d=65)[:, 8 * mwin:8 * mwin + 8, 0:64],
                                 psv[i % 2][:].rearrange("p (h d) -> p h d", d=64)))
            P.run()
        rope_scope.close()

        with contextlib.ExitStack() as p4:
            sb4 = lambda n, s, d: p4.enter_context(nc.sbuf_tensor(f"{tag}4_{n}", s, d))
            ps4 = lambda n, s: p4.enter_context(nc.psum_tensor(f"{tag}4_{n}", s, F32))
            wob = sb4("wob", [128, 8 * 1024], BF16)
            qtb = sb4("qtb", [128, 2 * 512], BF16)
            expb = sb4("expb", [128, 4 * 1024], BF16)
            outT = sb4("outT", [128, 8 * 512], BF16)
            rcb = sb4("rcb", [1, 1024], F32)
            avsb = sb4("avsb", [65, 2 * 1024], F32)
            bcsb = sb4("bcsb", [128, 1024], F32)
            resb = sb4("resb", [128, 2 * M], F32)
            outb = sb4("outb", [128, 2 * M], F32)
            scAB = [ps4(f"sc{i}", [128, 1024]) for i in range(2)]
            avAB = ps4("avAB", [128, 1024])
            bcps = ps4("bcps", [128, 1024])
            P = Plan(ctx, tag + "A3")
            vevF = [(tag, "vev", 2 * NT - 1)]
            P.dma("POOL", (tag, "wob"), vevF,
                  lambda e: e.dma_start(wob[:].rearrange("p (c f) -> p c f", c=8),
                                        W["wo"].rearrange("(c p) f -> p c f", p=128)),
                  stream="watt", pool=1)

            for qw in range(8):
                nkt = 4 * (qw + 1)
                for hp in range(8):
                    j = qw * 8 + hp
                    qb = j % 2
                    hA, hB = 2 * hp, 2 * hp + 1
                    QT = qtb[:, 512 * qb:512 * qb + 512]
                    deps = list(vevF) + [(tag, "qst", 4 * hp + k) for k in range(4)]
                    if j >= 2:
                        deps.append((tag, "avg", j - 2, 4 * ((j - 2) // 8 + 1) - 1))
                    P.dma("SP", (tag, "qld", j), deps,
                          lambda e, QT=QT, hp=hp, qw=qw: e.dma_start(
                              QT, qT_d[hp][:, 512 * qw:512 * qw + 512]),
                          stream="qld", idx=j, pool=3)

                    def emit_sc(kt, j=j, hp=hp, qw=qw, nkt=nkt, QT=QT):
                        SC = scAB[kt % 2]
                        deps = []
                        if kt <= 1:
                            deps = [(tag, "qld", j), (tag, "wob")]
                            if j >= 1:
                                pnkt = 4 * ((j - 1) // 8 + 1)
                                deps.append((tag, "exp", j - 1, pnkt - 2 + kt))
                            else:
                                deps.append((tag, "krope", 31))
                            if kt == 0 and hp == 0 and qw > 0:
                                deps += [(tag, "outev", 4 * qw - 1, 0),
                                         (tag, "outev", 4 * qw - 1, 1),
                                         (tag, "outev", 4 * qw - 2, 0),
                                         (tag, "outev", 4 * qw - 2, 1)]
                        else:
                            deps = [(tag, "exp", j, kt - 2)]
                        P.op("PE", None, deps,
                             lambda e, SC=SC, kt=kt: e.matmul(
                                 SC[:, 0:512],
                                 kTs[0:64, S * hp + 128 * kt:S * hp + 128 * kt + 128],
                                 QT[0:64, :], start=True, stop=True, tile_position=(0, 0)))
                        P.op("PE", (tag, "scmm", j, kt), (),
                             lambda e, SC=SC, kt=kt: e.matmul(
                                 SC[:, 512:1024],
                                 kTs[64:128, S * hp + 128 * kt:S * hp + 128 * kt + 128],
                                 QT[64:128, :], start=True, stop=True, tile_position=(64, 0)))
                        EX = expb[:, 1024 * (kt % 4):1024 * (kt % 4) + 1024]
                        deps = [(tag, "scmm", j, kt)]
                        if kt >= 4:
                            deps.append((tag, "avg", j, kt - 4))
                        elif j >= 1:
                            pnkt = 4 * ((j - 1) // 8 + 1)
                            deps.append((tag, "avg", j - 1, pnkt - 4 + kt))
                        P.op("ACT", (tag, "exp", j, kt), deps,
                             lambda e, EX=EX, SC=SC: e.activation(EX, SC[:], AF.Exp,
                                                                  bias=0.0, scale=0.125))
                        if kt >= nkt - 2:
                            for off in (0, 512):
                                key = (tag, "msk", j, kt) if off == 512 else None
                                P.op("DVE", key, [(tag, "exp", j, kt)] if off == 0 else (),
                                     lambda e, EX=EX, off=off: e.memset(EX[:, off:off + 256], 0.0))

                    def emit_av(kt, j=j, hp=hp, nkt=nkt, hA=hA, hB=hB):
                        EX = expb[:, 1024 * (kt % 4):1024 * (kt % 4) + 1024]
                        deps = [(tag, "msk", j, kt) if kt >= nkt - 2 else (tag, "exp", j, kt)]
                        if kt == 0 and j >= 1:
                            deps.append((tag, "avev", j - 1))
                        P.op("PE", None, deps,
                             lambda e, EX=EX, kt=kt: e.matmul(
                                 avAB[0:65, 0:512],
                                 vab[:, 1040 * kt + 65 * hA:1040 * kt + 65 * hA + 65],
                                 EX[:, 0:512], start=(kt == 0), stop=(kt == nkt - 1)))
                        P.op("PE", (tag, "avg", j, kt), (),
                             lambda e, EX=EX, kt=kt: e.matmul(
                                 avAB[0:65, 512:1024],
                                 vab[:, 1040 * kt + 65 * hB:1040 * kt + 65 * hB + 65],
                                 EX[:, 512:1024], start=(kt == 0), stop=(kt == nkt - 1)))

                    def emit_epilogue(jj):
                        AVS = avsb[:, 1024 * (jj % 2):1024 * (jj % 2) + 1024]
                        for half in range(2):
                            deps = [(tag, "rc", jj)]
                            if jj >= 1 and half == 0:
                                deps.append((tag, "bcev", jj - 1))
                            P.op("PE", (tag, "bcmm", jj, half), deps,
                                 lambda e, half=half: e.matmul(
                                     bcps[:, 512 * half:512 * half + 512], cs["ones1f"][:],
                                     rcb[0:1, 512 * half:512 * half + 512],
                                     start=True, stop=True))
                        deps = [(tag, "bcmm", jj, 1)]
                        if jj >= 1:
                            deps.append((tag, "outTev", jj - 1, 1))
                        P.op("DVE", (tag, "bcev", jj), deps,
                             lambda e: e.tensor_copy(bcsb[:], bcps[:]))
                        OT = outT[:, 512 * (jj % 8):512 * (jj % 8) + 512]
                        deps = [(tag, "bcev", jj)]
                        if jj >= 8:
                            deps.append((tag, "fmm", 4 * (jj // 8 - 1) + 3, 1))
                        P.op("DVE", (tag, "outTev", jj, 0), deps,
                             lambda e, OT=OT, AVS=AVS: e.tensor_tensor(
                                 OT[0:64, :], AVS[0:64, 0:512], bcsb[0:64, 0:512], ALU.mult))
                        P.op("DVE", (tag, "outTev", jj, 1), (),
                             lambda e, OT=OT, AVS=AVS: e.tensor_tensor(
                                 OT[64:128, :], AVS[0:64, 512:1024], bcsb[0:64, 512:1024],
                                 ALU.mult))

                    POS = max(1, min(nkt - 3, 7))
                    emit_sc(0)
                    for kt in range(1, nkt):
                        emit_sc(kt)
                        emit_av(kt - 1)
                        if kt == POS and hp >= 1:
                            emit_epilogue(j - 1)
                    emit_av(nkt - 1)
                    AVS = avsb[:, 1024 * (j % 2):1024 * (j % 2) + 1024]
                    deps = [(tag, "avg", j, nkt - 1)]
                    if j >= 2:
                        deps.append((tag, "outTev", j - 2, 1))
                    P.op("DVE", (tag, "avev", j), deps,
                         lambda e, AVS=AVS: e.tensor_copy(AVS, avAB[0:65, :]))
                    deps = [(tag, "avev", j)]
                    if j >= 1:
                        deps.append((tag, "bcmm", j - 1, 1))
                    P.op("DVE", (tag, "rc", j), deps,
                         lambda e, AVS=AVS: e.reciprocal(rcb[0:1, :], AVS[64:65, :]))
                    if hp == 7:
                        emit_epilogue(j)
                for ssub in range(4):
                    t = 4 * qw + ssub
                    b = t % 2
                    RS = resb[:, M * b:M * b + M]
                    OB = outb[:, M * b:M * b + M]
                    deps = [(tag, "outTev", qw * 8 + 7, 1)] if ssub == 0 else []
                    if t >= 2:
                        deps.append((tag, "outst", t - 2))
                    P.dma("SP", (tag, "res", t), deps + list(vevF),
                          lambda e, RS=RS, t=t: e.dma_start(RS, x_src[128 * t:128 * t + 128, :]),
                          stream="res", idx=t, pool=3)
                    for mw in range(2):
                        fps = scAB[ssub % 2][:, 512 * mw:512 * mw + 512]
                        for hp in range(8):
                            deps = []
                            if ssub == 0 and mw == 0:
                                deps = [(tag, "outTev", qw * 8 + hp, 1)]
                            if hp == 0:
                                deps = deps + [(tag, "exp", qw * 8 + 7, nkt - 1)]
                                if t >= 2:
                                    deps.append((tag, "outev", t - 2, mw))
                            P.op("PE", (tag, "fmm", t, mw) if hp == 7 else None, deps,
                                 lambda e, fps=fps, hp=hp, ssub=ssub, mw=mw: e.matmul(
                                     fps, outT[:, 512 * hp + 128 * ssub:512 * hp + 128 * ssub + 128],
                                     wob[:, 1024 * hp + 512 * mw:1024 * hp + 512 * mw + 512],
                                     start=(hp == 0), stop=(hp == 7)))
                        P.op("DVE", (tag, "outev", t, mw), [(tag, "fmm", t, mw), (tag, "res", t)],
                             lambda e, OB=OB, RS=RS, mw=mw, fps=fps: e.tensor_tensor(
                                 OB[:, 512 * mw:512 * mw + 512], fps,
                                 RS[:, 512 * mw:512 * mw + 512], ALU.add))
                    P.dma("SP", (tag, "outst", t), [(tag, "outev", t, 0), (tag, "outev", t, 1)],
                          lambda e, OB=OB, t=t: e.dma_start(x_dst[128 * t:128 * t + 128, :], OB),
                          stream="outst", idx=t, pool=4)
            P.run()
    emit_fence(ctx)


from concourse.bass_utils import run_bass_kernel_spmd

_BUILD_CACHE = {}


def _build_program():
    if "nc" in _BUILD_CACHE:
        return _BUILD_CACHE["nc"], _BUILD_CACHE["consts"], _BUILD_CACHE["rope"]
    nc = bass.Bass("TRN2", target_bir_lowering=False)
    ctx = Ctx(nc)
    x_d = nc.dram_tensor("x", [S, M], F32, kind="ExternalInput")
    out_d = nc.dram_tensor("out", [S, M], F32, kind="ExternalOutput")
    x1_d = nc.dram_tensor("x1s", [S, M], F32, kind="Internal")
    x2_d = nc.dram_tensor("x2s", [S, M], F32, kind="Internal")
    rw_d = nc.dram_tensor("rw", [M, E], F32, kind="ExternalInput")
    w0_d = nc.dram_tensor("w0", [E, M, H], F32, kind="ExternalInput")
    w1_d = nc.dram_tensor("w1", [E, M, H], F32, kind="ExternalInput")
    wof_d = nc.dram_tensor("wof", [E, H, M], F32, kind="ExternalInput")
    wq_d = nc.dram_tensor("wqp", [M, M], F32, kind="ExternalInput")
    wd_d = nc.dram_tensor("wd", [M, 512], F32, kind="ExternalInput")
    wk_d = nc.dram_tensor("wkp", [512, M], F32, kind="ExternalInput")
    wv_d = nc.dram_tensor("wv", [512, M], F32, kind="ExternalInput")
    woa_d = nc.dram_tensor("woa", [M, M], F32, kind="ExternalInput")

    w0c_d = nc.dram_tensor("w0c", [E, 128, 8, H], BF16, kind="Internal")
    w1c_d = nc.dram_tensor("w1c", [E, 128, 8, H], BF16, kind="Internal")
    woc_d = nc.dram_tensor("woc", [E, H, M], BF16, kind="Internal")
    consts = declare_consts(ctx)
    emit_pool_init(ctx)
    emit_const_loads(ctx)
    WF = dict(router_w=rw_d.ap(), w0=w0_d.ap(), w1=w1_d.ap(), wo=wof_d.ap(),
              w0c=w0c_d.ap(), w1c=w1c_d.ap(), woc=woc_d.ap())
    WA = dict(wq=wq_d.ap(), w_down=wd_d.ap(), wk_up=wk_d.ap(), wv_up=wv_d.ap(),
              wo=woa_d.ap())
    emit_ffn(ctx, "A", x_d.ap(), x1_d.ap(), WF, first=True)
    emit_attention(ctx, "T", x1_d.ap(), x2_d.ap(), WA)
    rope = dict(ctx.rope_inputs)
    emit_ffn(ctx, "B", x2_d.ap(), out_d.ap(), WF, first=False)
    emit_finish(ctx)
    mybir.codegen_inst_isa_subclasses(nc)
    _BUILD_CACHE.update(nc=nc, consts=consts, rope=rope)
    return nc, consts, rope


def kernel(L0_attn_x, L0_attn_wq, L0_attn_w_down, L0_attn_wk_up, L0_attn_wv_up,
           L0_attn_wo, L0_ffn_router_w, L0_ffn_w0_kernel, L0_ffn_w1_kernel,
           L0_ffn_wo_kernel):
    x = np.ascontiguousarray(np.asarray(L0_attn_x, dtype=np.float32))
    nc, consts, rope = _build_program()
    perm = attn_perm()
    wqp = np.ascontiguousarray(np.asarray(L0_attn_wq, np.float32)[:, perm])
    wkp = np.ascontiguousarray(np.asarray(L0_attn_wk_up, np.float32)[:, perm])
    base = dict(
        rw=np.ascontiguousarray(np.asarray(L0_ffn_router_w, np.float32)),
        w0=np.ascontiguousarray(np.asarray(L0_ffn_w0_kernel, np.float32)),
        w1=np.ascontiguousarray(np.asarray(L0_ffn_w1_kernel, np.float32)),
        wof=np.ascontiguousarray(np.asarray(L0_ffn_wo_kernel, np.float32)),
        wqp=wqp,
        wd=np.ascontiguousarray(np.asarray(L0_attn_w_down, np.float32)),
        wkp=wkp,
        wv=np.ascontiguousarray(np.asarray(L0_attn_wv_up, np.float32)),
        woa=np.ascontiguousarray(np.asarray(L0_attn_wo, np.float32)),
    )
    for k, v in consts.items():
        base["c_" + k] = v
    base.update(rope)
    in_maps = []
    for b in range(8):
        m = dict(base)
        m["x"] = np.ascontiguousarray(x[b])
        in_maps.append(m)
    res = run_bass_kernel_spmd(nc, in_maps, core_ids=list(range(8)))
    return np.stack([res.results[b]["out"] for b in range(8)], axis=0)
